# version 13
# baseline (speedup 1.0000x reference)
import numpy as np
import ml_dtypes
import concourse.bass as bass
import concourse.mybir as mybir
from concourse.ap import AP
from concourse.bass_utils import run_bass_kernel_spmd

F32 = mybir.dt.float32
BF16 = mybir.dt.bfloat16
AF = mybir.ActivationFunctionType
ALU = mybir.AluOpType
BFNP = ml_dtypes.bfloat16

B, SEQ, HID, NH = 2, 4096, 2048, 16
T = 1024
NCH = 16
IFF = 8192
EPS = 1e-5
NTOK_LN = float(HID)
NTOK_GN = float(SEQ * 128)

STAGE = 99
DUMPS = ()
SUB12 = "all"
SKIP_SPARSIFY = False
RAW_H = False


def _perm_ffn():
    i = np.arange(IFF)
    g, j = i // 4, i % 4
    newpos = (4 * (g // 128) + j) * 128 + (g % 128)
    perm = np.empty(IFF, np.int64)
    perm[newpos] = i
    return perm


class Ctr:
    def __init__(self, sem):
        self.sem = sem
        self.v = 0


class Builder:
    def __init__(self):
        nc = bass.Bass("TRN2", target_bir_lowering=False)
        self.nc = nc
        self.prog = []

        def ein(name, shape):
            return nc.dram_tensor(name, shape, F32, kind="ExternalInput").ap()
        self.x_fm = ein("x_fm", [HID, T])
        self.w = {n: ein(n, [HID, HID]) for n in ("wq", "wk", "wv", "wg", "wo")}
        self.wgate = ein("wgate", [HID, IFF])
        self.wup = ein("wup", [HID, IFF])
        self.wdown = ein("wdown", [IFF, HID])
        self.bq_c = ein("bq_c", [128, 16]); self.bk_c = ein("bk_c", [128, 16])
        self.bg_c = ein("bg_c", [128, 16]); self.bo_c = ein("bo_c", [128, 16])
        self.gnw_c = ein("gnw_c", [128, 16]); self.gnb_c = ein("gnb_c", [128, 16])
        self.bv_c = ein("bv_c", [128, 16])
        self.gamma_rep = ein("gamma_rep", [128, 16])
        self.mixw_rep = ein("mixw_rep", [128, 4])
        self.out_fm = nc.dram_tensor("out_fm", [HID, T], F32, kind="ExternalOutput").ap()

        self.dbg = {}
        for name, shape in [("q", [128, NH * T]), ("k", [128, NH * 512]),
                            ("vt", [128, 4 * HID]), ("g", [128, NH * T]),
                            ("ret", [128, NH * T]), ("st", [128, HID]),
                            ("acol", [128, 16]), ("bcol", [128, 16]),
                            ("resid1", [128, NCH * T]), ("ybf", [128, NCH * T]),
                            ("h", [128, 16 * T]), ("gated", [128, NH * T]),
                            ("csum", [128, 32]), ("ctot", [32, 1]),
                            ("ct1", [16, 1]), ("ct3", [16, 1]),
                            ("cmur", [128, 16]), ("crsr", [128, 16]),
                            ("wf", [128, 4 * NCH * 128]), ("wd", [128, 2 * NCH * 128]),
                            ("csg", [128, 2 * 512])]:
            if name in DUMPS:
                self.dbg[name] = nc.dram_tensor("dbg_" + name, shape, F32,
                                                kind="ExternalOutput").ap()

        self.l_loc = nc.dram_tensor("l_loc", [128, HID], F32).ap()
        self.l_all = nc.dram_tensor("l_all", [4 * 128, HID], F32).ap()
        self.st_loc = nc.dram_tensor("st_loc", [32, 1], F32).ap()
        self.st_all = nc.dram_tensor("st_all", [32, 1], F32).ap()

        self.i_ident_b = nc.inline_tensor(np.eye(128, dtype=np.float32).astype(BFNP), "i_identb").ap()
        self.i_tri = nc.inline_tensor(np.tril(np.ones((128, 128), dtype=np.float32)).T.copy(), "i_tri").ap()
        self.i_iop1 = nc.inline_tensor(
            np.broadcast_to(np.arange(1, 129, dtype=np.float32), (128, 128)).copy(), "i_iop1").ap()
        self.i_ior = nc.inline_tensor(
            np.broadcast_to(127.0 - np.arange(128, dtype=np.float32), (128, 128)).copy(), "i_ior").ap()
        self.i_ones_b = nc.inline_tensor(np.ones((128, 128), np.float32).astype(BFNP), "i_onesb").ap()
        self.i_ones_c = nc.inline_tensor(np.ones((128, 1), np.float32), "i_onesc").ap()
        self.i_ones_r = nc.inline_tensor(np.ones((1, 128), np.float32), "i_onesr").ap()
        self.i_ones_rb = nc.inline_tensor(np.ones((1, 128), np.float32).astype(BFNP), "i_onesrb").ap()

        KB = 1024
        TOP = 229344
        total = 192 * KB
        base = TOP - total
        assert base > 27 * KB, base
        def at(name, shape, dtype, rel):
            return nc.alloc_sbuf_tensor_at(name, shape, dtype, offset=base + rel)
        self.sQ = at("sQ", [128, NH * T], BF16, 0)
        self.sDQ = at("sDQ", [128, NH * 128], BF16, 32 * KB)
        self.sDK = at("sDK", [128, NH * 128], BF16, 36 * KB)
        self.sMASK = at("sMASK", [128, NH * 128], BF16, 40 * KB)
        self.sL = at("sL", [128, HID], F32, 44 * KB)
        self.sSf = at("sSf", [128, HID], F32, 52 * KB)
        self.sSb = at("sSb", [128, HID], BF16, 60 * KB)
        self.sSin = at("sSin", [128, HID], BF16, 60 * KB)
        self.sA2 = at("sA2", [128, NCH * T], F32, 0)
        self.sRET = at("sRET", [128, NH * T], BF16, 64 * KB)
        self.sM = at("sM", [128, T], F32, 64 * KB)
        self.sR = at("sR", [128, T], F32, 68 * KB)
        self.sMR = at("sMR", [128, T], F32, 72 * KB)
        self.sSC1 = at("sSC1", [128, T], F32, 76 * KB)
        self.sSC2 = at("sSC2", [128, T], F32, 80 * KB)
        self.sCB = at("sCB", [128, 2, T], BF16, 84 * KB)
        self.sWD = at("sWD", [128, 2, NCH, 128], BF16, 88 * KB)
        self.sG = at("sG", [128, NH * T], BF16, 96 * KB)
        self.sY = at("sY", [128, NCH * T], BF16, 96 * KB)
        self.sK = at("sK", [128, NH * 512], BF16, 128 * KB)
        self.sLB = at("sLB", [128, 4, HID], F32, 128 * KB)
        self.sVT = at("sVT", [128, 4 * HID], BF16, 144 * KB)
        self.sH = at("sH", [128, 16 * T], BF16, 128 * KB)
        self.sXB = at("sXB", [128, NCH, 512], BF16, 160 * KB)
        self.sSq = at("sSq", [128, T], BF16, 160 * KB)
        self.sX1 = at("sX1", [128, 2, T], F32, 160 * KB)
        self.sSP = at("sSP", [128, 6, T], BF16, 160 * KB)
        self.sWA = at("sWA", [128, 4, NCH, 128], BF16, 176 * KB)
        self.sWF = at("sWF", [128, 4, NCH, 128], BF16, 176 * KB)

        def small(name, shape, dtype=F32):
            return nc.alloc_sbuf_tensor(name, shape, dtype)
        self.cBQ = small("cBQ", [128, 16]); self.cBK = small("cBK", [128, 16])
        self.cBV = small("cBV", [128, 16])
        self.cSV = small("cSV", [128, 2, 512], BF16)
        self.cBG = small("cBG", [128, 16]); self.cBO = small("cBO", [128, 16])
        self.cGNW = small("cGNW", [128, 16]); self.cGNB = small("cGNB", [128, 16])
        self.cGAM = small("cGAM", [128, 16]); self.cMIX = small("cMIX", [128, 4])
        self.cLNG = small("cLNG", [128, 16])
        self.cG128 = small("cG128", [128, 16])
        self.cGinv = small("cGinv", [128, 16])
        self.cG128t = small("cG128t", [128, 8, 16])
        self.cWMIX = small("cWMIX", [128, 4, 16])
        self.cONES = small("cONES", [128, 128], BF16)
        self.cONEC = small("cONEC", [128, 1])
        self.cONER = small("cONER", [1, 128])
        self.cIDb = small("cIDb", [128, 128], BF16)
        self.cIOT = small("cIOT", [128, 128])
        self.cSUM = small("cSUM", [128, 32])
        self.cTOT = small("cTOT", [32, 1])
        self.cT1 = small("cT1", [16, 1]); self.cT2 = small("cT2", [16, 1])
        self.cT3 = small("cT3", [16, 1]); self.cT4 = small("cT4", [16, 1])
        self.cEPS = small("cEPS", [128, 1])
        self.cTRa = small("cTRa", [1, 16])
        self.cTRb = small("cTRb", [1, 16])
        self.cMUr = small("cMUr", [128, 16]); self.cRSr = small("cRSr", [128, 16])
        self.cACOL = small("cACOL", [128, 16]); self.cBCOL = small("cBCOL", [128, 16])
        self.cSCB = small("cSCB", [128, 2, 512], BF16)
        self.cKT2 = small("cKT2", [128, 2, 512], BF16)
        self.cSG = small("cSG", [128, 2, 512], BF16)
        self.cONERb = small("cONERb", [1, 128], BF16)
        self.cNMR = small("cNMR", [1, T], BF16)

        self.ps = [nc.alloc_psum_tensor(f"ps{i}", [128, 512], F32) for i in range(6)]
        self.psTt = [nc.alloc_psum_tensor(f"psT{i}", [128, 1024], BF16) for i in range(2)]
        self.ps_free = {i: (None, 0) for i in range(9)}

        self.dC = Ctr(nc.alloc_semaphore("dC"))
        self.dXB = Ctr(nc.alloc_semaphore("dXB"))
        self.dXBa = Ctr(nc.alloc_semaphore("dXBa"))
        self.dWA = [Ctr(nc.alloc_semaphore(f"dWA{i}")) for i in range(4)]
        self.dFG = [Ctr(nc.alloc_semaphore(f"dFG{i}")) for i in range(2)]
        self.dFU = [Ctr(nc.alloc_semaphore(f"dFU{i}")) for i in range(2)]
        self.dFD = [Ctr(nc.alloc_semaphore(f"dFD{i}")) for i in range(2)]
        self.dX1 = [Ctr(nc.alloc_semaphore(f"dX1{i}")) for i in range(2)]
        self.dSL = Ctr(nc.alloc_semaphore("dSL"))
        self.dLL = Ctr(nc.alloc_semaphore("dLL"))
        self.dST = Ctr(nc.alloc_semaphore("dST"))
        self.dSTL = Ctr(nc.alloc_semaphore("dSTL"))
        self.dTR = Ctr(nc.alloc_semaphore("dTR"))
        self.dO = Ctr(nc.alloc_semaphore("dO"))
        self.dDBG = Ctr(nc.alloc_semaphore("dDBG"))
        self.cc = Ctr(nc.alloc_semaphore("cc"))
        self.pe = Ctr(nc.alloc_semaphore("pe"))
        self.gp = Ctr(nc.alloc_semaphore("gp"))
        self._lastw = {}
        self.ve = Ctr(nc.alloc_semaphore("ve"))
        self.se = Ctr(nc.alloc_semaphore("se"))

    def on(self, eng, fn, inc=None):
        if inc is None:
            self.prog.append((eng, fn))
            return None
        inc.v += 1
        sem = inc.sem
        def wrapped(e, fn=fn, sem=sem):
            inst = fn(e)
            inst.then_inc(sem, 1)
        self.prog.append((eng, wrapped))
        return inc.v

    def dma(self, eng, ctr, dst, src):
        ctr.v += 16
        sem = ctr.sem
        def f(e, dst=dst, src=src, sem=sem):
            e.dma_start(out=dst, in_=src).then_inc(sem, 16)
        self.prog.append((eng, f))
        return ctr.v

    def wait(self, eng, ctr, val=None):
        v = ctr.v if val is None else val
        if v <= 0:
            return
        key = (eng, ctr.sem.num)
        if self._lastw.get(key, -1) >= v:
            return
        self._lastw[key] = v
        sem = ctr.sem
        def f(e, sem=sem, v=v):
            e.wait_ge(sem, v)
        self.prog.append((eng, f))

    def pe_wait_free(self, tile):
        ctr, v = self.ps_free[tile]
        if ctr is not None and v > 0:
            self.wait("tensor", ctr, v)

    def mark_ps_free(self, tile, ctr):
        self.ps_free[tile] = (ctr, ctr.v)

    def build(self):
        nc = self.nc
        self.phase0_consts_tables()
        if STAGE >= 1:
            self.phase12_proj_retention()
        if STAGE >= 3 and SUB12 == "all":
            self.phase3_states_correction()
            if STAGE >= 4:
                self.phase35_stats()
                self.phase4_norm_gate()
            if STAGE >= 5:
                self.phase5_wo()
            if STAGE >= 6:
                self.phase6_ln1()
            if STAGE >= 7:
                self.phase7_ffn()
            if STAGE >= 8:
                self.phase8_ln2_out()
        self.emit_debug_dumps()
        self.finalize()
        with nc.Block() as block:
            items = self.prog
            def mk(name):
                def runner(e):
                    for en, fn in items:
                        if en == name:
                            fn(e)
                return runner
            block.sync(mk("sync"))
            block.tensor(mk("tensor"))
            block.vector(mk("vector"))
            block.scalar(mk("scalar"))
            block.gpsimd(mk("gpsimd"))
        return nc

    def phase0_consts_tables(self):
        dC, ve, se = self.dC, self.ve, self.se
        for dst, src in [(self.cBQ[:, :], self.bq_c), (self.cBK[:, :], self.bk_c),
                         (self.cBG[:, :], self.bg_c), (self.cBO[:, :], self.bo_c),
                         (self.cGNW[:, :], self.gnw_c), (self.cGNB[:, :], self.gnb_c),
                         (self.cGAM[:, :], self.gamma_rep), (self.cMIX[:, :], self.mixw_rep),
                         (self.cONES[:, :], self.i_ones_b), (self.cONEC[:, :], self.i_ones_c),
                         (self.cONER[:, :], self.i_ones_r),
                         (self.cIDb[:, :], self.i_ident_b), (self.cBV[:, :], self.bv_c),
                         (self.cONERb[:, :], self.i_ones_rb)]:
            self.dma("sync", dC, dst, src)
        self.dma("sync", dC, self.cIOT[:, :], self.i_iop1)

        self.wait("vector", dC, dC.v)
        def f1(e):
            return e.tensor_scalar_max(self.cGAM[:, :], self.cGAM[:, :], 1e-30)
        v_clamp = self.on("vector", f1, inc=ve)

        self.wait("scalar", ve, v_clamp)
        self.wait("scalar", dC, dC.v)
        def f2a(e):
            return e.activation(out=self.cLNG[:, :], in_=self.cGAM[:, :], func=AF.Ln)
        v_ln = self.on("scalar", f2a, inc=se)
        self.wait("scalar", se, v_ln)
        def f2(e):
            e.activation(out=self.cG128[:, :], in_=self.cLNG[:, :], func=AF.Exp, scale=128.0)
            e.activation(out=self.cGinv[:, :], in_=self.cLNG[:, :], func=AF.Exp, scale=-128.0)
            for t in range(8):
                e.activation(out=self.cG128t[:, t, :], in_=self.cLNG[:, :], func=AF.Exp,
                             scale=128.0 * t)
            last = None
            for h in range(NH):
                last = e.activation(out=self.sDQ[:, h * 128:(h + 1) * 128],
                                    in_=self.cIOT[:, :], func=AF.Exp,
                                    scale=self.cLNG[:, h:h + 1])
            return last
        v_dq = self.on("scalar", f2, inc=se)
        self.wait("sync", se, v_dq)
        self.dma("sync", dC, self.cIOT[:, :], self.i_ior)
        self.wait("scalar", dC, dC.v)
        def f3(e):
            last = None
            for h in range(NH):
                last = e.activation(out=self.sDK[:, h * 128:(h + 1) * 128],
                                    in_=self.cIOT[:, :], func=AF.Exp,
                                    scale=self.cLNG[:, h:h + 1])
            return last
        v_dk = self.on("scalar", f3, inc=se)
        self.wait("sync", se, v_dk)
        self.dma("sync", dC, self.cIOT[:, :], self.i_tri)

        self.wait("vector", se, v_dk)
        self.wait("vector", dC, dC.v)
        def f4(e):
            for h in range(NH):
                e.tensor_scalar(self.sMASK[:, h * 128:(h + 1) * 128], self.cIOT[:, :],
                                self.cGinv[:, h:h + 1], None, ALU.mult)
            for qt in range(4):
                e.tensor_scalar(self.cWMIX[:, qt, :], self.cLNG[:, :],
                                self.cMIX[:, qt:qt + 1], None, ALU.mult)
            e.memset(self.sSf[:, :], 0.0)
            e.memset(self.cEPS[:, :], EPS)
            return e.memset(self.sSb[:, :], 0.0)
        v_mask = self.on("vector", f4, inc=ve)
        self.wait("scalar", ve, v_mask)
        def f5(e):
            return e.activation(out=self.cWMIX[:, :, :], in_=self.cWMIX[:, :, :], func=AF.Exp)
        self.c_tables_se = self.on("scalar", f5, inc=se)
        self.c_tables_ve = v_mask

    def _att_stream_order(self):
        mats = ["wq", "wk", "wv", "wg"]
        order = []
        for half in range(2):
            for i in range(64):
                order.append((mats[i // 16], i % 16))
        for b in range(16):
            order.append(("wo", b))
        return order

    def _issue_att_load(self):
        st = self.att_st
        i = st["next"]
        mat, b = self.att_order[i]
        if i >= 4:
            gate = st["consumed"].get(i - 4)
            if gate:
                self.wait("gpsimd", self.pe, gate)
        src = self.w[mat][:, b * 128:(b + 1) * 128].rearrange("(k p) c -> p k c", p=128)
        ctr = self.dWA[i % 4]
        v = self.dma("gpsimd", ctr, self.sWA[:, i % 4], src)
        st["vals"].append((ctr, v))
        st["next"] += 1

    def _att_block_val(self, idx):
        while self.att_st["next"] <= idx:
            self._issue_att_load()
        return self.att_st["vals"][idx]

    def _att_consumed(self, idx):
        self.att_st["consumed"][idx] = self.pe.v
        if self.att_st["next"] < len(self.att_order):
            self._issue_att_load()

    def phase12_proj_retention(self):
        pe, ve, se = self.pe, self.ve, self.se
        self.att_order = self._att_stream_order()
        self.att_st = {"next": 0, "vals": [], "consumed": {}}
        self.sb_ready = {hg: 0 for hg in range(4)}
        self.scb_busy = {0: 0, 1: 0}
        self.kt_busy = {0: 0, 1: 0}

        for half in range(2):
            if half == 1:
                self.wait("gpsimd", pe, self.v_xb_users)
            v_xb_lo = self.dma(
                "gpsimd", self.dXBa, self.sXB[:, 0:8, :],
                self.x_fm[0:1024, half * 512:(half + 1) * 512].rearrange("(k p) t -> p k t", p=128))
            v_xb = self.dma(
                "gpsimd", self.dXB, self.sXB[:, 8:16, :],
                self.x_fm[1024:2048, half * 512:(half + 1) * 512].rearrange("(k p) t -> p k t", p=128))
            base_blk = half * 64
            if half == 0:
                self.wait("vector", se, self.c_tables_se)

            for wi, wname in enumerate(["wq", "wk"]):
                if SUB12 == "q" and wi == 1:
                    return
                for m in range(NCH):
                    if isinstance(SUB12, str) and SUB12.startswith("q") and len(SUB12) > 1 \
                            and (wi > 0 or m >= int(SUB12[1:])):
                        return
                    blk = base_blk + wi * 16 + m
                    bctr, bval = self._att_block_val(blk)
                    self.wait("tensor", bctr, bval)
                    tile = m % 2
                    self.pe_wait_free(tile)
                    par = blk % 4
                    if wi == 0 and m < 2:
                        self.wait("tensor", self.dXBa, v_xb_lo)
                        def fl(e, par=par, tile=tile):
                            last = None
                            for kc in range(8):
                                last = e.matmul(self.ps[tile][:, 0:512],
                                                self.sWA[:, par, kc, :],
                                                self.sXB[:, kc, :],
                                                start=(kc == 0), stop=False,
                                                skip_group_check=True)
                            return last
                        self.on("tensor", fl, inc=pe)
                        self.wait("tensor", self.dXB, v_xb)
                        def fh(e, par=par, tile=tile):
                            last = None
                            for kc in range(8, NCH):
                                last = e.matmul(self.ps[tile][:, 0:512],
                                                self.sWA[:, par, kc, :],
                                                self.sXB[:, kc, :],
                                                start=False, stop=(kc == NCH - 1),
                                                skip_group_check=True)
                            return last
                        v_mm = self.on("tensor", fh, inc=pe)
                    else:
                        self.wait("tensor", self.dXB, v_xb)
                        def f(e, par=par, tile=tile):
                            last = None
                            for kc in range(NCH):
                                last = e.matmul(self.ps[tile][:, 0:512],
                                                self.sWA[:, par, kc, :],
                                                self.sXB[:, kc, :],
                                                start=(kc == 0), stop=(kc == NCH - 1),
                                                skip_group_check=True)
                            return last
                        v_mm = self.on("tensor", f, inc=pe)
                    self._att_consumed(blk)
                    self.wait("vector", pe, v_mm)
                    if wname == "wq":
                        dst = self.sQ[:, m * T + half * 512: m * T + half * 512 + 512]
                        dec, bias = self.sDQ, self.cBQ
                    else:
                        dst = self.sK[:, m * 512:(m + 1) * 512]
                        dec, bias = self.sDK, self.cBK
                    def fe(e, m=m, tile=tile, dst=dst, dec=dec, bias=bias):
                        din = AP(dec, m * 128, [[NH * 128, 128], [0, 4], [1, 128]])
                        return e.scalar_tensor_tensor(
                            out=dst.rearrange("p (r c) -> p r c", r=4),
                            in0=self.ps[tile][:, 0:512].rearrange("p (r c) -> p r c", r=4),
                            scalar=bias[:, m:m + 1], in1=din,
                            op0=ALU.add, op1=ALU.mult)
                    self.on("vector", fe, inc=ve)
                    self.mark_ps_free(tile, ve)

            if SUB12 == "k":
                return
            pend_tr = [None]
            def emit_vt_transposes():
                m, par, v_csv = pend_tr[0]
                tT = m % 2
                self.pe_wait_free(7 + tT)
                self.wait("tensor", ve, v_csv)
                def f_tr(e, par=par, tT=tT):
                    last = None
                    for ts in range(4):
                        last = e.transpose(
                            self.psTt[tT][:, ts * 128: ts * 128 + 128],
                            self.cSV[:, par, ts * 128: ts * 128 + 128],
                            self.cIDb[:, :])
                    return last
                v_tr = self.on("tensor", f_tr, inc=pe)
                self.csv_busy[par] = v_tr
                self.wait("vector", pe, v_tr)
                def f_tre(e, m=m, tT=tT):
                    out = AP(self.sVT, m * 128, [[4 * HID, 128], [HID, 4], [1, 128]])
                    return e.tensor_copy(
                        out, self.psTt[tT][:, 0:512].rearrange("p (r c) -> p r c", r=4))
                self.on("vector", f_tre, inc=ve)
                self.mark_ps_free(7 + tT, ve)
                pend_tr[0] = None
            self.csv_busy = {0: 0, 1: 0}
            for m in range(NCH):
                blk = base_blk + 32 + m
                bctr, bval = self._att_block_val(blk)
                self.wait("tensor", bctr, bval)
                self.wait("tensor", self.dXB, v_xb)
                tile = 2 + m % 2
                self.pe_wait_free(tile)
                par = blk % 4
                def f(e, par=par, tile=tile):
                    last = None
                    for kc in range(NCH):
                        last = e.matmul(self.ps[tile][:, 0:512],
                                        self.sWA[:, par, kc, :],
                                        self.sXB[:, kc, :],
                                        start=(kc == 0), stop=(kc == NCH - 1),
                                        skip_group_check=True)
                    return last
                v_mm = self.on("tensor", f, inc=pe)
                self._att_consumed(blk)
                cpar = m % 2
                self.wait("vector", pe, max(v_mm, self.csv_busy[cpar]))
                def fe(e, m=m, tile=tile, cpar=cpar):
                    return e.tensor_scalar(self.cSV[:, cpar, :], self.ps[tile][:, 0:512],
                                           self.cBV[:, m:m + 1], None, ALU.add)
                v_csv = self.on("vector", fe, inc=ve)
                self.mark_ps_free(tile, ve)
                if pend_tr[0] is not None:
                    emit_vt_transposes()
                pend_tr[0] = (m, cpar, v_csv)
            emit_vt_transposes()

            if SUB12 == "vt":
                return
            for m in ([] if half == 1 and SUB12 == "all" and STAGE >= 3 else range(NCH)):
                blk = base_blk + 48 + m
                bctr, bval = self._att_block_val(blk)
                self.wait("tensor", bctr, bval)
                self.wait("tensor", self.dXB, v_xb)
                tile = m % 2
                self.pe_wait_free(tile)
                par = blk % 4
                def f(e, par=par, tile=tile):
                    last = None
                    for kc in range(NCH):
                        last = e.matmul(self.ps[tile][:, 0:512],
                                        self.sWA[:, par, kc, :],
                                        self.sXB[:, kc, :],
                                        start=(kc == 0), stop=(kc == NCH - 1),
                                        skip_group_check=True)
                    return last
                v_mm = self.on("tensor", f, inc=pe)
                self._att_consumed(blk)
                self.wait("scalar", pe, v_mm)
                def fe(e, m=m, tile=tile, half=half):
                    return e.activation(
                        out=self.sG[:, m * T + half * 512: m * T + half * 512 + 512],
                        in_=self.ps[tile][:, 0:512], func=AF.Silu,
                        bias=self.cBG[:, m:m + 1])
                self.on("scalar", fe, inc=se)
                self.mark_ps_free(tile, se)

            self.v_xb_users = pe.v

            if SUB12 == "g" or STAGE < 2:
                self.v_ret_done_ve = ve.v
                return

            if half == 0:
                self.wait("tensor", se, self.c_tables_se)
                self.wait("tensor", ve, self.c_tables_ve)

            def stage_a(tl, hg):
                gt = half * 4 + tl
                tS = hg % 2
                tT = hg % 2
                self.pe_wait_free(tS)
                self.pe_wait_free(7 + tT)
                def f_a(e, tl=tl, gt=gt, hg=hg, tS=tS, tT=tT):
                    for hl in range(4):
                        h = hg * 4 + hl
                        e.matmul(
                            self.ps[tS][:, hl * 128: hl * 128 + 128],
                            self.sK[:, h * 512 + tl * 128: h * 512 + tl * 128 + 128],
                            self.sQ[:, h * T + gt * 128: h * T + gt * 128 + 128],
                            start=True, stop=True, skip_group_check=True)
                    last = None
                    for hl in range(4):
                        h = hg * 4 + hl
                        last = e.transpose(
                            self.psTt[tT][:, hl * 128: hl * 128 + 128],
                            self.sK[:, h * 512 + tl * 128: h * 512 + tl * 128 + 128],
                            self.cIDb[:, :])
                    return last
                v_a = self.on("tensor", f_a, inc=pe)
                self.wait("vector", pe, max(v_a, self.scb_busy[hg % 2], self.kt_busy[tT]))
                def f_me(e, hg=hg, tS=tS):
                    return e.tensor_tensor(self.cSCB[:, hg % 2, :],
                                           self.ps[tS][:, 0:512],
                                           self.sMASK[:, hg * 512: hg * 512 + 512],
                                           ALU.mult)
                v_me = self.on("vector", f_me, inc=ve)
                self.mark_ps_free(tS, ve)
                def f_kte(e, tT=tT):
                    return e.tensor_copy(self.cKT2[:, tT, :],
                                         self.psTt[tT][:, 0:512])
                v_kte = self.on("vector", f_kte, inc=ve)
                self.mark_ps_free(7 + tT, ve)
                return (tl, hg, v_me, v_kte)

            def stage_b(rec):
                tl, hg, v_me, v_kte = rec
                gt = half * 4 + tl
                tR = 2 + hg % 2
                tSt = 4 + hg % 2
                tT = hg % 2
                self.pe_wait_free(tR)
                self.pe_wait_free(tSt)
                self.wait("tensor", ve, v_kte)
                if self.sb_ready[hg]:
                    self.wait("tensor", ve, self.sb_ready[hg])
                def f_b(e, tl=tl, gt=gt, hg=hg, tR=tR, tSt=tSt, tT=tT):
                    for hl in range(4):
                        h = hg * 4 + hl
                        e.matmul(self.ps[tR][:, hl * 128: hl * 128 + 128],
                                 self.sVT[:, tl * HID + h * 128: tl * HID + h * 128 + 128],
                                 self.cSCB[:, hg % 2, hl * 128: hl * 128 + 128],
                                 start=True, stop=False, skip_group_check=True)
                        e.matmul(self.ps[tR][:, hl * 128: hl * 128 + 128],
                                 self.sSb[:, h * 128: h * 128 + 128],
                                 self.sQ[:, h * T + gt * 128: h * T + gt * 128 + 128],
                                 start=False, stop=True, skip_group_check=True)
                    last = None
                    for hl in range(4):
                        h = hg * 4 + hl
                        last = e.matmul(
                            self.ps[tSt][:, hl * 128: hl * 128 + 128],
                            self.cKT2[:, tT, hl * 128: hl * 128 + 128],
                            self.sVT[:, tl * HID + h * 128: tl * HID + h * 128 + 128],
                            start=True, stop=True, skip_group_check=True)
                    return last
                v_b = self.on("tensor", f_b, inc=pe)
                self.scb_busy[hg % 2] = v_b
                self.kt_busy[tT] = v_b
                self.wait("vector", pe, v_b)
                def f_re(e, gt=gt, hg=hg, tR=tR):
                    out = AP(self.sRET, (hg * 4) * T + gt * 128,
                             [[NH * T, 128], [T, 4], [1, 128]])
                    return e.tensor_copy(
                        out, self.ps[tR][:, 0:512].rearrange("p (r c) -> p r c", r=4))
                self.on("vector", f_re, inc=ve)
                self.mark_ps_free(tR, ve)
                def f_su(e, hg=hg, tSt=tSt):
                    last = None
                    for hl in range(4):
                        h = hg * 4 + hl
                        last = e.scalar_tensor_tensor(
                            out=self.sSf[:, h * 128: h * 128 + 128],
                            in0=self.sSf[:, h * 128: h * 128 + 128],
                            scalar=self.cG128[:, h:h + 1],
                            in1=self.ps[tSt][:, hl * 128: hl * 128 + 128],
                            op0=ALU.mult, op1=ALU.add)
                    return last
                v_sup = self.on("vector", f_su, inc=ve)
                self.mark_ps_free(tSt, ve)
                self.wait("vector", ve, v_sup)
                def f_sub(e, hg=hg):
                    return e.tensor_copy(self.sSb[:, hg * 512: hg * 512 + 512],
                                         self.sSf[:, hg * 512: hg * 512 + 512])
                self.sb_ready[hg] = self.on("vector", f_sub, inc=ve)

            items = [(tl, hg) for tl in range(4) for hg in range(4)]
            prev = None
            for it in items:
                rec = stage_a(*it)
                if prev is not None:
                    stage_b(prev)
                prev = rec
            stage_b(prev)
        self.v_ret_done_ve = self.ve.v

    def _emit_g_half1(self):
        pe, se = self.pe, self.se
        for m in range(NCH):
            blk = 64 + 48 + m
            bctr, bval = self._att_block_val(blk)
            self.wait("tensor", bctr, bval)
            tile = m % 2
            self.pe_wait_free(tile)
            par = blk % 4
            def f(e, par=par, tile=tile):
                last = None
                for kc in range(NCH):
                    last = e.matmul(self.ps[tile][:, 0:512],
                                    self.sWA[:, par, kc, :],
                                    self.sXB[:, kc, :],
                                    start=(kc == 0), stop=(kc == NCH - 1),
                                    skip_group_check=True)
                return last
            v_mm = self.on("tensor", f, inc=pe)
            self._att_consumed(blk)
            self.wait("scalar", pe, v_mm)
            def fe(e, m=m, tile=tile):
                return e.activation(
                    out=self.sG[:, m * T + 512: m * T + 1024],
                    in_=self.ps[tile][:, 0:512], func=AF.Silu,
                    bias=self.cBG[:, m:m + 1])
            self.on("scalar", fe, inc=se)
            self.mark_ps_free(tile, se)

    def phase3_states_correction(self):
        pe, ve, cc, se = self.pe, self.ve, self.cc, self.se
        self.wait("sync", ve, self.v_ret_done_ve)
        v_l = self.dma("sync", self.dLL, self.l_loc, self.sSf[:, :])
        self.wait("gpsimd", self.dLL, v_l)
        cc.v += 1
        def f_cc(e):
            e.collective_compute(
                "AllGather", ALU.bypass,
                replica_groups=[[0, 1, 2, 3], [4, 5, 6, 7]],
                ins=[self.l_loc], outs=[self.l_all],
            ).then_inc(self.cc.sem, 1)
        self.prog.append(("gpsimd", f_cc))
        v_ccag = cc.v
        self._emit_g_half1()
        self.wait("sync", cc, v_ccag)
        v_lb = self.dma("sync", self.dSL, self.sLB[:, :, :],
                        self.l_all.rearrange("(q p) c -> p q c", p=128))
        self.wait("vector", self.dSL, v_lb)
        v_acc = 0
        for qt in range(4):
            if v_acc:
                self.wait("vector", ve, v_acc)
            def f_acc(e, qt=qt):
                last = None
                for h in range(NH):
                    if qt == 0:
                        last = e.tensor_scalar(
                            self.sSin[:, h * 128: h * 128 + 128],
                            self.sLB[:, qt, h * 128: h * 128 + 128],
                            self.cWMIX[:, qt, h:h + 1], None, ALU.mult)
                    else:
                        last = e.scalar_tensor_tensor(
                            out=self.sSin[:, h * 128: h * 128 + 128],
                            in0=self.sLB[:, qt, h * 128: h * 128 + 128],
                            scalar=self.cWMIX[:, qt, h:h + 1],
                            in1=self.sSin[:, h * 128: h * 128 + 128],
                            op0=ALU.mult, op1=ALU.add)
                return last
            v_acc = self.on("vector", f_acc, inc=ve)
        v_sin = v_acc

        self.wait("tensor", ve, v_sin)
        stats_pend = []
        def drain_stat():
            fn, vdep = stats_pend.pop(0)
            self.wait("vector", ve, vdep)
            self.on("vector", fn, inc=ve)
        for hg in range(4):
            for gt in range(8):
                if stats_pend and gt % 2 == 0:
                    drain_stat()
                tile = gt % 2
                self.pe_wait_free(tile)
                def f_mm(e, gt=gt, hg=hg, tile=tile):
                    last = None
                    for hl in range(4):
                        h = hg * 4 + hl
                        last = e.matmul(
                            self.ps[tile][:, hl * 128: hl * 128 + 128],
                            self.sSin[:, h * 128: h * 128 + 128],
                            self.sQ[:, h * T + gt * 128: h * T + gt * 128 + 128],
                            start=True, stop=True, skip_group_check=True)
                    return last
                v_mm = self.on("tensor", f_mm, inc=pe)
                self.wait("vector", pe, v_mm)
                def f_ev(e, gt=gt, hg=hg, tile=tile):
                    last = None
                    for hl in range(4):
                        h = hg * 4 + hl
                        dst = self.sRET[:, h * T + gt * 128: h * T + gt * 128 + 128]
                        last = e.scalar_tensor_tensor(
                            out=dst,
                            in0=self.ps[tile][:, hl * 128: hl * 128 + 128],
                            scalar=self.cG128t[:, gt, h:h + 1],
                            in1=dst, op0=ALU.mult, op1=ALU.add)
                    return last
                self.on("vector", f_ev, inc=ve)
                self.mark_ps_free(tile, ve)
            v_done = self.ve.v
            self.wait("scalar", ve, v_done)
            for hl in range(4):
                self.wait("scalar", se, self.se.v)
                def f_q(e, h=hg * 4 + hl):
                    return e.activation(out=self.sSq[:, :],
                                        in_=self.sRET[:, h * T:(h + 1) * T],
                                        func=AF.Square,
                                        accum_out=self.cSUM[:, 16 + h:17 + h])
                self.on("scalar", f_q, inc=se)
            for hl in range(4):
                def f_s(e, h=hg * 4 + hl):
                    return e.reduce_sum(self.cSUM[:, h:h + 1],
                                        self.sRET[:, h * T:(h + 1) * T],
                                        mybir.AxisListType.X)
                stats_pend.append((f_s, v_done))
            if hg == 3:
                while stats_pend:
                    drain_stat()
        self.v_ret_final_ve = ve.v

    def phase35_stats(self):
        pe, ve, se, cc = self.pe, self.ve, self.se, self.cc
        v_sums = self.ve.v
        v_sq = self.se.v
        self.wait("tensor", ve, v_sums)
        self.wait("tensor", se, v_sq)
        self.pe_wait_free(5)
        def f_rr(e):
            return e.matmul(self.ps[5][0:32, 0:1], self.cSUM[:, :], self.cONEC[:, :],
                            start=True, stop=True, skip_group_check=True)
        v_rr = self.on("tensor", f_rr, inc=pe)
        self.wait("vector", pe, v_rr)
        def f_ev(e):
            return e.tensor_copy(self.cTOT[:, :], self.ps[5][0:32, 0:1])
        v_tot = self.on("vector", f_ev, inc=ve)
        self.mark_ps_free(5, ve)
        self.wait("sync", ve, v_tot)
        v_st = self.dma("sync", self.dST, self.st_loc, self.cTOT[:, :])
        self.wait("gpsimd", self.dST, v_st)
        cc.v += 1
        def f_cc(e):
            e.collective_compute(
                "AllReduce", ALU.add,
                replica_groups=[[0, 1, 2, 3], [4, 5, 6, 7]],
                ins=[self.st_loc], outs=[self.st_all],
            ).then_inc(cc.sem, 1)
        self.prog.append(("gpsimd", f_cc))
        self.wait("sync", cc, cc.v)
        self.dma("sync", self.dTR, self.cTRa[0:1, :], self.st_all[0:16, 0:1])
        v_t2 = self.dma("sync", self.dTR, self.cTRb[0:1, :], self.st_all[16:32, 0:1])
        self.wait("tensor", self.dTR, v_t2)
        self.pe_wait_free(5)
        def f_rep(e):
            e.matmul(self.ps[5][:, 0:16], self.cONER[0:1, :], self.cTRa[0:1, :],
                     start=True, stop=True, skip_group_check=True)
            return e.matmul(self.ps[5][:, 16:32], self.cONER[0:1, :], self.cTRb[0:1, :],
                            start=True, stop=True, skip_group_check=True)
        v_rep = self.on("tensor", f_rep, inc=pe)
        self.wait("vector", pe, v_rep)
        def f_c1(e):
            e.tensor_scalar_mul(self.cMUr[:, :], self.ps[5][:, 0:16], 1.0 / NTOK_GN)
            return e.tensor_scalar_mul(self.cRSr[:, :], self.ps[5][:, 16:32], 1.0 / NTOK_GN)
        v_c1 = self.on("vector", f_c1, inc=ve)
        self.mark_ps_free(5, ve)
        self.wait("vector", ve, v_c1)
        def f_c2(e):
            e.tensor_mul(self.cACOL[:, :], self.cMUr[:, :], self.cMUr[:, :])
            return e.tensor_sub(self.cRSr[:, :], self.cRSr[:, :], self.cACOL[:, :])
        v_c2 = self.on("vector", f_c2, inc=ve)
        self.wait("scalar", ve, v_c2)
        def f_sd(e):
            return e.activation(out=self.cRSr[:, :], in_=self.cRSr[:, :], func=AF.Sqrt,
                                bias=self.cEPS[:, 0:1])
        v_sd = self.on("scalar", f_sd, inc=se)
        self.wait("vector", se, v_sd)
        def f_rs(e):
            return e.reciprocal(self.cRSr[:, :], self.cRSr[:, :])
        v_rs = self.on("vector", f_rs, inc=ve)
        self.wait("vector", ve, v_rs)
        def f_ab1(e):
            return e.tensor_mul(self.cACOL[:, :], self.cRSr[:, :], self.cGNW[:, :])
        v_a1 = self.on("vector", f_ab1, inc=ve)
        self.wait("vector", ve, v_a1)
        def f_ab2(e):
            return e.tensor_mul(self.cMUr[:, :], self.cMUr[:, :], self.cACOL[:, :])
        v_a2 = self.on("vector", f_ab2, inc=ve)
        self.wait("vector", ve, v_a2)
        def f_ab3(e):
            return e.tensor_sub(self.cBCOL[:, :], self.cGNB[:, :], self.cMUr[:, :])
        self.v_ab = self.on("vector", f_ab3, inc=ve)
        self.mark_ps_free(5, ve)

    def phase4_norm_gate(self):
        ve, se, gp = self.ve, self.se, self.gp
        self.wait("vector", ve, self.v_ab)
        self.wait("vector", se, self.se.v)
        for h in range(NH):
            par = 4 + h % 2
            def f0(e, h=h, par=par):
                return e.tensor_scalar(self.sSP[:, par, :], self.sRET[:, h * T:(h + 1) * T],
                                       self.cACOL[:, h:h + 1], self.cBCOL[:, h:h + 1],
                                       ALU.mult, ALU.add)
            v0 = self.on("vector", f0, inc=ve)
            self.wait("vector", ve, v0)
            def f1(e, h=h, par=par):
                return e.tensor_tensor(self.sRET[:, h * T:(h + 1) * T],
                                       self.sSP[:, par, :],
                                       self.sG[:, h * T:(h + 1) * T], ALU.mult)
            self.on("vector", f1, inc=ve)
        self.v_gated = self.ve.v
        self.v_gated_gp = 0

    def phase5_wo(self):
        pe, ve = self.pe, self.ve
        self.wait("tensor", ve, self.v_gated)
        self.wait("tensor", self.gp, self.v_gated_gp)
        x1_free = {0: 0, 1: 0}
        for m in range(NCH):
            blk = 128 + m
            bctr, bval = self._att_block_val(blk)
            par_x = m % 2
            if x1_free[par_x]:
                self.wait("sync", ve, x1_free[par_x])
            v_x = self.dma("sync", self.dX1[par_x], self.sX1[:, par_x, :],
                           self.x_fm[m * 128:(m + 1) * 128, :])
            self.wait("tensor", bctr, bval)
            for th in range(2):
                tile = th
                self.pe_wait_free(tile)
                parw = blk % 4
                def f(e, parw=parw, tile=tile, th=th):
                    last = None
                    for kc in range(NCH):
                        last = e.matmul(
                            self.ps[tile][:, 0:512],
                            self.sWA[:, parw, kc, :],
                            self.sRET[:, kc * T + th * 512: kc * T + th * 512 + 512],
                            start=(kc == 0), stop=(kc == NCH - 1),
                            skip_group_check=True)
                    return last
                v_mm = self.on("tensor", f, inc=pe)
                self.wait("vector", pe, v_mm)
                self.wait("vector", self.dX1[par_x], v_x)
                def fe(e, m=m, th=th, tile=tile, par_x=par_x):
                    return e.scalar_tensor_tensor(
                        out=self.sA2[:, m * T + th * 512: m * T + th * 512 + 512],
                        in0=self.ps[tile][:, 0:512],
                        scalar=self.cBO[:, m:m + 1],
                        in1=self.sX1[:, par_x, th * 512: th * 512 + 512],
                        op0=ALU.add, op1=ALU.add)
                self.on("vector", fe, inc=ve)
                self.mark_ps_free(tile, ve)
            x1_free[par_x] = ve.v
            self._att_consumed(blk)
        self.v_resid1 = ve.v
        self.v_wo_pe = pe.v

    def _ln_stats(self):
        pe, ve, se = self.pe, self.ve, self.se
        for t in range(4):
            self.pe_wait_free(t)
        cb_busy = {0: 0, 1: 0}
        for kc in range(NCH):
            pcb = kc % 2
            if cb_busy[pcb]:
                self.wait("vector", pe, cb_busy[pcb])
                self.wait("scalar", pe, cb_busy[pcb])
            def f_c(e, kc=kc, pcb=pcb):
                return e.tensor_copy(self.sCB[:, pcb, :], self.sA2[:, kc * T:(kc + 1) * T])
            v_c = self.on("vector", f_c, inc=ve)
            def f_q(e, kc=kc, pcb=pcb):
                return e.activation(out=self.sSP[:, 4 + pcb, :],
                                    in_=self.sA2[:, kc * T:(kc + 1) * T],
                                    func=AF.Square)
            v_q = self.on("scalar", f_q, inc=se)
            self.wait("tensor", ve, v_c)
            self.wait("tensor", se, v_q)
            def f_mm(e, kc=kc, pcb=pcb):
                last = None
                for th in range(2):
                    e.matmul(self.ps[0 + th][:, 0:512], self.cONES[:, :],
                             self.sCB[:, pcb, th * 512: th * 512 + 512],
                             start=(kc == 0), stop=(kc == NCH - 1),
                             skip_group_check=True)
                    last = e.matmul(self.ps[2 + th][:, 0:512], self.cONES[:, :],
                                    self.sSP[:, 4 + pcb, th * 512: th * 512 + 512],
                                    start=(kc == 0), stop=(kc == NCH - 1),
                                    skip_group_check=True)
                return last
            cb_busy[pcb] = self.on("tensor", f_mm, inc=pe)
        v_stats_pe = self.pe.v
        self.wait("vector", pe, v_stats_pe)
        def f_mr0(e):
            last = None
            for th in range(2):
                sl = slice(th * 512, th * 512 + 512)
                e.tensor_scalar_mul(self.sM[:, sl], self.ps[th][:, 0:512], 1.0 / NTOK_LN)
                last = e.tensor_scalar_mul(self.sSC1[:, sl], self.ps[2 + th][:, 0:512],
                                           1.0 / NTOK_LN)
            return last
        v_mr0 = self.on("vector", f_mr0, inc=ve)
        self.wait("vector", ve, v_mr0)
        def f_mr1(e):
            return e.tensor_mul(self.sSC2[:, :], self.sM[:, :], self.sM[:, :])
        v_mr1 = self.on("vector", f_mr1, inc=ve)
        self.wait("vector", ve, v_mr1)
        def f_mr2(e):
            return e.tensor_sub(self.sSC1[:, :], self.sSC1[:, :], self.sSC2[:, :])
        v_var = self.on("vector", f_mr2, inc=ve)
        for t in range(4):
            self.mark_ps_free(t, ve)
        self.wait("scalar", ve, v_var)
        def f_sd(e):
            return e.activation(out=self.sSC2[:, :], in_=self.sSC1[:, :], func=AF.Sqrt,
                                bias=self.cEPS[:, 0:1])
        v_sd = self.on("scalar", f_sd, inc=se)
        self.wait("vector", se, v_sd)
        def f_r(e):
            return e.reciprocal(self.sR[:, :], self.sSC2[:, :])
        return self.on("vector", f_r, inc=ve)

    def phase6_ln1(self):
        pe, ve = self.pe, self.ve
        self.wait("vector", pe, self.v_wo_pe)
        self.wait("vector", ve, self.v_resid1)
        self.wait("scalar", pe, self.v_wo_pe)
        self.wait("scalar", ve, self.v_resid1)
        self.wait("tensor", ve, self.v_resid1)
        v_lnr = self._ln_stats()
        self.wait("vector", ve, v_lnr)
        def f_mr(e):
            return e.tensor_mul(self.sMR[:, :], self.sM[:, :], self.sR[:, :])
        v_mrp = self.on("vector", f_mr, inc=ve)
        self.wait("vector", ve, v_mrp)
        def f_nmr(e):
            return e.tensor_scalar_mul(self.cNMR[0:1, :], self.sMR[0:1, :], -1.0)
        self.v_nmr = self.on("vector", f_nmr, inc=ve)
        self.v_ychunk = {}
        for kc in range(NCH):
            scr = self.sSC1 if kc % 2 == 0 else self.sSC2
            def f0(e, kc=kc, scr=scr):
                return e.tensor_sub(scr[:, :], self.sA2[:, kc * T:(kc + 1) * T], self.sM[:, :])
            v0 = self.on("vector", f0, inc=ve)
            self.wait("vector", ve, v0)
            def f1(e, kc=kc, scr=scr):
                return e.tensor_tensor(self.sY[:, kc * T:(kc + 1) * T], scr[:, :],
                                       self.sR[:, :], ALU.mult)
            self.v_ychunk[kc] = self.on("vector", f1, inc=ve)
        self.v_ybf = self.ve.v

    def _ffn_issue(self, kind, j):
        st = self.ffn_st[kind]
        while st["next"] <= j:
            i = st["next"]
            if i >= 2:
                gate = st["consumed"].get(i - 2)
                if gate:
                    self.wait("gpsimd", self.pe, gate)
            if kind == "g":
                srcap = self.wgate[:, i * 128:(i + 1) * 128]
                ctr, dst = self.dFG[i % 2], self.sWF[:, i % 2]
            elif kind == "u":
                srcap = self.wup[:, i * 128:(i + 1) * 128]
                ctr, dst = self.dFU[i % 2], self.sWF[:, 2 + i % 2]
            else:
                iq, mo = i // 16, i % 16
                srcap = self.wdown[iq * 2048:(iq + 1) * 2048, mo * 128:(mo + 1) * 128]
                ctr, dst = self.dFD[i % 2], self.sWD[:, i % 2]
            v = self.dma("gpsimd", ctr, dst, srcap.rearrange("(k p) c -> p k c", p=128))
            st["vals"].append((ctr, v))
            st["next"] += 1
        return st["vals"][j]

    def _ffn_consumed(self, kind, j):
        st = self.ffn_st[kind]
        st["consumed"][j] = self.pe.v
        if st["next"] <= j + 2 and st["next"] < st["count"]:
            self._ffn_issue(kind, min(j + 2, st["count"] - 1))

    def _down_evac(self, iq, sl, tile):
        ve = self.ve
        if iq == 0:
            scr = self.sSC1 if tile % 2 == 0 else self.sSC2
            def fe0(e, sl=sl, scr=scr):
                th0 = (sl.start % T) // 512
                tsl = slice(th0 * 512, th0 * 512 + 512)
                return e.tensor_tensor(scr[:, 0:512], self.sA2[:, sl],
                                       self.sR[:, tsl], ALU.mult)
            v0 = self.on("vector", fe0, inc=ve)
            self.wait("vector", ve, v0)
            def fe2(e, sl=sl, tile=tile, scr=scr):
                return e.tensor_add(self.sA2[:, sl], scr[:, 0:512],
                                    self.ps[tile][:, 0:512])
            self.on("vector", fe2, inc=ve)
        else:
            def fe(e, sl=sl, tile=tile):
                return e.tensor_add(self.sA2[:, sl], self.sA2[:, sl],
                                    self.ps[tile][:, 0:512])
            self.on("vector", fe, inc=ve)

    def phase7_ffn(self):
        pe, ve, se = self.pe, self.ve, self.se
        self.ffn_st = {k: {"next": 0, "vals": [], "consumed": {}, "count": 64}
                       for k in ("g", "u", "d")}
        self.wait("gpsimd", self.pe, self.v_wo_pe)
        sg_busy = {0: 0, 1: 0}
        for iq in range(4):
            pending = []
            for mi in range(16):
                gmi = iq * 16 + mi
                gctr, gval = self._ffn_issue("g", gmi)
                uctr, uval = self._ffn_issue("u", gmi)
                self.wait("tensor", gctr, gval)
                self.wait("tensor", uctr, uval)
                for th in range(2):
                    tG = mi % 2
                    tU = 2 + mi % 2
                    self.pe_wait_free(tG)
                    self.pe_wait_free(tU)
                    if iq == 0 and mi == 0:
                        for piece in range(4):
                            self.wait("tensor", ve, self.v_ychunk[piece * 4 + 3])
                            def fp(e, th=th, tG=tG, tU=tU, piece=piece):
                                last = None
                                for kc in range(piece * 4, piece * 4 + 4):
                                    e.matmul(self.ps[tG][:, 0:512],
                                             self.sWF[:, 0, kc, :],
                                             self.sY[:, kc * T + th * 512: kc * T + th * 512 + 512],
                                             start=(kc == 0), stop=(kc == NCH - 1),
                                             skip_group_check=True)
                                    last = e.matmul(
                                        self.ps[tU][:, 0:512],
                                        self.sWF[:, 2, kc, :],
                                        self.sY[:, kc * T + th * 512: kc * T + th * 512 + 512],
                                        start=(kc == 0), stop=(kc == NCH - 1),
                                        skip_group_check=True)
                                return last
                            v_mm = self.on("tensor", fp, inc=pe)
                    else:
                        def f(e, mi=mi, th=th, tG=tG, tU=tU):
                            for kc in range(NCH):
                                e.matmul(self.ps[tG][:, 0:512],
                                         self.sWF[:, mi % 2, kc, :],
                                         self.sY[:, kc * T + th * 512: kc * T + th * 512 + 512],
                                         start=(kc == 0), stop=(kc == NCH - 1),
                                         skip_group_check=True)
                            last = None
                            for kc in range(NCH):
                                last = e.matmul(
                                    self.ps[tU][:, 0:512],
                                    self.sWF[:, 2 + mi % 2, kc, :],
                                    self.sY[:, kc * T + th * 512: kc * T + th * 512 + 512],
                                    start=(kc == 0), stop=(kc == NCH - 1),
                                    skip_group_check=True)
                            return last
                        v_mm = self.on("tensor", f, inc=pe)
                    self.wait("scalar", pe, v_mm)
                    if sg_busy[th]:
                        self.wait("scalar", ve, sg_busy[th])
                    def f_s(e, tG=tG, th=th):
                        return e.activation(out=self.cSG[:, th, :], in_=self.ps[tG][:, 0:512],
                                            func=AF.Silu)
                    v_si = self.on("scalar", f_s, inc=se)
                    self.mark_ps_free(tG, se)
                    self.wait("vector", se, v_si)
                    self.wait("vector", pe, v_mm)
                    def f_h(e, mi=mi, th=th, tU=tU):
                        if RAW_H:
                            return e.tensor_copy(
                                self.sH[:, mi * T + th * 512: mi * T + th * 512 + 512],
                                self.ps[tU][:, 0:512])
                        return e.tensor_tensor(
                            self.sH[:, mi * T + th * 512: mi * T + th * 512 + 512],
                            self.cSG[:, th, :], self.ps[tU][:, 0:512], ALU.mult)
                    v_h = self.on("vector", f_h, inc=ve)
                    sg_busy[th] = v_h
                    self.mark_ps_free(tU, ve)
                self._ffn_consumed("g", gmi)
                self._ffn_consumed("u", gmi)
                if mi == 15 and not SKIP_SPARSIFY:
                    while pending:
                        pending.pop(0)()
                    v_sp012 = self.ve.v
                    pending += self._sparsify_steps(3)
                if mi % 4 == 3 and mi != 15 and not SKIP_SPARSIFY:
                    pending += self._sparsify_steps(mi // 4)
                for _ in range(5):
                    if pending:
                        pending.pop(0)()
            while pending:
                pending.pop(0)()
            v_sp3 = self.ve.v
            if SKIP_SPARSIFY:
                v_sp012 = v_sp3
            if iq == 0:
                self.wait("tensor", ve, self.v_nmr)
            head_mo = 2
            for mo in range(head_mo):
                dctr, dval = self._ffn_issue("d", iq * 16 + mo)
                self.wait("tensor", dctr, dval)
                self.wait("tensor", ve, v_sp012)
                for th in range(2):
                    tile = 2 * mo + th
                    self.pe_wait_free(tile)
                    def fA(e, mo=mo, th=th, tile=tile):
                        last = None
                        for ik in range(12):
                            last = e.matmul(
                                self.ps[tile][:, 0:512],
                                self.sWD[:, mo % 2, ik, :],
                                self.sH[:, ik * T + th * 512: ik * T + th * 512 + 512],
                                start=(ik == 0), stop=False,
                                skip_group_check=True)
                        return last
                    self.on("tensor", fA, inc=pe)
            self.wait("tensor", ve, v_sp3)
            for mo in range(head_mo):
                for th in range(2):
                    tile = 2 * mo + th
                    def fB(e, mo=mo, th=th, tile=tile, iq=iq):
                        last = None
                        for ik in range(12, 16):
                            last = e.matmul(
                                self.ps[tile][:, 0:512],
                                self.sWD[:, mo % 2, ik, :],
                                self.sH[:, ik * T + th * 512: ik * T + th * 512 + 512],
                                start=False, stop=(ik == 15 and iq != 0),
                                skip_group_check=True)
                        if iq == 0:
                            last = e.matmul(
                                self.ps[tile][:, 0:512], self.cONERb[0:1, :],
                                self.cNMR[0:1, th * 512: th * 512 + 512],
                                start=False, stop=True, skip_group_check=True)
                        return last
                    v_mm = self.on("tensor", fB, inc=pe)
                    self.wait("vector", pe, v_mm)
                    sl = slice(mo * T + th * 512, mo * T + th * 512 + 512)
                    self._down_evac(iq, sl, tile)
                    self.mark_ps_free(tile, ve)
                self._ffn_consumed("d", iq * 16 + mo)
            for mo in range(head_mo, NCH):
                gmo = iq * 16 + mo
                dctr, dval = self._ffn_issue("d", gmo)
                self.wait("tensor", dctr, dval)
                for th in range(2):
                    tile = 4 + th
                    self.pe_wait_free(tile)
                    def f(e, mo=mo, th=th, tile=tile, iq=iq):
                        last = None
                        for ik in range(16):
                            last = e.matmul(
                                self.ps[tile][:, 0:512],
                                self.sWD[:, mo % 2, ik, :],
                                self.sH[:, ik * T + th * 512: ik * T + th * 512 + 512],
                                start=(ik == 0), stop=(ik == 15 and iq != 0),
                                skip_group_check=True)
                        if iq == 0:
                            last = e.matmul(
                                self.ps[tile][:, 0:512], self.cONERb[0:1, :],
                                self.cNMR[0:1, th * 512: th * 512 + 512],
                                start=False, stop=True, skip_group_check=True)
                        return last
                    v_mm = self.on("tensor", f, inc=pe)
                    self.wait("vector", pe, v_mm)
                    sl = slice(mo * T + th * 512, mo * T + th * 512 + 512)
                    self._down_evac(iq, sl, tile)
                    self.mark_ps_free(tile, ve)
                self._ffn_consumed("d", gmo)
        self.v_resid2 = self.ve.v
        self.v_ffn_pe = self.pe.v

    def _sparsify_steps(self, c0):
        ve, se = self.ve, self.se
        sl = [slice((4 * c0 + j) * T, (4 * c0 + j) * T + T) for j in range(4)]
        s = [self.sSP[:, i, :] for i in range(6)]
        st = {}
        def t0():
            self.wait("scalar", ve, self.ve.v)
            def fa01(e):
                e.activation(out=s[3], in_=self.sH[:, sl[0]], func=AF.Abs)
                return e.activation(out=s[4], in_=self.sH[:, sl[1]], func=AF.Abs)
            st["a01"] = self.on("scalar", fa01, inc=se)
        def t1():
            self.wait("vector", se, st["a01"])
            def fmm01(e):
                e.tensor_tensor(s[0], s[3], s[4], ALU.max)
                return e.tensor_tensor(s[1], s[3], s[4], ALU.min)
            st["mm01"] = self.on("vector", fmm01, inc=ve)
        def t2():
            self.wait("scalar", ve, st["mm01"])
            def fa23(e):
                e.activation(out=s[3], in_=self.sH[:, sl[2]], func=AF.Abs)
                return e.activation(out=s[4], in_=self.sH[:, sl[3]], func=AF.Abs)
            st["a23"] = self.on("scalar", fa23, inc=se)
        def t3():
            self.wait("vector", se, st["a23"])
            def fmm23(e):
                e.tensor_tensor(s[2], s[3], s[4], ALU.max)
                return e.tensor_tensor(s[5], s[3], s[4], ALU.min)
            st["mm23"] = self.on("vector", fmm23, inc=ve)
        def t4():
            self.wait("vector", ve, st["mm23"])
            def fab(e):
                e.tensor_tensor(s[0], s[0], s[2], ALU.min)
                return e.tensor_tensor(s[1], s[1], s[5], ALU.max)
            st["ab"] = self.on("vector", fab, inc=ve)
        def t5():
            self.wait("vector", ve, st["ab"])
            def fthr(e):
                return e.tensor_tensor(s[2], s[0], s[1], ALU.max)
            st["thr"] = self.on("vector", fthr, inc=ve)
        steps = [t0, t1, t2, t3, t4, t5]
        for j in range(4):
            def tj_a(j=j):
                self.wait("scalar", ve, self.ve.v)
                def f6(e, j=j):
                    return e.activation(out=s[3], in_=self.sH[:, sl[j]], func=AF.Abs)
                st["aj"] = self.on("scalar", f6, inc=se)
            def tj_b(j=j):
                self.wait("vector", se, st["aj"])
                def f7a(e):
                    return e.tensor_tensor(s[4], s[3], s[2], ALU.is_ge)
                st["ge"] = self.on("vector", f7a, inc=ve)
            def tj_c(j=j):
                self.wait("vector", ve, st["ge"])
                def f7(e, j=j):
                    return e.tensor_tensor(self.sH[:, sl[j]], self.sH[:, sl[j]], s[4], ALU.mult)
                self.on("vector", f7, inc=ve)
            steps += [tj_a, tj_b, tj_c]
        return steps

    def phase8_ln2_out(self):
        pe, ve, dO = self.pe, self.ve, self.dO
        self.wait("vector", pe, self.v_ffn_pe)
        self.wait("vector", ve, self.v_resid2)
        self.wait("scalar", pe, self.v_ffn_pe)
        self.wait("scalar", ve, self.v_resid2)
        self.wait("tensor", ve, self.v_resid2)
        self._ln_stats()
        for kc in range(NCH):
            par = kc % 2
            def f0(e, kc=kc, par=par):
                scr = self.sSC1 if par == 0 else self.sSC2
                return e.tensor_sub(scr[:, :], self.sA2[:, kc * T:(kc + 1) * T], self.sM[:, :])
            v0 = self.on("vector", f0, inc=ve)
            self.wait("vector", ve, v0)
            def f1(e, kc=kc, par=par):
                scr = self.sSC1 if par == 0 else self.sSC2
                return e.tensor_tensor(self.sA2[:, kc * T:(kc + 1) * T], scr[:, :],
                                       self.sR[:, :], ALU.mult)
            v_z = self.on("vector", f1, inc=ve)
            self.wait("sync", ve, v_z)
            self.dma("sync", dO, self.out_fm[kc * 128:(kc + 1) * 128, :],
                     self.sA2[:, kc * T:(kc + 1) * T])

    def emit_debug_dumps(self):
        d = self.dbg
        if not d:
            return
        self.wait("gpsimd", self.ve, self.ve.v)
        self.wait("gpsimd", self.se, self.se.v)
        srcs = dict(q=self.sQ, k=self.sK, vt=self.sVT, g=self.sG, ret=self.sRET,
                    st=self.sSf, acol=self.cACOL, bcol=self.cBCOL, resid1=self.sA2,
                    ybf=self.sY, h=self.sH, gated=self.sRET, csum=self.cSUM,
                    ctot=self.cTOT, ct1=self.cT1, ct3=self.cT3, cmur=self.cMUr,
                    crsr=self.cRSr, wf=self.sWF, wd=self.sWD, csg=self.cSG)
        for name in d:
            self.dma("gpsimd", self.dDBG, d[name], srcs[name][:, :])

    def finalize(self):
        for ctr in ([self.dO, self.dC, self.dXB, self.dXBa, self.dSL, self.dLL, self.dST,
                     self.dSTL, self.dTR, self.dDBG] + self.dWA + self.dFG +
                    self.dFU + self.dFD + self.dX1):
            self.wait("sync", ctr)


def _prep_inputs(inputs):
    perm = _perm_ffn()
    x = np.asarray(inputs["L0_attn_x"], np.float32)
    col = lambda b: np.ascontiguousarray(np.asarray(b, np.float32).reshape(16, 128).T)
    shared = dict(
        wq=np.ascontiguousarray(inputs["L0_attn_wq"], np.float32),
        wk=np.ascontiguousarray(inputs["L0_attn_wk"], np.float32),
        wv=np.ascontiguousarray(inputs["L0_attn_wv"], np.float32),
        wg=np.ascontiguousarray(inputs["L0_attn_wg"], np.float32),
        wo=np.ascontiguousarray(inputs["L0_attn_wo"], np.float32),
        wgate=np.ascontiguousarray(np.asarray(inputs["L0_ffn_w_gate"], np.float32)[:, perm]),
        wup=np.ascontiguousarray(np.asarray(inputs["L0_ffn_w_up"], np.float32)[:, perm]),
        wdown=np.ascontiguousarray(np.asarray(inputs["L0_ffn_w_down"], np.float32)[perm, :]),
        bq_c=col(inputs["L0_attn_bq"]), bk_c=col(inputs["L0_attn_bk"]),
        bg_c=col(inputs["L0_attn_bg"]), bo_c=col(inputs["L0_attn_bo"]),
        gnw_c=col(inputs["L0_attn_gn_w"]), gnb_c=col(inputs["L0_attn_gn_b"]),
        bv_c=col(inputs["L0_attn_bv"]),
        gamma_rep=np.ascontiguousarray(
            np.broadcast_to(np.asarray(inputs["L0_attn_gamma"], np.float32), (128, 16))),
    )
    in_maps = []
    for c in range(8):
        b, qt = c // 4, c % 4
        m = dict(shared)
        m["x_fm"] = np.ascontiguousarray(x[b, qt * T:(qt + 1) * T, :].T)
        mix = np.full(4, 1e9, np.float32)
        for qp in range(qt):
            mix[qp] = 1024.0 * (qt - qp - 1)
        m["mixw_rep"] = np.ascontiguousarray(np.broadcast_to(mix, (128, 4)))
        in_maps.append(m)
    return in_maps


_CACHED = {}


def kernel(**inputs):
    if "nc" not in _CACHED:
        _CACHED["nc"] = Builder().build()
    nc = _CACHED["nc"]
    in_maps = _prep_inputs(inputs)
    res = run_bass_kernel_spmd(nc, in_maps, core_ids=list(range(8)))
    out = np.empty((B, SEQ, HID), np.float32)
    for c in range(8):
        b, qt = c // 4, c % 4
        out[b, qt * T:(qt + 1) * T, :] = res.results[c]["out_fm"].T
    return out


# version 14
# speedup vs baseline: 1.0088x; 1.0088x over previous
import numpy as np
import ml_dtypes
import concourse.bass as bass
import concourse.mybir as mybir
from concourse.ap import AP
from concourse.bass_utils import run_bass_kernel_spmd

F32 = mybir.dt.float32
BF16 = mybir.dt.bfloat16
AF = mybir.ActivationFunctionType
ALU = mybir.AluOpType
BFNP = ml_dtypes.bfloat16

B, SEQ, HID, NH = 2, 4096, 2048, 16
T = 1024
NCH = 16
IFF = 8192
EPS = 1e-5
NTOK_LN = float(HID)
NTOK_GN = float(SEQ * 128)

STAGE = 99
DUMPS = ()
SUB12 = "all"
SKIP_SPARSIFY = False
RAW_H = False


def _perm_ffn():
    i = np.arange(IFF)
    g, j = i // 4, i % 4
    newpos = (4 * (g // 128) + j) * 128 + (g % 128)
    perm = np.empty(IFF, np.int64)
    perm[newpos] = i
    return perm


class Ctr:
    def __init__(self, sem):
        self.sem = sem
        self.v = 0


class Builder:
    def __init__(self):
        nc = bass.Bass("TRN2", target_bir_lowering=False)
        self.nc = nc
        self.prog = []

        def ein(name, shape):
            return nc.dram_tensor(name, shape, F32, kind="ExternalInput").ap()
        self.x_fm = ein("x_fm", [HID, T])
        self.w = {n: ein(n, [HID, HID]) for n in ("wq", "wk", "wv", "wg", "wo")}
        self.wgate = ein("wgate", [HID, IFF])
        self.wup = ein("wup", [HID, IFF])
        self.wdown = ein("wdown", [IFF, HID])
        self.bq_c = ein("bq_c", [128, 16]); self.bk_c = ein("bk_c", [128, 16])
        self.bg_c = ein("bg_c", [128, 16]); self.bo_c = ein("bo_c", [128, 16])
        self.gnw_c = ein("gnw_c", [128, 16]); self.gnb_c = ein("gnb_c", [128, 16])
        self.bv_c = ein("bv_c", [128, 16])
        self.gamma_rep = ein("gamma_rep", [128, 16])
        self.mixw_rep = ein("mixw_rep", [128, 4])
        self.out_fm = nc.dram_tensor("out_fm", [HID, T], F32, kind="ExternalOutput").ap()

        self.dbg = {}
        for name, shape in [("q", [128, NH * T]), ("k", [128, NH * 512]),
                            ("vt", [128, 4 * HID]), ("g", [128, NH * T]),
                            ("ret", [128, NH * T]), ("st", [128, HID]),
                            ("acol", [128, 16]), ("bcol", [128, 16]),
                            ("resid1", [128, NCH * T]), ("ybf", [128, NCH * T]),
                            ("h", [128, 16 * T]), ("gated", [128, NH * T]),
                            ("csum", [128, 32]), ("ctot", [32, 1]),
                            ("ct1", [16, 1]), ("ct3", [16, 1]),
                            ("cmur", [128, 16]), ("crsr", [128, 16]),
                            ("wf", [128, 4 * NCH * 128]), ("wd", [128, 2 * NCH * 128]),
                            ("csg", [128, 2 * 512])]:
            if name in DUMPS:
                self.dbg[name] = nc.dram_tensor("dbg_" + name, shape, F32,
                                                kind="ExternalOutput").ap()

        self.l_loc = nc.dram_tensor("l_loc", [128, HID], F32).ap()
        self.l_all = nc.dram_tensor("l_all", [4 * 128, HID], F32).ap()
        self.st_loc = nc.dram_tensor("st_loc", [32, 1], F32).ap()
        self.st_all = nc.dram_tensor("st_all", [32, 1], F32).ap()

        self.i_ident_b = nc.inline_tensor(np.eye(128, dtype=np.float32).astype(BFNP), "i_identb").ap()
        self.i_tri = nc.inline_tensor(np.tril(np.ones((128, 128), dtype=np.float32)).T.copy(), "i_tri").ap()
        self.i_iop1 = nc.inline_tensor(
            np.broadcast_to(np.arange(1, 129, dtype=np.float32), (128, 128)).copy(), "i_iop1").ap()
        self.i_ior = nc.inline_tensor(
            np.broadcast_to(127.0 - np.arange(128, dtype=np.float32), (128, 128)).copy(), "i_ior").ap()
        self.i_ones_b = nc.inline_tensor(np.ones((128, 128), np.float32).astype(BFNP), "i_onesb").ap()
        self.i_ones_c = nc.inline_tensor(np.ones((128, 1), np.float32), "i_onesc").ap()
        self.i_ones_r = nc.inline_tensor(np.ones((1, 128), np.float32), "i_onesr").ap()
        self.i_ones_rb = nc.inline_tensor(np.ones((1, 128), np.float32).astype(BFNP), "i_onesrb").ap()

        KB = 1024
        TOP = 229344
        total = 192 * KB
        base = TOP - total
        assert base > 27 * KB, base
        def at(name, shape, dtype, rel):
            return nc.alloc_sbuf_tensor_at(name, shape, dtype, offset=base + rel)
        self.sQ = at("sQ", [128, NH * T], BF16, 0)
        self.sDQ = at("sDQ", [128, NH * 128], BF16, 32 * KB)
        self.sDK = at("sDK", [128, NH * 128], BF16, 36 * KB)
        self.sMASK = at("sMASK", [128, NH * 128], BF16, 40 * KB)
        self.sL = at("sL", [128, HID], F32, 44 * KB)
        self.sSf = at("sSf", [128, HID], F32, 52 * KB)
        self.sSb = at("sSb", [128, HID], BF16, 60 * KB)
        self.sSin = at("sSin", [128, HID], BF16, 60 * KB)
        self.sA2 = at("sA2", [128, NCH * T], F32, 0)
        self.sRET = at("sRET", [128, NH * T], BF16, 64 * KB)
        self.sM = at("sM", [128, T], F32, 64 * KB)
        self.sR = at("sR", [128, T], F32, 68 * KB)
        self.sMR = at("sMR", [128, T], F32, 72 * KB)
        self.sSC1 = at("sSC1", [128, T], F32, 76 * KB)
        self.sSC2 = at("sSC2", [128, T], F32, 80 * KB)
        self.sCB = at("sCB", [128, 2, T], BF16, 84 * KB)
        self.sWD = at("sWD", [128, 2, NCH, 128], BF16, 88 * KB)
        self.sG = at("sG", [128, NH * T], BF16, 96 * KB)
        self.sY = at("sY", [128, NCH * T], BF16, 96 * KB)
        self.sK = at("sK", [128, NH * 512], BF16, 128 * KB)
        self.sLB = at("sLB", [128, 4, HID], F32, 128 * KB)
        self.sVT = at("sVT", [128, 4 * HID], BF16, 144 * KB)
        self.sH = at("sH", [128, 16 * T], BF16, 128 * KB)
        self.sXB = at("sXB", [128, NCH, 512], BF16, 160 * KB)
        self.sSq = at("sSq", [128, T], BF16, 160 * KB)
        self.sX1 = at("sX1", [128, 2, T], F32, 160 * KB)
        self.sSP = at("sSP", [128, 6, T], BF16, 160 * KB)
        self.sWA = at("sWA", [128, 4, NCH, 128], BF16, 176 * KB)
        self.sWF = at("sWF", [128, 4, NCH, 128], BF16, 176 * KB)

        def small(name, shape, dtype=F32):
            return nc.alloc_sbuf_tensor(name, shape, dtype)
        self.cBQ = small("cBQ", [128, 16]); self.cBK = small("cBK", [128, 16])
        self.cBV = small("cBV", [128, 16])
        self.cSV = small("cSV", [128, 2, 512], BF16)
        self.cBG = small("cBG", [128, 16]); self.cBO = small("cBO", [128, 16])
        self.cGNW = small("cGNW", [128, 16]); self.cGNB = small("cGNB", [128, 16])
        self.cGAM = small("cGAM", [128, 16]); self.cMIX = small("cMIX", [128, 4])
        self.cLNG = small("cLNG", [128, 16])
        self.cG128 = small("cG128", [128, 16])
        self.cGinv = small("cGinv", [128, 16])
        self.cG128t = small("cG128t", [128, 8, 16])
        self.cWMIX = small("cWMIX", [128, 4, 16])
        self.cONES = small("cONES", [128, 128], BF16)
        self.cONEC = small("cONEC", [128, 1])
        self.cONER = small("cONER", [1, 128])
        self.cIDb = small("cIDb", [128, 128], BF16)
        self.cIOT = small("cIOT", [128, 128])
        self.cSUM = small("cSUM", [128, 32])
        self.cTOT = small("cTOT", [32, 1])
        self.cT1 = small("cT1", [16, 1]); self.cT2 = small("cT2", [16, 1])
        self.cT3 = small("cT3", [16, 1]); self.cT4 = small("cT4", [16, 1])
        self.cEPS = small("cEPS", [128, 1])
        self.cTRa = small("cTRa", [1, 16])
        self.cTRb = small("cTRb", [1, 16])
        self.cMUr = small("cMUr", [128, 16]); self.cRSr = small("cRSr", [128, 16])
        self.cACOL = small("cACOL", [128, 16]); self.cBCOL = small("cBCOL", [128, 16])
        self.cSCB = small("cSCB", [128, 2, 512], BF16)
        self.cKT2 = small("cKT2", [128, 2, 512], BF16)
        self.cSG = small("cSG", [128, 2, 512], BF16)
        self.cONERb = small("cONERb", [1, 128], BF16)
        self.cNMR = small("cNMR", [1, T], BF16)

        self.ps = [nc.alloc_psum_tensor(f"ps{i}", [128, 512], F32) for i in range(6)]
        self.psTt = [nc.alloc_psum_tensor(f"psT{i}", [128, 1024], BF16) for i in range(2)]
        self.ps_free = {i: (None, 0) for i in range(9)}

        self.dC = Ctr(nc.alloc_semaphore("dC"))
        self.dXB = Ctr(nc.alloc_semaphore("dXB"))
        self.dXBa = Ctr(nc.alloc_semaphore("dXBa"))
        self.dWA = [Ctr(nc.alloc_semaphore(f"dWA{i}")) for i in range(4)]
        self.dFG = [Ctr(nc.alloc_semaphore(f"dFG{i}")) for i in range(2)]
        self.dFU = [Ctr(nc.alloc_semaphore(f"dFU{i}")) for i in range(2)]
        self.dFD = [Ctr(nc.alloc_semaphore(f"dFD{i}")) for i in range(2)]
        self.dX1 = [Ctr(nc.alloc_semaphore(f"dX1{i}")) for i in range(2)]
        self.dSL = Ctr(nc.alloc_semaphore("dSL"))
        self.dLL = Ctr(nc.alloc_semaphore("dLL"))
        self.dST = Ctr(nc.alloc_semaphore("dST"))
        self.dSTL = Ctr(nc.alloc_semaphore("dSTL"))
        self.dTR = Ctr(nc.alloc_semaphore("dTR"))
        self.dO = Ctr(nc.alloc_semaphore("dO"))
        self.dDBG = Ctr(nc.alloc_semaphore("dDBG"))
        self.cc = Ctr(nc.alloc_semaphore("cc"))
        self.pe = Ctr(nc.alloc_semaphore("pe"))
        self.gp = Ctr(nc.alloc_semaphore("gp"))
        self._lastw = {}
        self.ve = Ctr(nc.alloc_semaphore("ve"))
        self.se = Ctr(nc.alloc_semaphore("se"))

    def on(self, eng, fn, inc=None):
        if inc is None:
            self.prog.append((eng, fn))
            return None
        inc.v += 1
        sem = inc.sem
        def wrapped(e, fn=fn, sem=sem):
            inst = fn(e)
            inst.then_inc(sem, 1)
        self.prog.append((eng, wrapped))
        return inc.v

    def dma(self, eng, ctr, dst, src):
        ctr.v += 16
        sem = ctr.sem
        def f(e, dst=dst, src=src, sem=sem):
            e.dma_start(out=dst, in_=src).then_inc(sem, 16)
        self.prog.append((eng, f))
        return ctr.v

    def wait(self, eng, ctr, val=None):
        v = ctr.v if val is None else val
        if v <= 0:
            return
        key = (eng, ctr.sem.num)
        if self._lastw.get(key, -1) >= v:
            return
        self._lastw[key] = v
        sem = ctr.sem
        def f(e, sem=sem, v=v):
            e.wait_ge(sem, v)
        self.prog.append((eng, f))

    def pe_wait_free(self, tile):
        ctr, v = self.ps_free[tile]
        if ctr is not None and v > 0:
            self.wait("tensor", ctr, v)

    def mark_ps_free(self, tile, ctr):
        self.ps_free[tile] = (ctr, ctr.v)

    def build(self):
        nc = self.nc
        self.phase0_consts_tables()
        if STAGE >= 1:
            self.phase12_proj_retention()
        if STAGE >= 3 and SUB12 == "all":
            self.phase3_states_correction()
            if STAGE >= 4:
                self.phase35_stats()
                self.phase4_norm_gate()
            if STAGE >= 5:
                self.phase5_wo()
            if STAGE >= 6:
                self.phase6_ln1()
            if STAGE >= 7:
                self.phase7_ffn()
            if STAGE >= 8:
                self.phase8_ln2_out()
        self.emit_debug_dumps()
        self.finalize()
        with nc.Block() as block:
            items = self.prog
            def mk(name):
                def runner(e):
                    for en, fn in items:
                        if en == name:
                            fn(e)
                return runner
            block.sync(mk("sync"))
            block.tensor(mk("tensor"))
            block.vector(mk("vector"))
            block.scalar(mk("scalar"))
            block.gpsimd(mk("gpsimd"))
        return nc

    def phase0_consts_tables(self):
        dC, ve, se = self.dC, self.ve, self.se
        for dst, src in [(self.cBQ[:, :], self.bq_c), (self.cBK[:, :], self.bk_c),
                         (self.cBG[:, :], self.bg_c), (self.cBO[:, :], self.bo_c),
                         (self.cGNW[:, :], self.gnw_c), (self.cGNB[:, :], self.gnb_c),
                         (self.cGAM[:, :], self.gamma_rep), (self.cMIX[:, :], self.mixw_rep),
                         (self.cONES[:, :], self.i_ones_b), (self.cONEC[:, :], self.i_ones_c),
                         (self.cONER[:, :], self.i_ones_r),
                         (self.cIDb[:, :], self.i_ident_b), (self.cBV[:, :], self.bv_c),
                         (self.cONERb[:, :], self.i_ones_rb)]:
            self.dma("sync", dC, dst, src)
        self.dma("sync", dC, self.cIOT[:, :], self.i_iop1)

        self.wait("vector", dC, dC.v)
        def f1(e):
            return e.tensor_scalar_max(self.cGAM[:, :], self.cGAM[:, :], 1e-30)
        v_clamp = self.on("vector", f1, inc=ve)

        self.wait("scalar", ve, v_clamp)
        self.wait("scalar", dC, dC.v)
        def f2a(e):
            return e.activation(out=self.cLNG[:, :], in_=self.cGAM[:, :], func=AF.Ln)
        v_ln = self.on("scalar", f2a, inc=se)
        self.wait("scalar", se, v_ln)
        def f2(e):
            e.activation(out=self.cG128[:, :], in_=self.cLNG[:, :], func=AF.Exp, scale=128.0)
            e.activation(out=self.cGinv[:, :], in_=self.cLNG[:, :], func=AF.Exp, scale=-128.0)
            for t in range(8):
                e.activation(out=self.cG128t[:, t, :], in_=self.cLNG[:, :], func=AF.Exp,
                             scale=128.0 * t)
            last = None
            for h in range(NH):
                last = e.activation(out=self.sDQ[:, h * 128:(h + 1) * 128],
                                    in_=self.cIOT[:, :], func=AF.Exp,
                                    scale=self.cLNG[:, h:h + 1])
            return last
        v_dq = self.on("scalar", f2, inc=se)
        self.wait("sync", se, v_dq)
        self.dma("sync", dC, self.cIOT[:, :], self.i_ior)
        self.wait("scalar", dC, dC.v)
        def f3(e):
            last = None
            for h in range(NH):
                last = e.activation(out=self.sDK[:, h * 128:(h + 1) * 128],
                                    in_=self.cIOT[:, :], func=AF.Exp,
                                    scale=self.cLNG[:, h:h + 1])
            return last
        v_dk = self.on("scalar", f3, inc=se)
        self.wait("sync", se, v_dk)
        self.dma("sync", dC, self.cIOT[:, :], self.i_tri)

        self.wait("vector", se, v_dk)
        self.wait("vector", dC, dC.v)
        def f4(e):
            for h in range(NH):
                e.tensor_scalar(self.sMASK[:, h * 128:(h + 1) * 128], self.cIOT[:, :],
                                self.cGinv[:, h:h + 1], None, ALU.mult)
            for qt in range(4):
                e.tensor_scalar(self.cWMIX[:, qt, :], self.cLNG[:, :],
                                self.cMIX[:, qt:qt + 1], None, ALU.mult)
            e.memset(self.sSf[:, :], 0.0)
            e.memset(self.cEPS[:, :], EPS)
            return e.memset(self.sSb[:, :], 0.0)
        v_mask = self.on("vector", f4, inc=ve)
        self.wait("scalar", ve, v_mask)
        def f5(e):
            return e.activation(out=self.cWMIX[:, :, :], in_=self.cWMIX[:, :, :], func=AF.Exp)
        self.c_tables_se = self.on("scalar", f5, inc=se)
        self.c_tables_ve = v_mask

    def _att_stream_order(self):
        mats = ["wq", "wk", "wv", "wg"]
        order = []
        for half in range(2):
            for i in range(64):
                order.append((mats[i // 16], i % 16))
        for b in range(16):
            order.append(("wo", b))
        return order

    def _issue_att_load(self):
        st = self.att_st
        i = st["next"]
        mat, b = self.att_order[i]
        if i >= 4:
            gate = st["consumed"].get(i - 4)
            if gate:
                self.wait("gpsimd", self.pe, gate)
        src = self.w[mat][:, b * 128:(b + 1) * 128].rearrange("(k p) c -> p k c", p=128)
        ctr = self.dWA[i % 4]
        v = self.dma("gpsimd", ctr, self.sWA[:, i % 4], src)
        st["vals"].append((ctr, v))
        st["next"] += 1

    def _att_block_val(self, idx):
        while self.att_st["next"] <= idx:
            self._issue_att_load()
        return self.att_st["vals"][idx]

    def _att_consumed(self, idx):
        self.att_st["consumed"][idx] = self.pe.v
        if self.att_st["next"] < len(self.att_order):
            self._issue_att_load()

    def phase12_proj_retention(self):
        pe, ve, se = self.pe, self.ve, self.se
        self.att_order = self._att_stream_order()
        self.att_st = {"next": 0, "vals": [], "consumed": {}}
        self.sb_ready = {hg: 0 for hg in range(4)}
        self.scb_busy = {0: 0, 1: 0}
        self.kt_busy = {0: 0, 1: 0}

        for half in range(2):
            if half == 1:
                self.wait("gpsimd", pe, self.v_xb_users)
            v_xb_lo = self.dma(
                "gpsimd", self.dXBa, self.sXB[:, 0:8, :],
                self.x_fm[0:1024, half * 512:(half + 1) * 512].rearrange("(k p) t -> p k t", p=128))
            v_xb = self.dma(
                "gpsimd", self.dXB, self.sXB[:, 8:16, :],
                self.x_fm[1024:2048, half * 512:(half + 1) * 512].rearrange("(k p) t -> p k t", p=128))
            base_blk = half * 64
            if half == 0:
                self.wait("vector", se, self.c_tables_se)

            for wi, wname in enumerate(["wq", "wk"]):
                if SUB12 == "q" and wi == 1:
                    return
                for m in range(NCH):
                    if isinstance(SUB12, str) and SUB12.startswith("q") and len(SUB12) > 1 \
                            and (wi > 0 or m >= int(SUB12[1:])):
                        return
                    blk = base_blk + wi * 16 + m
                    bctr, bval = self._att_block_val(blk)
                    self.wait("tensor", bctr, bval)
                    tile = m % 2
                    self.pe_wait_free(tile)
                    par = blk % 4
                    if wi == 0 and m < 2:
                        self.wait("tensor", self.dXBa, v_xb_lo)
                        def fl(e, par=par, tile=tile):
                            last = None
                            for kc in range(8):
                                last = e.matmul(self.ps[tile][:, 0:512],
                                                self.sWA[:, par, kc, :],
                                                self.sXB[:, kc, :],
                                                start=(kc == 0), stop=False,
                                                skip_group_check=True)
                            return last
                        self.on("tensor", fl, inc=pe)
                        self.wait("tensor", self.dXB, v_xb)
                        def fh(e, par=par, tile=tile):
                            last = None
                            for kc in range(8, NCH):
                                last = e.matmul(self.ps[tile][:, 0:512],
                                                self.sWA[:, par, kc, :],
                                                self.sXB[:, kc, :],
                                                start=False, stop=(kc == NCH - 1),
                                                skip_group_check=True)
                            return last
                        v_mm = self.on("tensor", fh, inc=pe)
                    else:
                        self.wait("tensor", self.dXB, v_xb)
                        def f(e, par=par, tile=tile):
                            last = None
                            for kc in range(NCH):
                                last = e.matmul(self.ps[tile][:, 0:512],
                                                self.sWA[:, par, kc, :],
                                                self.sXB[:, kc, :],
                                                start=(kc == 0), stop=(kc == NCH - 1),
                                                skip_group_check=True)
                            return last
                        v_mm = self.on("tensor", f, inc=pe)
                    self._att_consumed(blk)
                    self.wait("vector", pe, v_mm)
                    if wname == "wq":
                        dst = self.sQ[:, m * T + half * 512: m * T + half * 512 + 512]
                        dec, bias = self.sDQ, self.cBQ
                    else:
                        dst = self.sK[:, m * 512:(m + 1) * 512]
                        dec, bias = self.sDK, self.cBK
                    def fe(e, m=m, tile=tile, dst=dst, dec=dec, bias=bias):
                        din = AP(dec, m * 128, [[NH * 128, 128], [0, 4], [1, 128]])
                        return e.scalar_tensor_tensor(
                            out=dst.rearrange("p (r c) -> p r c", r=4),
                            in0=self.ps[tile][:, 0:512].rearrange("p (r c) -> p r c", r=4),
                            scalar=bias[:, m:m + 1], in1=din,
                            op0=ALU.add, op1=ALU.mult)
                    self.on("vector", fe, inc=ve)
                    self.mark_ps_free(tile, ve)

            if SUB12 == "k":
                return
            pend_tr = [None]
            def emit_vt_transposes():
                m, par, v_csv = pend_tr[0]
                tT = m % 2
                self.pe_wait_free(7 + tT)
                self.wait("tensor", ve, v_csv)
                def f_tr(e, par=par, tT=tT):
                    last = None
                    for ts in range(4):
                        last = e.transpose(
                            self.psTt[tT][:, ts * 128: ts * 128 + 128],
                            self.cSV[:, par, ts * 128: ts * 128 + 128],
                            self.cIDb[:, :])
                    return last
                v_tr = self.on("tensor", f_tr, inc=pe)
                self.csv_busy[par] = v_tr
                self.wait("vector", pe, v_tr)
                def f_tre(e, m=m, tT=tT):
                    out = AP(self.sVT, m * 128, [[4 * HID, 128], [HID, 4], [1, 128]])
                    return e.tensor_copy(
                        out, self.psTt[tT][:, 0:512].rearrange("p (r c) -> p r c", r=4))
                self.on("vector", f_tre, inc=ve)
                self.mark_ps_free(7 + tT, ve)
                pend_tr[0] = None
            self.csv_busy = {0: 0, 1: 0}
            for m in range(NCH):
                blk = base_blk + 32 + m
                bctr, bval = self._att_block_val(blk)
                self.wait("tensor", bctr, bval)
                self.wait("tensor", self.dXB, v_xb)
                tile = 2 + m % 2
                self.pe_wait_free(tile)
                par = blk % 4
                def f(e, par=par, tile=tile):
                    last = None
                    for kc in range(NCH):
                        last = e.matmul(self.ps[tile][:, 0:512],
                                        self.sWA[:, par, kc, :],
                                        self.sXB[:, kc, :],
                                        start=(kc == 0), stop=(kc == NCH - 1),
                                        skip_group_check=True)
                    return last
                v_mm = self.on("tensor", f, inc=pe)
                self._att_consumed(blk)
                cpar = m % 2
                self.wait("vector", pe, max(v_mm, self.csv_busy[cpar]))
                def fe(e, m=m, tile=tile, cpar=cpar):
                    return e.tensor_scalar(self.cSV[:, cpar, :], self.ps[tile][:, 0:512],
                                           self.cBV[:, m:m + 1], None, ALU.add)
                v_csv = self.on("vector", fe, inc=ve)
                self.mark_ps_free(tile, ve)
                if pend_tr[0] is not None:
                    emit_vt_transposes()
                pend_tr[0] = (m, cpar, v_csv)
            emit_vt_transposes()

            if SUB12 == "vt":
                return
            for m in ([] if half == 1 and SUB12 == "all" and STAGE >= 3 else range(NCH)):
                blk = base_blk + 48 + m
                bctr, bval = self._att_block_val(blk)
                self.wait("tensor", bctr, bval)
                self.wait("tensor", self.dXB, v_xb)
                tile = m % 2
                self.pe_wait_free(tile)
                par = blk % 4
                def f(e, par=par, tile=tile):
                    last = None
                    for kc in range(NCH):
                        last = e.matmul(self.ps[tile][:, 0:512],
                                        self.sWA[:, par, kc, :],
                                        self.sXB[:, kc, :],
                                        start=(kc == 0), stop=(kc == NCH - 1),
                                        skip_group_check=True)
                    return last
                v_mm = self.on("tensor", f, inc=pe)
                self._att_consumed(blk)
                self.wait("scalar", pe, v_mm)
                def fe(e, m=m, tile=tile, half=half):
                    return e.activation(
                        out=self.sG[:, m * T + half * 512: m * T + half * 512 + 512],
                        in_=self.ps[tile][:, 0:512], func=AF.Silu,
                        bias=self.cBG[:, m:m + 1])
                self.on("scalar", fe, inc=se)
                self.mark_ps_free(tile, se)

            self.v_xb_users = pe.v

            if SUB12 == "g" or STAGE < 2:
                self.v_ret_done_ve = ve.v
                return

            if half == 0:
                self.wait("tensor", se, self.c_tables_se)
                self.wait("tensor", ve, self.c_tables_ve)

            def stage_a(tl, hg):
                gt = half * 4 + tl
                tS = hg % 2
                tT = hg % 2
                self.pe_wait_free(tS)
                self.pe_wait_free(7 + tT)
                def f_a(e, tl=tl, gt=gt, hg=hg, tS=tS, tT=tT):
                    for hl in range(4):
                        h = hg * 4 + hl
                        e.matmul(
                            self.ps[tS][:, hl * 128: hl * 128 + 128],
                            self.sK[:, h * 512 + tl * 128: h * 512 + tl * 128 + 128],
                            self.sQ[:, h * T + gt * 128: h * T + gt * 128 + 128],
                            start=True, stop=True, skip_group_check=True)
                    last = None
                    for hl in range(4):
                        h = hg * 4 + hl
                        last = e.transpose(
                            self.psTt[tT][:, hl * 128: hl * 128 + 128],
                            self.sK[:, h * 512 + tl * 128: h * 512 + tl * 128 + 128],
                            self.cIDb[:, :])
                    return last
                v_a = self.on("tensor", f_a, inc=pe)
                self.wait("vector", pe, max(v_a, self.scb_busy[hg % 2], self.kt_busy[tT]))
                def f_me(e, hg=hg, tS=tS):
                    return e.tensor_tensor(self.cSCB[:, hg % 2, :],
                                           self.ps[tS][:, 0:512],
                                           self.sMASK[:, hg * 512: hg * 512 + 512],
                                           ALU.mult)
                v_me = self.on("vector", f_me, inc=ve)
                self.mark_ps_free(tS, ve)
                def f_kte(e, tT=tT):
                    return e.tensor_copy(self.cKT2[:, tT, :],
                                         self.psTt[tT][:, 0:512])
                v_kte = self.on("vector", f_kte, inc=ve)
                self.mark_ps_free(7 + tT, ve)
                return (tl, hg, v_me, v_kte)

            def stage_b(rec):
                tl, hg, v_me, v_kte = rec
                gt = half * 4 + tl
                tR = 2 + hg % 2
                tSt = 4 + hg % 2
                tT = hg % 2
                self.pe_wait_free(tR)
                self.pe_wait_free(tSt)
                self.wait("tensor", ve, v_kte)
                if self.sb_ready[hg]:
                    self.wait("tensor", ve, self.sb_ready[hg])
                def f_b(e, tl=tl, gt=gt, hg=hg, tR=tR, tSt=tSt, tT=tT):
                    for hl in range(4):
                        h = hg * 4 + hl
                        e.matmul(self.ps[tR][:, hl * 128: hl * 128 + 128],
                                 self.sVT[:, tl * HID + h * 128: tl * HID + h * 128 + 128],
                                 self.cSCB[:, hg % 2, hl * 128: hl * 128 + 128],
                                 start=True, stop=False, skip_group_check=True)
                        e.matmul(self.ps[tR][:, hl * 128: hl * 128 + 128],
                                 self.sSb[:, h * 128: h * 128 + 128],
                                 self.sQ[:, h * T + gt * 128: h * T + gt * 128 + 128],
                                 start=False, stop=True, skip_group_check=True)
                    last = None
                    for hl in range(4):
                        h = hg * 4 + hl
                        last = e.matmul(
                            self.ps[tSt][:, hl * 128: hl * 128 + 128],
                            self.cKT2[:, tT, hl * 128: hl * 128 + 128],
                            self.sVT[:, tl * HID + h * 128: tl * HID + h * 128 + 128],
                            start=True, stop=True, skip_group_check=True)
                    return last
                v_b = self.on("tensor", f_b, inc=pe)
                self.scb_busy[hg % 2] = v_b
                self.kt_busy[tT] = v_b
                self.wait("vector", pe, v_b)
                def f_re(e, gt=gt, hg=hg, tR=tR):
                    out = AP(self.sRET, (hg * 4) * T + gt * 128,
                             [[NH * T, 128], [T, 4], [1, 128]])
                    return e.tensor_copy(
                        out, self.ps[tR][:, 0:512].rearrange("p (r c) -> p r c", r=4))
                self.on("vector", f_re, inc=ve)
                self.mark_ps_free(tR, ve)
                def f_su(e, hg=hg, tSt=tSt):
                    last = None
                    for hl in range(4):
                        h = hg * 4 + hl
                        last = e.scalar_tensor_tensor(
                            out=self.sSf[:, h * 128: h * 128 + 128],
                            in0=self.sSf[:, h * 128: h * 128 + 128],
                            scalar=self.cG128[:, h:h + 1],
                            in1=self.ps[tSt][:, hl * 128: hl * 128 + 128],
                            op0=ALU.mult, op1=ALU.add)
                    return last
                v_sup = self.on("vector", f_su, inc=ve)
                self.mark_ps_free(tSt, ve)
                self.wait("vector", ve, v_sup)
                def f_sub(e, hg=hg):
                    return e.tensor_copy(self.sSb[:, hg * 512: hg * 512 + 512],
                                         self.sSf[:, hg * 512: hg * 512 + 512])
                self.sb_ready[hg] = self.on("vector", f_sub, inc=ve)

            items = [(tl, hg) for tl in range(4) for hg in range(4)]
            prev = None
            for it in items:
                rec = stage_a(*it)
                if prev is not None:
                    stage_b(prev)
                prev = rec
            stage_b(prev)
        self.v_ret_done_ve = self.ve.v

    def _emit_g_half1(self):
        pe, se = self.pe, self.se
        for m in range(NCH):
            blk = 64 + 48 + m
            bctr, bval = self._att_block_val(blk)
            self.wait("tensor", bctr, bval)
            tile = m % 2
            self.pe_wait_free(tile)
            par = blk % 4
            def f(e, par=par, tile=tile):
                last = None
                for kc in range(NCH):
                    last = e.matmul(self.ps[tile][:, 0:512],
                                    self.sWA[:, par, kc, :],
                                    self.sXB[:, kc, :],
                                    start=(kc == 0), stop=(kc == NCH - 1),
                                    skip_group_check=True)
                return last
            v_mm = self.on("tensor", f, inc=pe)
            self._att_consumed(blk)
            self.wait("scalar", pe, v_mm)
            def fe(e, m=m, tile=tile):
                return e.activation(
                    out=self.sG[:, m * T + 512: m * T + 1024],
                    in_=self.ps[tile][:, 0:512], func=AF.Silu,
                    bias=self.cBG[:, m:m + 1])
            self.on("scalar", fe, inc=se)
            self.mark_ps_free(tile, se)

    def phase3_states_correction(self):
        pe, ve, cc, se = self.pe, self.ve, self.cc, self.se
        self.wait("sync", ve, self.v_ret_done_ve)
        v_l = self.dma("sync", self.dLL, self.l_loc, self.sSf[:, :])
        self.wait("gpsimd", self.dLL, v_l)
        cc.v += 1
        def f_cc(e):
            e.collective_compute(
                "AllGather", ALU.bypass,
                replica_groups=[[0, 1, 2, 3], [4, 5, 6, 7]],
                ins=[self.l_loc], outs=[self.l_all],
            ).then_inc(self.cc.sem, 1)
        self.prog.append(("gpsimd", f_cc))
        v_ccag = cc.v
        self._emit_g_half1()
        self.wait("sync", cc, v_ccag)
        v_lb = self.dma("sync", self.dSL, self.sLB[:, :, :],
                        self.l_all.rearrange("(q p) c -> p q c", p=128))
        self.wait("vector", self.dSL, v_lb)
        v_acc = 0
        for qt in range(4):
            if v_acc:
                self.wait("vector", ve, v_acc)
            def f_acc(e, qt=qt):
                last = None
                for h in range(NH):
                    if qt == 0:
                        last = e.tensor_scalar(
                            self.sSin[:, h * 128: h * 128 + 128],
                            self.sLB[:, qt, h * 128: h * 128 + 128],
                            self.cWMIX[:, qt, h:h + 1], None, ALU.mult)
                    else:
                        last = e.scalar_tensor_tensor(
                            out=self.sSin[:, h * 128: h * 128 + 128],
                            in0=self.sLB[:, qt, h * 128: h * 128 + 128],
                            scalar=self.cWMIX[:, qt, h:h + 1],
                            in1=self.sSin[:, h * 128: h * 128 + 128],
                            op0=ALU.mult, op1=ALU.add)
                return last
            v_acc = self.on("vector", f_acc, inc=ve)
        v_sin = v_acc

        self.wait("tensor", ve, v_sin)
        stats_pend = []
        def drain_stat():
            fn, vdep = stats_pend.pop(0)
            self.wait("vector", ve, vdep)
            self.on("vector", fn, inc=ve)
        for hg in range(4):
            for gt in range(8):
                if stats_pend and gt % 2 == 0:
                    drain_stat()
                tile = gt % 2
                self.pe_wait_free(tile)
                def f_mm(e, gt=gt, hg=hg, tile=tile):
                    last = None
                    for hl in range(4):
                        h = hg * 4 + hl
                        last = e.matmul(
                            self.ps[tile][:, hl * 128: hl * 128 + 128],
                            self.sSin[:, h * 128: h * 128 + 128],
                            self.sQ[:, h * T + gt * 128: h * T + gt * 128 + 128],
                            start=True, stop=True, skip_group_check=True)
                    return last
                v_mm = self.on("tensor", f_mm, inc=pe)
                self.wait("vector", pe, v_mm)
                def f_ev(e, gt=gt, hg=hg, tile=tile):
                    last = None
                    for hl in range(4):
                        h = hg * 4 + hl
                        dst = self.sRET[:, h * T + gt * 128: h * T + gt * 128 + 128]
                        last = e.scalar_tensor_tensor(
                            out=dst,
                            in0=self.ps[tile][:, hl * 128: hl * 128 + 128],
                            scalar=self.cG128t[:, gt, h:h + 1],
                            in1=dst, op0=ALU.mult, op1=ALU.add)
                    return last
                self.on("vector", f_ev, inc=ve)
                self.mark_ps_free(tile, ve)
            v_done = self.ve.v
            self.wait("scalar", ve, v_done)
            for hl in range(4):
                self.wait("scalar", se, self.se.v)
                def f_q(e, h=hg * 4 + hl):
                    return e.activation(out=self.sSq[:, :],
                                        in_=self.sRET[:, h * T:(h + 1) * T],
                                        func=AF.Square,
                                        accum_out=self.cSUM[:, 16 + h:17 + h])
                self.on("scalar", f_q, inc=se)
            for hl in range(4):
                def f_s(e, h=hg * 4 + hl):
                    return e.reduce_sum(self.cSUM[:, h:h + 1],
                                        self.sRET[:, h * T:(h + 1) * T],
                                        mybir.AxisListType.X)
                stats_pend.append((f_s, v_done))
            if hg == 3:
                while stats_pend:
                    drain_stat()
        self.v_ret_final_ve = ve.v

    def phase35_stats(self):
        pe, ve, se, cc = self.pe, self.ve, self.se, self.cc
        v_sums = self.ve.v
        v_sq = self.se.v
        self.wait("tensor", ve, v_sums)
        self.wait("tensor", se, v_sq)
        self.pe_wait_free(5)
        def f_rr(e):
            return e.matmul(self.ps[5][0:32, 0:1], self.cSUM[:, :], self.cONEC[:, :],
                            start=True, stop=True, skip_group_check=True)
        v_rr = self.on("tensor", f_rr, inc=pe)
        self.wait("vector", pe, v_rr)
        def f_ev(e):
            return e.tensor_copy(self.cTOT[:, :], self.ps[5][0:32, 0:1])
        v_tot = self.on("vector", f_ev, inc=ve)
        self.mark_ps_free(5, ve)
        self.wait("sync", ve, v_tot)
        v_st = self.dma("sync", self.dST, self.st_loc, self.cTOT[:, :])
        self.wait("gpsimd", self.dST, v_st)
        cc.v += 1
        def f_cc(e):
            e.collective_compute(
                "AllReduce", ALU.add,
                replica_groups=[[0, 1, 2, 3], [4, 5, 6, 7]],
                ins=[self.st_loc], outs=[self.st_all],
            ).then_inc(cc.sem, 1)
        self.prog.append(("gpsimd", f_cc))
        self.wait("sync", cc, cc.v)
        self.dma("sync", self.dTR, self.cTRa[0:1, :], self.st_all[0:16, 0:1])
        v_t2 = self.dma("sync", self.dTR, self.cTRb[0:1, :], self.st_all[16:32, 0:1])
        self.wait("tensor", self.dTR, v_t2)
        self.pe_wait_free(5)
        def f_rep(e):
            e.matmul(self.ps[5][:, 0:16], self.cONER[0:1, :], self.cTRa[0:1, :],
                     start=True, stop=True, skip_group_check=True)
            return e.matmul(self.ps[5][:, 16:32], self.cONER[0:1, :], self.cTRb[0:1, :],
                            start=True, stop=True, skip_group_check=True)
        v_rep = self.on("tensor", f_rep, inc=pe)
        self.wait("vector", pe, v_rep)
        def f_c1(e):
            e.tensor_scalar_mul(self.cMUr[:, :], self.ps[5][:, 0:16], 1.0 / NTOK_GN)
            return e.tensor_scalar_mul(self.cRSr[:, :], self.ps[5][:, 16:32], 1.0 / NTOK_GN)
        v_c1 = self.on("vector", f_c1, inc=ve)
        self.mark_ps_free(5, ve)
        self.wait("vector", ve, v_c1)
        def f_c2a(e):
            return e.tensor_mul(self.cACOL[:, :], self.cMUr[:, :], self.cMUr[:, :])
        v_c2a = self.on("vector", f_c2a, inc=ve)
        self.wait("vector", ve, v_c2a)
        def f_c2(e):
            return e.tensor_sub(self.cRSr[:, :], self.cRSr[:, :], self.cACOL[:, :])
        v_c2 = self.on("vector", f_c2, inc=ve)
        self.wait("scalar", ve, v_c2)
        def f_sd(e):
            return e.activation(out=self.cRSr[:, :], in_=self.cRSr[:, :], func=AF.Sqrt,
                                bias=self.cEPS[:, 0:1])
        v_sd = self.on("scalar", f_sd, inc=se)
        self.wait("vector", se, v_sd)
        def f_rs(e):
            return e.reciprocal(self.cRSr[:, :], self.cRSr[:, :])
        v_rs = self.on("vector", f_rs, inc=ve)
        self.wait("vector", ve, v_rs)
        def f_ab1(e):
            return e.tensor_mul(self.cACOL[:, :], self.cRSr[:, :], self.cGNW[:, :])
        v_a1 = self.on("vector", f_ab1, inc=ve)
        self.wait("vector", ve, v_a1)
        def f_ab2(e):
            return e.tensor_mul(self.cMUr[:, :], self.cMUr[:, :], self.cACOL[:, :])
        v_a2 = self.on("vector", f_ab2, inc=ve)
        self.wait("vector", ve, v_a2)
        def f_ab3(e):
            return e.tensor_sub(self.cBCOL[:, :], self.cGNB[:, :], self.cMUr[:, :])
        self.v_ab = self.on("vector", f_ab3, inc=ve)
        self.mark_ps_free(5, ve)

    def phase4_norm_gate(self):
        ve, se, gp = self.ve, self.se, self.gp
        self.wait("vector", ve, self.v_ab)
        self.wait("vector", se, self.se.v)
        for h in range(NH):
            par = 4 + h % 2
            def f0(e, h=h, par=par):
                return e.tensor_scalar(self.sSP[:, par, :], self.sRET[:, h * T:(h + 1) * T],
                                       self.cACOL[:, h:h + 1], self.cBCOL[:, h:h + 1],
                                       ALU.mult, ALU.add)
            v0 = self.on("vector", f0, inc=ve)
            self.wait("vector", ve, v0)
            def f1(e, h=h, par=par):
                return e.tensor_tensor(self.sRET[:, h * T:(h + 1) * T],
                                       self.sSP[:, par, :],
                                       self.sG[:, h * T:(h + 1) * T], ALU.mult)
            self.on("vector", f1, inc=ve)
        self.v_gated = self.ve.v
        self.v_gated_gp = 0

    def phase5_wo(self):
        pe, ve = self.pe, self.ve
        self.wait("tensor", ve, self.v_gated)
        self.wait("tensor", self.gp, self.v_gated_gp)
        x1_free = {0: 0, 1: 0}
        for m in range(NCH):
            blk = 128 + m
            bctr, bval = self._att_block_val(blk)
            par_x = m % 2
            if x1_free[par_x]:
                self.wait("sync", ve, x1_free[par_x])
            v_x = self.dma("sync", self.dX1[par_x], self.sX1[:, par_x, :],
                           self.x_fm[m * 128:(m + 1) * 128, :])
            self.wait("tensor", bctr, bval)
            for th in range(2):
                tile = th
                self.pe_wait_free(tile)
                parw = blk % 4
                def f(e, parw=parw, tile=tile, th=th):
                    last = None
                    for kc in range(NCH):
                        last = e.matmul(
                            self.ps[tile][:, 0:512],
                            self.sWA[:, parw, kc, :],
                            self.sRET[:, kc * T + th * 512: kc * T + th * 512 + 512],
                            start=(kc == 0), stop=(kc == NCH - 1),
                            skip_group_check=True)
                    return last
                v_mm = self.on("tensor", f, inc=pe)
                self.wait("vector", pe, v_mm)
                self.wait("vector", self.dX1[par_x], v_x)
                def fe(e, m=m, th=th, tile=tile, par_x=par_x):
                    return e.scalar_tensor_tensor(
                        out=self.sA2[:, m * T + th * 512: m * T + th * 512 + 512],
                        in0=self.ps[tile][:, 0:512],
                        scalar=self.cBO[:, m:m + 1],
                        in1=self.sX1[:, par_x, th * 512: th * 512 + 512],
                        op0=ALU.add, op1=ALU.add)
                self.on("vector", fe, inc=ve)
                self.mark_ps_free(tile, ve)
            x1_free[par_x] = ve.v
            self._att_consumed(blk)
        self.v_resid1 = ve.v
        self.v_wo_pe = pe.v

    def _ln_stats(self):
        pe, ve, se = self.pe, self.ve, self.se
        for t in range(4):
            self.pe_wait_free(t)
        cb_busy = {0: 0, 1: 0}
        for kc in range(NCH):
            pcb = kc % 2
            if cb_busy[pcb]:
                self.wait("vector", pe, cb_busy[pcb])
                self.wait("scalar", pe, cb_busy[pcb])
            def f_c(e, kc=kc, pcb=pcb):
                return e.tensor_copy(self.sCB[:, pcb, :], self.sA2[:, kc * T:(kc + 1) * T])
            v_c = self.on("vector", f_c, inc=ve)
            def f_q(e, kc=kc, pcb=pcb):
                return e.activation(out=self.sSP[:, 4 + pcb, :],
                                    in_=self.sA2[:, kc * T:(kc + 1) * T],
                                    func=AF.Square)
            v_q = self.on("scalar", f_q, inc=se)
            self.wait("tensor", ve, v_c)
            self.wait("tensor", se, v_q)
            def f_mm(e, kc=kc, pcb=pcb):
                last = None
                for th in range(2):
                    e.matmul(self.ps[0 + th][:, 0:512], self.cONES[:, :],
                             self.sCB[:, pcb, th * 512: th * 512 + 512],
                             start=(kc == 0), stop=(kc == NCH - 1),
                             skip_group_check=True)
                    last = e.matmul(self.ps[2 + th][:, 0:512], self.cONES[:, :],
                                    self.sSP[:, 4 + pcb, th * 512: th * 512 + 512],
                                    start=(kc == 0), stop=(kc == NCH - 1),
                                    skip_group_check=True)
                return last
            cb_busy[pcb] = self.on("tensor", f_mm, inc=pe)
        v_stats_pe = self.pe.v
        self.wait("vector", pe, v_stats_pe)
        def f_mr0(e):
            last = None
            for th in range(2):
                sl = slice(th * 512, th * 512 + 512)
                e.tensor_scalar_mul(self.sM[:, sl], self.ps[th][:, 0:512], 1.0 / NTOK_LN)
                last = e.tensor_scalar_mul(self.sSC1[:, sl], self.ps[2 + th][:, 0:512],
                                           1.0 / NTOK_LN)
            return last
        v_mr0 = self.on("vector", f_mr0, inc=ve)
        self.wait("vector", ve, v_mr0)
        def f_mr1(e):
            return e.tensor_mul(self.sSC2[:, :], self.sM[:, :], self.sM[:, :])
        v_mr1 = self.on("vector", f_mr1, inc=ve)
        self.wait("vector", ve, v_mr1)
        def f_mr2(e):
            return e.tensor_sub(self.sSC1[:, :], self.sSC1[:, :], self.sSC2[:, :])
        v_var = self.on("vector", f_mr2, inc=ve)
        for t in range(4):
            self.mark_ps_free(t, ve)
        self.wait("scalar", ve, v_var)
        def f_sd(e):
            return e.activation(out=self.sSC2[:, :], in_=self.sSC1[:, :], func=AF.Sqrt,
                                bias=self.cEPS[:, 0:1])
        v_sd = self.on("scalar", f_sd, inc=se)
        self.wait("vector", se, v_sd)
        def f_r(e):
            return e.reciprocal(self.sR[:, :], self.sSC2[:, :])
        return self.on("vector", f_r, inc=ve)

    def phase6_ln1(self):
        pe, ve = self.pe, self.ve
        self.wait("vector", pe, self.v_wo_pe)
        self.wait("vector", ve, self.v_resid1)
        self.wait("scalar", pe, self.v_wo_pe)
        self.wait("scalar", ve, self.v_resid1)
        self.wait("tensor", ve, self.v_resid1)
        v_lnr = self._ln_stats()
        self.wait("vector", ve, v_lnr)
        def f_mr(e):
            return e.tensor_mul(self.sMR[:, :], self.sM[:, :], self.sR[:, :])
        v_mrp = self.on("vector", f_mr, inc=ve)
        self.wait("vector", ve, v_mrp)
        def f_nmr(e):
            return e.tensor_scalar_mul(self.cNMR[0:1, :], self.sMR[0:1, :], -1.0)
        self.v_nmr = self.on("vector", f_nmr, inc=ve)
        self.v_ychunk = {}
        for kc in range(NCH):
            scr = self.sSC1 if kc % 2 == 0 else self.sSC2
            def f0(e, kc=kc, scr=scr):
                return e.tensor_sub(scr[:, :], self.sA2[:, kc * T:(kc + 1) * T], self.sM[:, :])
            v0 = self.on("vector", f0, inc=ve)
            self.wait("vector", ve, v0)
            def f1(e, kc=kc, scr=scr):
                return e.tensor_tensor(self.sY[:, kc * T:(kc + 1) * T], scr[:, :],
                                       self.sR[:, :], ALU.mult)
            self.v_ychunk[kc] = self.on("vector", f1, inc=ve)
        self.v_ybf = self.ve.v

    def _ffn_issue(self, kind, j):
        st = self.ffn_st[kind]
        while st["next"] <= j:
            i = st["next"]
            if i >= 2:
                gate = st["consumed"].get(i - 2)
                if gate:
                    self.wait("gpsimd", self.pe, gate)
            if kind == "g":
                srcap = self.wgate[:, i * 128:(i + 1) * 128]
                ctr, dst = self.dFG[i % 2], self.sWF[:, i % 2]
            elif kind == "u":
                srcap = self.wup[:, i * 128:(i + 1) * 128]
                ctr, dst = self.dFU[i % 2], self.sWF[:, 2 + i % 2]
            else:
                iq, mo = i // 16, i % 16
                srcap = self.wdown[iq * 2048:(iq + 1) * 2048, mo * 128:(mo + 1) * 128]
                ctr, dst = self.dFD[i % 2], self.sWD[:, i % 2]
            v = self.dma("gpsimd", ctr, dst, srcap.rearrange("(k p) c -> p k c", p=128))
            st["vals"].append((ctr, v))
            st["next"] += 1
        return st["vals"][j]

    def _ffn_consumed(self, kind, j):
        st = self.ffn_st[kind]
        st["consumed"][j] = self.pe.v
        if st["next"] <= j + 2 and st["next"] < st["count"]:
            self._ffn_issue(kind, min(j + 2, st["count"] - 1))

    def _down_evac(self, iq, sl, tile):
        ve = self.ve
        if iq == 0:
            scr = self.sSC1 if tile % 2 == 0 else self.sSC2
            def fe0(e, sl=sl, scr=scr):
                th0 = (sl.start % T) // 512
                tsl = slice(th0 * 512, th0 * 512 + 512)
                return e.tensor_tensor(scr[:, 0:512], self.sA2[:, sl],
                                       self.sR[:, tsl], ALU.mult)
            v0 = self.on("vector", fe0, inc=ve)
            self.wait("vector", ve, v0)
            def fe2(e, sl=sl, tile=tile, scr=scr):
                return e.tensor_add(self.sA2[:, sl], scr[:, 0:512],
                                    self.ps[tile][:, 0:512])
            self.on("vector", fe2, inc=ve)
        else:
            def fe(e, sl=sl, tile=tile):
                return e.tensor_add(self.sA2[:, sl], self.sA2[:, sl],
                                    self.ps[tile][:, 0:512])
            self.on("vector", fe, inc=ve)

    def phase7_ffn(self):
        pe, ve, se = self.pe, self.ve, self.se
        self.ffn_st = {k: {"next": 0, "vals": [], "consumed": {}, "count": 64}
                       for k in ("g", "u", "d")}
        self.wait("gpsimd", self.pe, self.v_wo_pe)
        sg_busy = {0: 0, 1: 0}
        for iq in range(4):
            pending = []
            for mi in range(16):
                gmi = iq * 16 + mi
                gctr, gval = self._ffn_issue("g", gmi)
                uctr, uval = self._ffn_issue("u", gmi)
                self.wait("tensor", gctr, gval)
                self.wait("tensor", uctr, uval)
                for th in range(2):
                    tG = mi % 2
                    tU = 2 + mi % 2
                    self.pe_wait_free(tG)
                    self.pe_wait_free(tU)
                    if iq == 0 and mi == 0:
                        for piece in range(4):
                            self.wait("tensor", ve, self.v_ychunk[piece * 4 + 3])
                            def fp(e, th=th, tG=tG, tU=tU, piece=piece):
                                last = None
                                for kc in range(piece * 4, piece * 4 + 4):
                                    e.matmul(self.ps[tG][:, 0:512],
                                             self.sWF[:, 0, kc, :],
                                             self.sY[:, kc * T + th * 512: kc * T + th * 512 + 512],
                                             start=(kc == 0), stop=(kc == NCH - 1),
                                             skip_group_check=True)
                                    last = e.matmul(
                                        self.ps[tU][:, 0:512],
                                        self.sWF[:, 2, kc, :],
                                        self.sY[:, kc * T + th * 512: kc * T + th * 512 + 512],
                                        start=(kc == 0), stop=(kc == NCH - 1),
                                        skip_group_check=True)
                                return last
                            v_mm = self.on("tensor", fp, inc=pe)
                    else:
                        def f(e, mi=mi, th=th, tG=tG, tU=tU):
                            for kc in range(NCH):
                                e.matmul(self.ps[tG][:, 0:512],
                                         self.sWF[:, mi % 2, kc, :],
                                         self.sY[:, kc * T + th * 512: kc * T + th * 512 + 512],
                                         start=(kc == 0), stop=(kc == NCH - 1),
                                         skip_group_check=True)
                            last = None
                            for kc in range(NCH):
                                last = e.matmul(
                                    self.ps[tU][:, 0:512],
                                    self.sWF[:, 2 + mi % 2, kc, :],
                                    self.sY[:, kc * T + th * 512: kc * T + th * 512 + 512],
                                    start=(kc == 0), stop=(kc == NCH - 1),
                                    skip_group_check=True)
                            return last
                        v_mm = self.on("tensor", f, inc=pe)
                    self.wait("scalar", pe, v_mm)
                    if sg_busy[th]:
                        self.wait("scalar", ve, sg_busy[th])
                    def f_s(e, tG=tG, th=th):
                        return e.activation(out=self.cSG[:, th, :], in_=self.ps[tG][:, 0:512],
                                            func=AF.Silu)
                    v_si = self.on("scalar", f_s, inc=se)
                    self.mark_ps_free(tG, se)
                    self.wait("vector", se, v_si)
                    self.wait("vector", pe, v_mm)
                    def f_h(e, mi=mi, th=th, tU=tU):
                        if RAW_H:
                            return e.tensor_copy(
                                self.sH[:, mi * T + th * 512: mi * T + th * 512 + 512],
                                self.ps[tU][:, 0:512])
                        return e.tensor_tensor(
                            self.sH[:, mi * T + th * 512: mi * T + th * 512 + 512],
                            self.cSG[:, th, :], self.ps[tU][:, 0:512], ALU.mult)
                    v_h = self.on("vector", f_h, inc=ve)
                    sg_busy[th] = v_h
                    self.mark_ps_free(tU, ve)
                self._ffn_consumed("g", gmi)
                self._ffn_consumed("u", gmi)
                if mi == 15 and not SKIP_SPARSIFY:
                    while pending:
                        pending.pop(0)()
                    v_sp012 = self.ve.v
                    pending += self._sparsify_steps(3)
                if mi % 4 == 3 and mi != 15 and not SKIP_SPARSIFY:
                    pending += self._sparsify_steps(mi // 4)
                for _ in range(5):
                    if pending:
                        pending.pop(0)()
            while pending:
                pending.pop(0)()
            v_sp3 = self.ve.v
            if SKIP_SPARSIFY:
                v_sp012 = v_sp3
            if iq == 0:
                self.wait("tensor", ve, self.v_nmr)
            head_mo = 2
            for mo in range(head_mo):
                dctr, dval = self._ffn_issue("d", iq * 16 + mo)
                self.wait("tensor", dctr, dval)
                self.wait("tensor", ve, v_sp012)
                for th in range(2):
                    tile = 2 * mo + th
                    self.pe_wait_free(tile)
                    def fA(e, mo=mo, th=th, tile=tile):
                        last = None
                        for ik in range(12):
                            last = e.matmul(
                                self.ps[tile][:, 0:512],
                                self.sWD[:, mo % 2, ik, :],
                                self.sH[:, ik * T + th * 512: ik * T + th * 512 + 512],
                                start=(ik == 0), stop=False,
                                skip_group_check=True)
                        return last
                    self.on("tensor", fA, inc=pe)
            self.wait("tensor", ve, v_sp3)
            for mo in range(head_mo):
                for th in range(2):
                    tile = 2 * mo + th
                    def fB(e, mo=mo, th=th, tile=tile, iq=iq):
                        last = None
                        for ik in range(12, 16):
                            last = e.matmul(
                                self.ps[tile][:, 0:512],
                                self.sWD[:, mo % 2, ik, :],
                                self.sH[:, ik * T + th * 512: ik * T + th * 512 + 512],
                                start=False, stop=(ik == 15 and iq != 0),
                                skip_group_check=True)
                        if iq == 0:
                            last = e.matmul(
                                self.ps[tile][:, 0:512], self.cONERb[0:1, :],
                                self.cNMR[0:1, th * 512: th * 512 + 512],
                                start=False, stop=True, skip_group_check=True)
                        return last
                    v_mm = self.on("tensor", fB, inc=pe)
                    self.wait("vector", pe, v_mm)
                    sl = slice(mo * T + th * 512, mo * T + th * 512 + 512)
                    self._down_evac(iq, sl, tile)
                    self.mark_ps_free(tile, ve)
                self._ffn_consumed("d", iq * 16 + mo)
            for mo in range(head_mo, NCH):
                gmo = iq * 16 + mo
                dctr, dval = self._ffn_issue("d", gmo)
                self.wait("tensor", dctr, dval)
                for th in range(2):
                    tile = 4 + th
                    self.pe_wait_free(tile)
                    def f(e, mo=mo, th=th, tile=tile, iq=iq):
                        last = None
                        for ik in range(16):
                            last = e.matmul(
                                self.ps[tile][:, 0:512],
                                self.sWD[:, mo % 2, ik, :],
                                self.sH[:, ik * T + th * 512: ik * T + th * 512 + 512],
                                start=(ik == 0), stop=(ik == 15 and iq != 0),
                                skip_group_check=True)
                        if iq == 0:
                            last = e.matmul(
                                self.ps[tile][:, 0:512], self.cONERb[0:1, :],
                                self.cNMR[0:1, th * 512: th * 512 + 512],
                                start=False, stop=True, skip_group_check=True)
                        return last
                    v_mm = self.on("tensor", f, inc=pe)
                    self.wait("vector", pe, v_mm)
                    sl = slice(mo * T + th * 512, mo * T + th * 512 + 512)
                    self._down_evac(iq, sl, tile)
                    self.mark_ps_free(tile, ve)
                self._ffn_consumed("d", gmo)
        self.v_resid2 = self.ve.v
        self.v_ffn_pe = self.pe.v

    def _sparsify_steps(self, c0):
        ve, se = self.ve, self.se
        sl = [slice((4 * c0 + j) * T, (4 * c0 + j) * T + T) for j in range(4)]
        s = [self.sSP[:, i, :] for i in range(6)]
        st = {}
        def t0():
            self.wait("scalar", ve, self.ve.v)
            def fa01(e):
                e.activation(out=s[3], in_=self.sH[:, sl[0]], func=AF.Abs)
                return e.activation(out=s[4], in_=self.sH[:, sl[1]], func=AF.Abs)
            st["a01"] = self.on("scalar", fa01, inc=se)
        def t1():
            self.wait("vector", se, st["a01"])
            def fmm01(e):
                e.tensor_tensor(s[0], s[3], s[4], ALU.max)
                return e.tensor_tensor(s[1], s[3], s[4], ALU.min)
            st["mm01"] = self.on("vector", fmm01, inc=ve)
        def t2():
            self.wait("scalar", ve, st["mm01"])
            def fa23(e):
                e.activation(out=s[3], in_=self.sH[:, sl[2]], func=AF.Abs)
                return e.activation(out=s[4], in_=self.sH[:, sl[3]], func=AF.Abs)
            st["a23"] = self.on("scalar", fa23, inc=se)
        def t3():
            self.wait("vector", se, st["a23"])
            def fmm23(e):
                e.tensor_tensor(s[2], s[3], s[4], ALU.max)
                return e.tensor_tensor(s[5], s[3], s[4], ALU.min)
            st["mm23"] = self.on("vector", fmm23, inc=ve)
        def t4():
            self.wait("vector", ve, st["mm23"])
            def fab(e):
                e.tensor_tensor(s[0], s[0], s[2], ALU.min)
                return e.tensor_tensor(s[1], s[1], s[5], ALU.max)
            st["ab"] = self.on("vector", fab, inc=ve)
        def t5():
            self.wait("vector", ve, st["ab"])
            def fthr(e):
                return e.tensor_tensor(s[2], s[0], s[1], ALU.max)
            st["thr"] = self.on("vector", fthr, inc=ve)
        steps = [t0, t1, t2, t3, t4, t5]
        for j in range(4):
            def tj_a(j=j):
                self.wait("scalar", ve, self.ve.v)
                def f6(e, j=j):
                    return e.activation(out=s[3], in_=self.sH[:, sl[j]], func=AF.Abs)
                st["aj"] = self.on("scalar", f6, inc=se)
            def tj_b(j=j):
                self.wait("vector", se, st["aj"])
                def f7a(e):
                    return e.tensor_tensor(s[4], s[3], s[2], ALU.is_ge)
                st["ge"] = self.on("vector", f7a, inc=ve)
            def tj_c(j=j):
                self.wait("vector", ve, st["ge"])
                def f7(e, j=j):
                    return e.tensor_tensor(self.sH[:, sl[j]], self.sH[:, sl[j]], s[4], ALU.mult)
                self.on("vector", f7, inc=ve)
            steps += [tj_a, tj_b, tj_c]
        return steps

    def phase8_ln2_out(self):
        pe, ve, dO = self.pe, self.ve, self.dO
        self.wait("vector", pe, self.v_ffn_pe)
        self.wait("vector", ve, self.v_resid2)
        self.wait("scalar", pe, self.v_ffn_pe)
        self.wait("scalar", ve, self.v_resid2)
        self.wait("tensor", ve, self.v_resid2)
        self._ln_stats()
        for kc in range(NCH):
            par = kc % 2
            def f0(e, kc=kc, par=par):
                scr = self.sSC1 if par == 0 else self.sSC2
                return e.tensor_sub(scr[:, :], self.sA2[:, kc * T:(kc + 1) * T], self.sM[:, :])
            v0 = self.on("vector", f0, inc=ve)
            self.wait("vector", ve, v0)
            def f1(e, kc=kc, par=par):
                scr = self.sSC1 if par == 0 else self.sSC2
                return e.tensor_tensor(self.sA2[:, kc * T:(kc + 1) * T], scr[:, :],
                                       self.sR[:, :], ALU.mult)
            v_z = self.on("vector", f1, inc=ve)
            self.wait("sync", ve, v_z)
            self.dma("sync", dO, self.out_fm[kc * 128:(kc + 1) * 128, :],
                     self.sA2[:, kc * T:(kc + 1) * T])

    def emit_debug_dumps(self):
        d = self.dbg
        if not d:
            return
        self.wait("gpsimd", self.ve, self.ve.v)
        self.wait("gpsimd", self.se, self.se.v)
        srcs = dict(q=self.sQ, k=self.sK, vt=self.sVT, g=self.sG, ret=self.sRET,
                    st=self.sSf, acol=self.cACOL, bcol=self.cBCOL, resid1=self.sA2,
                    ybf=self.sY, h=self.sH, gated=self.sRET, csum=self.cSUM,
                    ctot=self.cTOT, ct1=self.cT1, ct3=self.cT3, cmur=self.cMUr,
                    crsr=self.cRSr, wf=self.sWF, wd=self.sWD, csg=self.cSG)
        for name in d:
            self.dma("gpsimd", self.dDBG, d[name], srcs[name][:, :])

    def finalize(self):
        for ctr in ([self.dO, self.dC, self.dXB, self.dXBa, self.dSL, self.dLL, self.dST,
                     self.dSTL, self.dTR, self.dDBG] + self.dWA + self.dFG +
                    self.dFU + self.dFD + self.dX1):
            self.wait("sync", ctr)


def _prep_inputs(inputs):
    perm = _perm_ffn()
    x = np.asarray(inputs["L0_attn_x"], np.float32)
    col = lambda b: np.ascontiguousarray(np.asarray(b, np.float32).reshape(16, 128).T)
    shared = dict(
        wq=np.ascontiguousarray(inputs["L0_attn_wq"], np.float32),
        wk=np.ascontiguousarray(inputs["L0_attn_wk"], np.float32),
        wv=np.ascontiguousarray(inputs["L0_attn_wv"], np.float32),
        wg=np.ascontiguousarray(inputs["L0_attn_wg"], np.float32),
        wo=np.ascontiguousarray(inputs["L0_attn_wo"], np.float32),
        wgate=np.ascontiguousarray(np.asarray(inputs["L0_ffn_w_gate"], np.float32)[:, perm]),
        wup=np.ascontiguousarray(np.asarray(inputs["L0_ffn_w_up"], np.float32)[:, perm]),
        wdown=np.ascontiguousarray(np.asarray(inputs["L0_ffn_w_down"], np.float32)[perm, :]),
        bq_c=col(inputs["L0_attn_bq"]), bk_c=col(inputs["L0_attn_bk"]),
        bg_c=col(inputs["L0_attn_bg"]), bo_c=col(inputs["L0_attn_bo"]),
        gnw_c=col(inputs["L0_attn_gn_w"]), gnb_c=col(inputs["L0_attn_gn_b"]),
        bv_c=col(inputs["L0_attn_bv"]),
        gamma_rep=np.ascontiguousarray(
            np.broadcast_to(np.asarray(inputs["L0_attn_gamma"], np.float32), (128, 16))),
    )
    in_maps = []
    for c in range(8):
        b, qt = c // 4, c % 4
        m = dict(shared)
        m["x_fm"] = np.ascontiguousarray(x[b, qt * T:(qt + 1) * T, :].T)
        mix = np.full(4, 1e9, np.float32)
        for qp in range(qt):
            mix[qp] = 1024.0 * (qt - qp - 1)
        m["mixw_rep"] = np.ascontiguousarray(np.broadcast_to(mix, (128, 4)))
        in_maps.append(m)
    return in_maps


_CACHED = {}


def kernel(**inputs):
    if "nc" not in _CACHED:
        _CACHED["nc"] = Builder().build()
    nc = _CACHED["nc"]
    in_maps = _prep_inputs(inputs)
    res = run_bass_kernel_spmd(nc, in_maps, core_ids=list(range(8)))
    out = np.empty((B, SEQ, HID), np.float32)
    for c in range(8):
        b, qt = c // 4, c % 4
        out[b, qt * T:(qt + 1) * T, :] = res.results[c]["out_fm"].T
    return out


# version 15
# speedup vs baseline: 1.0185x; 1.0096x over previous
import numpy as np
import ml_dtypes
import concourse.bass as bass
import concourse.mybir as mybir
from concourse.ap import AP
from concourse.bass_utils import run_bass_kernel_spmd

F32 = mybir.dt.float32
BF16 = mybir.dt.bfloat16
AF = mybir.ActivationFunctionType
ALU = mybir.AluOpType
BFNP = ml_dtypes.bfloat16

B, SEQ, HID, NH = 2, 4096, 2048, 16
T = 1024
NCH = 16
IFF = 8192
EPS = 1e-5
NTOK_LN = float(HID)
NTOK_GN = float(SEQ * 128)

STAGE = 99
DUMPS = ()
SUB12 = "all"
SKIP_SPARSIFY = False
RAW_H = False


def _perm_ffn():
    i = np.arange(IFF)
    g, j = i // 4, i % 4
    newpos = (4 * (g // 128) + j) * 128 + (g % 128)
    perm = np.empty(IFF, np.int64)
    perm[newpos] = i
    return perm


class Ctr:
    def __init__(self, sem):
        self.sem = sem
        self.v = 0


class Builder:
    def __init__(self):
        nc = bass.Bass("TRN2", target_bir_lowering=False)
        self.nc = nc
        self.prog = []

        def ein(name, shape):
            return nc.dram_tensor(name, shape, F32, kind="ExternalInput").ap()
        self.x_fm = ein("x_fm", [HID, T])
        self.w = {n: ein(n, [HID, HID]) for n in ("wq", "wk", "wv", "wg", "wo")}
        self.wgate = ein("wgate", [HID, IFF])
        self.wup = ein("wup", [HID, IFF])
        self.wdown = ein("wdown", [IFF, HID])
        self.bq_c = ein("bq_c", [128, 16]); self.bk_c = ein("bk_c", [128, 16])
        self.bg_c = ein("bg_c", [128, 16]); self.bo_c = ein("bo_c", [128, 16])
        self.gnw_c = ein("gnw_c", [128, 16]); self.gnb_c = ein("gnb_c", [128, 16])
        self.bv_c = ein("bv_c", [128, 16])
        self.gamma_rep = ein("gamma_rep", [128, 16])
        self.mixw_rep = ein("mixw_rep", [128, 4])
        self.out_fm = nc.dram_tensor("out_fm", [HID, T], F32, kind="ExternalOutput").ap()

        self.dbg = {}
        for name, shape in [("q", [128, NH * T]), ("k", [128, NH * 512]),
                            ("vt", [128, 4 * HID]), ("g", [128, NH * T]),
                            ("ret", [128, NH * T]), ("st", [128, HID]),
                            ("acol", [128, 16]), ("bcol", [128, 16]),
                            ("resid1", [128, NCH * T]), ("ybf", [128, NCH * T]),
                            ("h", [128, 16 * T]), ("gated", [128, NH * T]),
                            ("csum", [128, 32]), ("ctot", [32, 1]),
                            ("ct1", [16, 1]), ("ct3", [16, 1]),
                            ("cmur", [128, 16]), ("crsr", [128, 16]),
                            ("wf", [128, 4 * NCH * 128]), ("wd", [128, 2 * NCH * 128]),
                            ("csg", [128, 2 * 512])]:
            if name in DUMPS:
                self.dbg[name] = nc.dram_tensor("dbg_" + name, shape, F32,
                                                kind="ExternalOutput").ap()

        self.l_loc = nc.dram_tensor("l_loc", [128, HID], F32).ap()
        self.l_all = nc.dram_tensor("l_all", [4 * 128, HID], F32).ap()
        self.st_loc = nc.dram_tensor("st_loc", [32, 1], F32).ap()
        self.st_all = nc.dram_tensor("st_all", [32, 1], F32).ap()

        self.i_ident_b = nc.inline_tensor(np.eye(128, dtype=np.float32).astype(BFNP), "i_identb").ap()
        self.i_tri = nc.inline_tensor(np.tril(np.ones((128, 128), dtype=np.float32)).T.copy(), "i_tri").ap()
        self.i_iop1 = nc.inline_tensor(
            np.broadcast_to(np.arange(1, 129, dtype=np.float32), (128, 128)).copy(), "i_iop1").ap()
        self.i_ior = nc.inline_tensor(
            np.broadcast_to(127.0 - np.arange(128, dtype=np.float32), (128, 128)).copy(), "i_ior").ap()
        self.i_ones_b = nc.inline_tensor(np.ones((128, 128), np.float32).astype(BFNP), "i_onesb").ap()
        self.i_ones_c = nc.inline_tensor(np.ones((128, 1), np.float32), "i_onesc").ap()
        self.i_ones_r = nc.inline_tensor(np.ones((1, 128), np.float32), "i_onesr").ap()
        self.i_ones_rb = nc.inline_tensor(np.ones((1, 128), np.float32).astype(BFNP), "i_onesrb").ap()

        KB = 1024
        TOP = 229344
        total = 192 * KB
        base = TOP - total
        assert base > 27 * KB, base
        def at(name, shape, dtype, rel):
            return nc.alloc_sbuf_tensor_at(name, shape, dtype, offset=base + rel)
        self.sQ = at("sQ", [128, NH * T], BF16, 0)
        self.sDQ = at("sDQ", [128, NH * 128], BF16, 32 * KB)
        self.sDK = at("sDK", [128, NH * 128], BF16, 36 * KB)
        self.sMASK = at("sMASK", [128, NH * 128], BF16, 40 * KB)
        self.sL = at("sL", [128, HID], F32, 44 * KB)
        self.sSf = at("sSf", [128, HID], F32, 52 * KB)
        self.sSb = at("sSb", [128, HID], BF16, 60 * KB)
        self.sSin = at("sSin", [128, HID], BF16, 60 * KB)
        self.sA2 = at("sA2", [128, NCH * T], F32, 0)
        self.sRET = at("sRET", [128, NH * T], BF16, 64 * KB)
        self.sM = at("sM", [128, T], F32, 64 * KB)
        self.sR = at("sR", [128, T], F32, 68 * KB)
        self.sMR = at("sMR", [128, T], F32, 72 * KB)
        self.sSC1 = at("sSC1", [128, T], F32, 76 * KB)
        self.sSC2 = at("sSC2", [128, T], F32, 80 * KB)
        self.sCB = at("sCB", [128, 2, T], BF16, 84 * KB)
        self.sWD = at("sWD", [128, 2, NCH, 128], BF16, 88 * KB)
        self.sG = at("sG", [128, NH * T], BF16, 96 * KB)
        self.sY = at("sY", [128, NCH * T], BF16, 96 * KB)
        self.sK = at("sK", [128, NH * 512], BF16, 128 * KB)
        self.sLB = at("sLB", [128, 4, HID], F32, 128 * KB)
        self.sVT = at("sVT", [128, 4 * HID], BF16, 144 * KB)
        self.sH = at("sH", [128, 16 * T], BF16, 128 * KB)
        self.sXB = at("sXB", [128, NCH, 512], BF16, 160 * KB)
        self.sSq = at("sSq", [128, T], BF16, 160 * KB)
        self.sX1 = at("sX1", [128, 2, T], F32, 160 * KB)
        self.sSP = at("sSP", [128, 6, T], BF16, 160 * KB)
        self.sWA = at("sWA", [128, 4, NCH, 128], BF16, 176 * KB)
        self.sWF = at("sWF", [128, 4, NCH, 128], BF16, 176 * KB)

        def small(name, shape, dtype=F32):
            return nc.alloc_sbuf_tensor(name, shape, dtype)
        self.cBQ = small("cBQ", [128, 16]); self.cBK = small("cBK", [128, 16])
        self.cBV = small("cBV", [128, 16])
        self.cSV = small("cSV", [128, 2, 512], BF16)
        self.cBG = small("cBG", [128, 16]); self.cBO = small("cBO", [128, 16])
        self.cGNW = small("cGNW", [128, 16]); self.cGNB = small("cGNB", [128, 16])
        self.cGAM = small("cGAM", [128, 16]); self.cMIX = small("cMIX", [128, 4])
        self.cLNG = small("cLNG", [128, 16])
        self.cG128 = small("cG128", [128, 16])
        self.cGinv = small("cGinv", [128, 16])
        self.cG128t = small("cG128t", [128, 8, 16])
        self.cWMIX = small("cWMIX", [128, 4, 16])
        self.cONES = small("cONES", [128, 128], BF16)
        self.cONEC = small("cONEC", [128, 1])
        self.cONER = small("cONER", [1, 128])
        self.cIDb = small("cIDb", [128, 128], BF16)
        self.cIOT = small("cIOT", [128, 128])
        self.cSUM = small("cSUM", [128, 32])
        self.cTOT = small("cTOT", [32, 1])
        self.cT1 = small("cT1", [16, 1]); self.cT2 = small("cT2", [16, 1])
        self.cT3 = small("cT3", [16, 1]); self.cT4 = small("cT4", [16, 1])
        self.cEPS = small("cEPS", [128, 1])
        self.cTRa = small("cTRa", [1, 16])
        self.cTRb = small("cTRb", [1, 16])
        self.cMUr = small("cMUr", [128, 16]); self.cRSr = small("cRSr", [128, 16])
        self.cACOL = small("cACOL", [128, 16]); self.cBCOL = small("cBCOL", [128, 16])
        self.cSCB = small("cSCB", [128, 2, 512], BF16)
        self.cKT2 = small("cKT2", [128, 2, 512], BF16)
        self.cSG = small("cSG", [128, 2, 512], BF16)
        self.cONERb = small("cONERb", [1, 128], BF16)
        self.cNMR = small("cNMR", [1, T], BF16)

        self.ps = [nc.alloc_psum_tensor(f"ps{i}", [128, 512], F32) for i in range(6)]
        self.psTt = [nc.alloc_psum_tensor(f"psT{i}", [128, 1024], BF16) for i in range(2)]
        self.ps_free = {i: (None, 0) for i in range(9)}

        self.dC = Ctr(nc.alloc_semaphore("dC"))
        self.dXB = Ctr(nc.alloc_semaphore("dXB"))
        self.dXBa = Ctr(nc.alloc_semaphore("dXBa"))
        self.dWA = [Ctr(nc.alloc_semaphore(f"dWA{i}")) for i in range(4)]
        self.dFG = [Ctr(nc.alloc_semaphore(f"dFG{i}")) for i in range(2)]
        self.dFU = [Ctr(nc.alloc_semaphore(f"dFU{i}")) for i in range(2)]
        self.dFD = [Ctr(nc.alloc_semaphore(f"dFD{i}")) for i in range(2)]
        self.dX1 = [Ctr(nc.alloc_semaphore(f"dX1{i}")) for i in range(2)]
        self.dSL = Ctr(nc.alloc_semaphore("dSL"))
        self.dLL = Ctr(nc.alloc_semaphore("dLL"))
        self.dST = Ctr(nc.alloc_semaphore("dST"))
        self.dSTL = Ctr(nc.alloc_semaphore("dSTL"))
        self.dTR = Ctr(nc.alloc_semaphore("dTR"))
        self.dO = Ctr(nc.alloc_semaphore("dO"))
        self.dDBG = Ctr(nc.alloc_semaphore("dDBG"))
        self.cc = Ctr(nc.alloc_semaphore("cc"))
        self.pe = Ctr(nc.alloc_semaphore("pe"))
        self.gp = Ctr(nc.alloc_semaphore("gp"))
        self._lastw = {}
        self.ve = Ctr(nc.alloc_semaphore("ve"))
        self.se = Ctr(nc.alloc_semaphore("se"))

    def on(self, eng, fn, inc=None):
        if inc is None:
            self.prog.append((eng, fn))
            return None
        inc.v += 1
        sem = inc.sem
        def wrapped(e, fn=fn, sem=sem):
            inst = fn(e)
            inst.then_inc(sem, 1)
        self.prog.append((eng, wrapped))
        return inc.v

    def dma(self, eng, ctr, dst, src):
        ctr.v += 16
        sem = ctr.sem
        def f(e, dst=dst, src=src, sem=sem):
            e.dma_start(out=dst, in_=src).then_inc(sem, 16)
        self.prog.append((eng, f))
        return ctr.v

    def wait(self, eng, ctr, val=None):
        v = ctr.v if val is None else val
        if v <= 0:
            return
        key = (eng, ctr.sem.num)
        if self._lastw.get(key, -1) >= v:
            return
        self._lastw[key] = v
        sem = ctr.sem
        def f(e, sem=sem, v=v):
            e.wait_ge(sem, v)
        self.prog.append((eng, f))

    def pe_wait_free(self, tile):
        ctr, v = self.ps_free[tile]
        if ctr is not None and v > 0:
            self.wait("tensor", ctr, v)

    def mark_ps_free(self, tile, ctr):
        self.ps_free[tile] = (ctr, ctr.v)

    def build(self):
        nc = self.nc
        self.phase0_consts_tables()
        if STAGE >= 1:
            self.phase12_proj_retention()
        if STAGE >= 3 and SUB12 == "all":
            self.phase3_states_correction()
            if STAGE >= 4:
                self.phase35_stats()
                self.phase4_norm_gate()
            if STAGE >= 5:
                self.phase5_wo()
            if STAGE >= 6:
                self.phase6_ln1()
            if STAGE >= 7:
                self.phase7_ffn()
            if STAGE >= 8:
                self.phase8_ln2_out()
        self.emit_debug_dumps()
        self.finalize()
        with nc.Block() as block:
            items = self.prog
            def mk(name):
                def runner(e):
                    for en, fn in items:
                        if en == name:
                            fn(e)
                return runner
            block.sync(mk("sync"))
            block.tensor(mk("tensor"))
            block.vector(mk("vector"))
            block.scalar(mk("scalar"))
            block.gpsimd(mk("gpsimd"))
        return nc

    def phase0_consts_tables(self):
        dC, ve, se = self.dC, self.ve, self.se
        for dst, src in [(self.cBQ[:, :], self.bq_c), (self.cBK[:, :], self.bk_c),
                         (self.cBG[:, :], self.bg_c), (self.cBO[:, :], self.bo_c),
                         (self.cGNW[:, :], self.gnw_c), (self.cGNB[:, :], self.gnb_c),
                         (self.cGAM[:, :], self.gamma_rep), (self.cMIX[:, :], self.mixw_rep),
                         (self.cONES[:, :], self.i_ones_b), (self.cONEC[:, :], self.i_ones_c),
                         (self.cONER[:, :], self.i_ones_r),
                         (self.cIDb[:, :], self.i_ident_b), (self.cBV[:, :], self.bv_c),
                         (self.cONERb[:, :], self.i_ones_rb)]:
            self.dma("sync", dC, dst, src)
        self.dma("sync", dC, self.cIOT[:, :], self.i_iop1)

        self.wait("vector", dC, dC.v)
        def f1(e):
            return e.tensor_scalar_max(self.cGAM[:, :], self.cGAM[:, :], 1e-30)
        v_clamp = self.on("vector", f1, inc=ve)

        self.wait("scalar", ve, v_clamp)
        self.wait("scalar", dC, dC.v)
        def f2a(e):
            return e.activation(out=self.cLNG[:, :], in_=self.cGAM[:, :], func=AF.Ln)
        v_ln = self.on("scalar", f2a, inc=se)
        self.wait("scalar", se, v_ln)
        def f2(e):
            e.activation(out=self.cG128[:, :], in_=self.cLNG[:, :], func=AF.Exp, scale=128.0)
            e.activation(out=self.cGinv[:, :], in_=self.cLNG[:, :], func=AF.Exp, scale=-128.0)
            for t in range(8):
                e.activation(out=self.cG128t[:, t, :], in_=self.cLNG[:, :], func=AF.Exp,
                             scale=128.0 * t)
            last = None
            for h in range(NH):
                last = e.activation(out=self.sDQ[:, h * 128:(h + 1) * 128],
                                    in_=self.cIOT[:, :], func=AF.Exp,
                                    scale=self.cLNG[:, h:h + 1])
            return last
        v_dq = self.on("scalar", f2, inc=se)
        self.wait("sync", se, v_dq)
        self.dma("sync", dC, self.cIOT[:, :], self.i_ior)
        self.wait("scalar", dC, dC.v)
        def f3(e):
            last = None
            for h in range(NH):
                last = e.activation(out=self.sDK[:, h * 128:(h + 1) * 128],
                                    in_=self.cIOT[:, :], func=AF.Exp,
                                    scale=self.cLNG[:, h:h + 1])
            return last
        v_dk = self.on("scalar", f3, inc=se)
        self.wait("sync", se, v_dk)
        self.dma("sync", dC, self.cIOT[:, :], self.i_tri)

        self.wait("vector", se, v_dk)
        self.wait("vector", dC, dC.v)
        def f4(e):
            for h in range(NH):
                e.tensor_scalar(self.sMASK[:, h * 128:(h + 1) * 128], self.cIOT[:, :],
                                self.cGinv[:, h:h + 1], None, ALU.mult)
            for qt in range(4):
                e.tensor_scalar(self.cWMIX[:, qt, :], self.cLNG[:, :],
                                self.cMIX[:, qt:qt + 1], None, ALU.mult)
            e.memset(self.sSf[:, :], 0.0)
            e.memset(self.cEPS[:, :], EPS)
            return e.memset(self.sSb[:, :], 0.0)
        v_mask = self.on("vector", f4, inc=ve)
        self.wait("scalar", ve, v_mask)
        def f5(e):
            return e.activation(out=self.cWMIX[:, :, :], in_=self.cWMIX[:, :, :], func=AF.Exp)
        self.c_tables_se = self.on("scalar", f5, inc=se)
        self.c_tables_ve = v_mask

    def _att_stream_order(self):
        mats = ["wq", "wk", "wv", "wg"]
        order = []
        for half in range(2):
            for i in range(64):
                order.append((mats[i // 16], i % 16))
        for b in range(16):
            order.append(("wo", b))
        return order

    def _issue_att_load(self):
        st = self.att_st
        i = st["next"]
        mat, b = self.att_order[i]
        if i >= 4:
            gate = st["consumed"].get(i - 4)
            if gate:
                self.wait("gpsimd", self.pe, gate)
        src = self.w[mat][:, b * 128:(b + 1) * 128].rearrange("(k p) c -> p k c", p=128)
        ctr = self.dWA[i % 4]
        v = self.dma("gpsimd", ctr, self.sWA[:, i % 4], src)
        st["vals"].append((ctr, v))
        st["next"] += 1

    def _att_block_val(self, idx):
        while self.att_st["next"] <= idx:
            self._issue_att_load()
        return self.att_st["vals"][idx]

    def _att_consumed(self, idx):
        self.att_st["consumed"][idx] = self.pe.v
        if self.att_st["next"] < len(self.att_order):
            self._issue_att_load()

    def phase12_proj_retention(self):
        pe, ve, se = self.pe, self.ve, self.se
        self.att_order = self._att_stream_order()
        self.att_st = {"next": 0, "vals": [], "consumed": {}}
        self.sb_ready = {hg: 0 for hg in range(4)}
        self.scb_busy = {0: 0, 1: 0}
        self.kt_busy = {0: 0, 1: 0}

        for half in range(2):
            if half == 1:
                self.wait("gpsimd", pe, self.v_xb_users)
            v_xb_lo = self.dma(
                "gpsimd", self.dXBa, self.sXB[:, 0:8, :],
                self.x_fm[0:1024, half * 512:(half + 1) * 512].rearrange("(k p) t -> p k t", p=128))
            v_xb = self.dma(
                "gpsimd", self.dXB, self.sXB[:, 8:16, :],
                self.x_fm[1024:2048, half * 512:(half + 1) * 512].rearrange("(k p) t -> p k t", p=128))
            base_blk = half * 64
            if half == 0:
                self.wait("vector", se, self.c_tables_se)

            for wi, wname in enumerate(["wq", "wk"]):
                if SUB12 == "q" and wi == 1:
                    return
                for m in range(NCH):
                    if isinstance(SUB12, str) and SUB12.startswith("q") and len(SUB12) > 1 \
                            and (wi > 0 or m >= int(SUB12[1:])):
                        return
                    blk = base_blk + wi * 16 + m
                    bctr, bval = self._att_block_val(blk)
                    self.wait("tensor", bctr, bval)
                    tile = m % 4
                    self.pe_wait_free(tile)
                    par = blk % 4
                    if wi == 0 and m < 2:
                        self.wait("tensor", self.dXBa, v_xb_lo)
                        def fl(e, par=par, tile=tile):
                            last = None
                            for kc in range(8):
                                last = e.matmul(self.ps[tile][:, 0:512],
                                                self.sWA[:, par, kc, :],
                                                self.sXB[:, kc, :],
                                                start=(kc == 0), stop=False,
                                                skip_group_check=True)
                            return last
                        self.on("tensor", fl, inc=pe)
                        self.wait("tensor", self.dXB, v_xb)
                        def fh(e, par=par, tile=tile):
                            last = None
                            for kc in range(8, NCH):
                                last = e.matmul(self.ps[tile][:, 0:512],
                                                self.sWA[:, par, kc, :],
                                                self.sXB[:, kc, :],
                                                start=False, stop=(kc == NCH - 1),
                                                skip_group_check=True)
                            return last
                        v_mm = self.on("tensor", fh, inc=pe)
                    else:
                        self.wait("tensor", self.dXB, v_xb)
                        def f(e, par=par, tile=tile):
                            last = None
                            for kc in range(NCH):
                                last = e.matmul(self.ps[tile][:, 0:512],
                                                self.sWA[:, par, kc, :],
                                                self.sXB[:, kc, :],
                                                start=(kc == 0), stop=(kc == NCH - 1),
                                                skip_group_check=True)
                            return last
                        v_mm = self.on("tensor", f, inc=pe)
                    self._att_consumed(blk)
                    self.wait("vector", pe, v_mm)
                    if wname == "wq":
                        dst = self.sQ[:, m * T + half * 512: m * T + half * 512 + 512]
                        dec, bias = self.sDQ, self.cBQ
                    else:
                        dst = self.sK[:, m * 512:(m + 1) * 512]
                        dec, bias = self.sDK, self.cBK
                    def fe(e, m=m, tile=tile, dst=dst, dec=dec, bias=bias):
                        din = AP(dec, m * 128, [[NH * 128, 128], [0, 4], [1, 128]])
                        return e.scalar_tensor_tensor(
                            out=dst.rearrange("p (r c) -> p r c", r=4),
                            in0=self.ps[tile][:, 0:512].rearrange("p (r c) -> p r c", r=4),
                            scalar=bias[:, m:m + 1], in1=din,
                            op0=ALU.add, op1=ALU.mult)
                    self.on("vector", fe, inc=ve)
                    self.mark_ps_free(tile, ve)

            if SUB12 == "k":
                return
            pend_tr = [None]
            def emit_vt_transposes():
                m, par, v_csv = pend_tr[0]
                tT = m % 2
                self.pe_wait_free(7 + tT)
                self.wait("tensor", ve, v_csv)
                def f_tr(e, par=par, tT=tT):
                    last = None
                    for ts in range(4):
                        last = e.transpose(
                            self.psTt[tT][:, ts * 128: ts * 128 + 128],
                            self.cSV[:, par, ts * 128: ts * 128 + 128],
                            self.cIDb[:, :])
                    return last
                v_tr = self.on("tensor", f_tr, inc=pe)
                self.csv_busy[par] = v_tr
                self.wait("vector", pe, v_tr)
                def f_tre(e, m=m, tT=tT):
                    out = AP(self.sVT, m * 128, [[4 * HID, 128], [HID, 4], [1, 128]])
                    return e.tensor_copy(
                        out, self.psTt[tT][:, 0:512].rearrange("p (r c) -> p r c", r=4))
                self.on("vector", f_tre, inc=ve)
                self.mark_ps_free(7 + tT, ve)
                pend_tr[0] = None
            self.csv_busy = {0: 0, 1: 0}
            for m in range(NCH):
                blk = base_blk + 32 + m
                bctr, bval = self._att_block_val(blk)
                self.wait("tensor", bctr, bval)
                self.wait("tensor", self.dXB, v_xb)
                tile = 2 + m % 4 if False else (2 + m % 4)
                tile = 2 + (m % 4)
                self.pe_wait_free(tile)
                par = blk % 4
                def f(e, par=par, tile=tile):
                    last = None
                    for kc in range(NCH):
                        last = e.matmul(self.ps[tile][:, 0:512],
                                        self.sWA[:, par, kc, :],
                                        self.sXB[:, kc, :],
                                        start=(kc == 0), stop=(kc == NCH - 1),
                                        skip_group_check=True)
                    return last
                v_mm = self.on("tensor", f, inc=pe)
                self._att_consumed(blk)
                cpar = m % 2
                self.wait("vector", pe, max(v_mm, self.csv_busy[cpar]))
                def fe(e, m=m, tile=tile, cpar=cpar):
                    return e.tensor_scalar(self.cSV[:, cpar, :], self.ps[tile][:, 0:512],
                                           self.cBV[:, m:m + 1], None, ALU.add)
                v_csv = self.on("vector", fe, inc=ve)
                self.mark_ps_free(tile, ve)
                if pend_tr[0] is not None:
                    emit_vt_transposes()
                pend_tr[0] = (m, cpar, v_csv)
            emit_vt_transposes()

            if SUB12 == "vt":
                return
            for m in ([] if half == 1 and SUB12 == "all" and STAGE >= 3 else range(NCH)):
                blk = base_blk + 48 + m
                bctr, bval = self._att_block_val(blk)
                self.wait("tensor", bctr, bval)
                self.wait("tensor", self.dXB, v_xb)
                tile = m % 4
                self.pe_wait_free(tile)
                par = blk % 4
                def f(e, par=par, tile=tile):
                    last = None
                    for kc in range(NCH):
                        last = e.matmul(self.ps[tile][:, 0:512],
                                        self.sWA[:, par, kc, :],
                                        self.sXB[:, kc, :],
                                        start=(kc == 0), stop=(kc == NCH - 1),
                                        skip_group_check=True)
                    return last
                v_mm = self.on("tensor", f, inc=pe)
                self._att_consumed(blk)
                self.wait("scalar", pe, v_mm)
                def fe(e, m=m, tile=tile, half=half):
                    return e.activation(
                        out=self.sG[:, m * T + half * 512: m * T + half * 512 + 512],
                        in_=self.ps[tile][:, 0:512], func=AF.Silu,
                        bias=self.cBG[:, m:m + 1])
                self.on("scalar", fe, inc=se)
                self.mark_ps_free(tile, se)

            self.v_xb_users = pe.v

            if SUB12 == "g" or STAGE < 2:
                self.v_ret_done_ve = ve.v
                return

            if half == 0:
                self.wait("tensor", se, self.c_tables_se)
                self.wait("tensor", ve, self.c_tables_ve)

            def stage_a(tl, hg):
                gt = half * 4 + tl
                tS = hg % 2
                tT = hg % 2
                self.pe_wait_free(tS)
                self.pe_wait_free(7 + tT)
                def f_a(e, tl=tl, gt=gt, hg=hg, tS=tS, tT=tT):
                    for hl in range(4):
                        h = hg * 4 + hl
                        e.matmul(
                            self.ps[tS][:, hl * 128: hl * 128 + 128],
                            self.sK[:, h * 512 + tl * 128: h * 512 + tl * 128 + 128],
                            self.sQ[:, h * T + gt * 128: h * T + gt * 128 + 128],
                            start=True, stop=True, skip_group_check=True)
                    last = None
                    for hl in range(4):
                        h = hg * 4 + hl
                        last = e.transpose(
                            self.psTt[tT][:, hl * 128: hl * 128 + 128],
                            self.sK[:, h * 512 + tl * 128: h * 512 + tl * 128 + 128],
                            self.cIDb[:, :])
                    return last
                v_a = self.on("tensor", f_a, inc=pe)
                self.wait("vector", pe, max(v_a, self.scb_busy[hg % 2], self.kt_busy[tT]))
                def f_me(e, hg=hg, tS=tS):
                    return e.tensor_tensor(self.cSCB[:, hg % 2, :],
                                           self.ps[tS][:, 0:512],
                                           self.sMASK[:, hg * 512: hg * 512 + 512],
                                           ALU.mult)
                v_me = self.on("vector", f_me, inc=ve)
                self.mark_ps_free(tS, ve)
                def f_kte(e, tT=tT):
                    return e.tensor_copy(self.cKT2[:, tT, :],
                                         self.psTt[tT][:, 0:512])
                v_kte = self.on("vector", f_kte, inc=ve)
                self.mark_ps_free(7 + tT, ve)
                return (tl, hg, v_me, v_kte)

            def stage_b(rec):
                tl, hg, v_me, v_kte = rec
                gt = half * 4 + tl
                tR = 2 + hg % 2
                tSt = 4 + hg % 2
                tT = hg % 2
                self.pe_wait_free(tR)
                self.pe_wait_free(tSt)
                self.wait("tensor", ve, v_kte)
                if self.sb_ready[hg]:
                    self.wait("tensor", ve, self.sb_ready[hg])
                def f_b(e, tl=tl, gt=gt, hg=hg, tR=tR, tSt=tSt, tT=tT):
                    for hl in range(4):
                        h = hg * 4 + hl
                        e.matmul(self.ps[tR][:, hl * 128: hl * 128 + 128],
                                 self.sVT[:, tl * HID + h * 128: tl * HID + h * 128 + 128],
                                 self.cSCB[:, hg % 2, hl * 128: hl * 128 + 128],
                                 start=True, stop=False, skip_group_check=True)
                        e.matmul(self.ps[tR][:, hl * 128: hl * 128 + 128],
                                 self.sSb[:, h * 128: h * 128 + 128],
                                 self.sQ[:, h * T + gt * 128: h * T + gt * 128 + 128],
                                 start=False, stop=True, skip_group_check=True)
                    last = None
                    for hl in range(4):
                        h = hg * 4 + hl
                        last = e.matmul(
                            self.ps[tSt][:, hl * 128: hl * 128 + 128],
                            self.cKT2[:, tT, hl * 128: hl * 128 + 128],
                            self.sVT[:, tl * HID + h * 128: tl * HID + h * 128 + 128],
                            start=True, stop=True, skip_group_check=True)
                    return last
                v_b = self.on("tensor", f_b, inc=pe)
                self.scb_busy[hg % 2] = v_b
                self.kt_busy[tT] = v_b
                self.wait("vector", pe, v_b)
                def f_re(e, gt=gt, hg=hg, tR=tR):
                    out = AP(self.sRET, (hg * 4) * T + gt * 128,
                             [[NH * T, 128], [T, 4], [1, 128]])
                    return e.tensor_copy(
                        out, self.ps[tR][:, 0:512].rearrange("p (r c) -> p r c", r=4))
                self.on("vector", f_re, inc=ve)
                self.mark_ps_free(tR, ve)
                def f_su(e, hg=hg, tSt=tSt):
                    last = None
                    for hl in range(4):
                        h = hg * 4 + hl
                        last = e.scalar_tensor_tensor(
                            out=self.sSf[:, h * 128: h * 128 + 128],
                            in0=self.sSf[:, h * 128: h * 128 + 128],
                            scalar=self.cG128[:, h:h + 1],
                            in1=self.ps[tSt][:, hl * 128: hl * 128 + 128],
                            op0=ALU.mult, op1=ALU.add)
                    return last
                v_sup = self.on("vector", f_su, inc=ve)
                self.mark_ps_free(tSt, ve)
                self.wait("vector", ve, v_sup)
                def f_sub(e, hg=hg):
                    return e.tensor_copy(self.sSb[:, hg * 512: hg * 512 + 512],
                                         self.sSf[:, hg * 512: hg * 512 + 512])
                self.sb_ready[hg] = self.on("vector", f_sub, inc=ve)

            items = [(tl, hg) for tl in range(4) for hg in range(4)]
            prev = None
            for it in items:
                rec = stage_a(*it)
                if prev is not None:
                    stage_b(prev)
                prev = rec
            stage_b(prev)
        self.v_ret_done_ve = self.ve.v

    def _emit_g_half1(self):
        pe, se = self.pe, self.se
        for m in range(NCH):
            blk = 64 + 48 + m
            bctr, bval = self._att_block_val(blk)
            self.wait("tensor", bctr, bval)
            tile = m % 4
            self.pe_wait_free(tile)
            par = blk % 4
            def f(e, par=par, tile=tile):
                last = None
                for kc in range(NCH):
                    last = e.matmul(self.ps[tile][:, 0:512],
                                    self.sWA[:, par, kc, :],
                                    self.sXB[:, kc, :],
                                    start=(kc == 0), stop=(kc == NCH - 1),
                                    skip_group_check=True)
                return last
            v_mm = self.on("tensor", f, inc=pe)
            self._att_consumed(blk)
            self.wait("scalar", pe, v_mm)
            def fe(e, m=m, tile=tile):
                return e.activation(
                    out=self.sG[:, m * T + 512: m * T + 1024],
                    in_=self.ps[tile][:, 0:512], func=AF.Silu,
                    bias=self.cBG[:, m:m + 1])
            self.on("scalar", fe, inc=se)
            self.mark_ps_free(tile, se)

    def phase3_states_correction(self):
        pe, ve, cc, se = self.pe, self.ve, self.cc, self.se
        self.wait("sync", ve, self.v_ret_done_ve)
        v_l = self.dma("sync", self.dLL, self.l_loc, self.sSf[:, :])
        self.wait("gpsimd", self.dLL, v_l)
        cc.v += 1
        def f_cc(e):
            e.collective_compute(
                "AllGather", ALU.bypass,
                replica_groups=[[0, 1, 2, 3], [4, 5, 6, 7]],
                ins=[self.l_loc], outs=[self.l_all],
            ).then_inc(self.cc.sem, 1)
        self.prog.append(("gpsimd", f_cc))
        v_ccag = cc.v
        self._emit_g_half1()
        self.wait("sync", cc, v_ccag)
        v_lb = self.dma("sync", self.dSL, self.sLB[:, :, :],
                        self.l_all.rearrange("(q p) c -> p q c", p=128))
        self.wait("vector", self.dSL, v_lb)
        v_acc = 0
        for qt in range(4):
            if v_acc:
                self.wait("vector", ve, v_acc)
            def f_acc(e, qt=qt):
                last = None
                for h in range(NH):
                    if qt == 0:
                        last = e.tensor_scalar(
                            self.sSin[:, h * 128: h * 128 + 128],
                            self.sLB[:, qt, h * 128: h * 128 + 128],
                            self.cWMIX[:, qt, h:h + 1], None, ALU.mult)
                    else:
                        last = e.scalar_tensor_tensor(
                            out=self.sSin[:, h * 128: h * 128 + 128],
                            in0=self.sLB[:, qt, h * 128: h * 128 + 128],
                            scalar=self.cWMIX[:, qt, h:h + 1],
                            in1=self.sSin[:, h * 128: h * 128 + 128],
                            op0=ALU.mult, op1=ALU.add)
                return last
            v_acc = self.on("vector", f_acc, inc=ve)
        v_sin = v_acc

        self.wait("tensor", ve, v_sin)
        stats_pend = []
        def drain_stat():
            fn, vdep = stats_pend.pop(0)
            self.wait("vector", ve, vdep)
            self.on("vector", fn, inc=ve)
        for hg in range(4):
            for gt in range(8):
                if stats_pend and gt % 2 == 0:
                    drain_stat()
                tile = gt % 2
                self.pe_wait_free(tile)
                def f_mm(e, gt=gt, hg=hg, tile=tile):
                    last = None
                    for hl in range(4):
                        h = hg * 4 + hl
                        last = e.matmul(
                            self.ps[tile][:, hl * 128: hl * 128 + 128],
                            self.sSin[:, h * 128: h * 128 + 128],
                            self.sQ[:, h * T + gt * 128: h * T + gt * 128 + 128],
                            start=True, stop=True, skip_group_check=True)
                    return last
                v_mm = self.on("tensor", f_mm, inc=pe)
                self.wait("vector", pe, v_mm)
                def f_ev(e, gt=gt, hg=hg, tile=tile):
                    last = None
                    for hl in range(4):
                        h = hg * 4 + hl
                        dst = self.sRET[:, h * T + gt * 128: h * T + gt * 128 + 128]
                        last = e.scalar_tensor_tensor(
                            out=dst,
                            in0=self.ps[tile][:, hl * 128: hl * 128 + 128],
                            scalar=self.cG128t[:, gt, h:h + 1],
                            in1=dst, op0=ALU.mult, op1=ALU.add)
                    return last
                self.on("vector", f_ev, inc=ve)
                self.mark_ps_free(tile, ve)
            v_done = self.ve.v
            self.wait("scalar", ve, v_done)
            for hl in range(4):
                self.wait("scalar", se, self.se.v)
                def f_q(e, h=hg * 4 + hl):
                    return e.activation(out=self.sSq[:, :],
                                        in_=self.sRET[:, h * T:(h + 1) * T],
                                        func=AF.Square,
                                        accum_out=self.cSUM[:, 16 + h:17 + h])
                self.on("scalar", f_q, inc=se)
            for hl in range(4):
                def f_s(e, h=hg * 4 + hl):
                    return e.reduce_sum(self.cSUM[:, h:h + 1],
                                        self.sRET[:, h * T:(h + 1) * T],
                                        mybir.AxisListType.X)
                stats_pend.append((f_s, v_done))
            if hg == 3:
                while stats_pend:
                    drain_stat()
        self.v_ret_final_ve = ve.v

    def phase35_stats(self):
        pe, ve, se, cc = self.pe, self.ve, self.se, self.cc
        v_sums = self.ve.v
        v_sq = self.se.v
        self.wait("tensor", ve, v_sums)
        self.wait("tensor", se, v_sq)
        self.pe_wait_free(5)
        def f_rr(e):
            return e.matmul(self.ps[5][0:32, 0:1], self.cSUM[:, :], self.cONEC[:, :],
                            start=True, stop=True, skip_group_check=True)
        v_rr = self.on("tensor", f_rr, inc=pe)
        self.wait("vector", pe, v_rr)
        def f_ev(e):
            return e.tensor_copy(self.cTOT[:, :], self.ps[5][0:32, 0:1])
        v_tot = self.on("vector", f_ev, inc=ve)
        self.mark_ps_free(5, ve)
        self.wait("sync", ve, v_tot)
        v_st = self.dma("sync", self.dST, self.st_loc, self.cTOT[:, :])
        self.wait("gpsimd", self.dST, v_st)
        cc.v += 1
        def f_cc(e):
            e.collective_compute(
                "AllReduce", ALU.add,
                replica_groups=[[0, 1, 2, 3], [4, 5, 6, 7]],
                ins=[self.st_loc], outs=[self.st_all],
            ).then_inc(cc.sem, 1)
        self.prog.append(("gpsimd", f_cc))
        self.wait("sync", cc, cc.v)
        self.dma("sync", self.dTR, self.cTRa[0:1, :], self.st_all[0:16, 0:1])
        v_t2 = self.dma("sync", self.dTR, self.cTRb[0:1, :], self.st_all[16:32, 0:1])
        self.wait("tensor", self.dTR, v_t2)
        self.pe_wait_free(5)
        def f_rep(e):
            e.matmul(self.ps[5][:, 0:16], self.cONER[0:1, :], self.cTRa[0:1, :],
                     start=True, stop=True, skip_group_check=True)
            return e.matmul(self.ps[5][:, 16:32], self.cONER[0:1, :], self.cTRb[0:1, :],
                            start=True, stop=True, skip_group_check=True)
        v_rep = self.on("tensor", f_rep, inc=pe)
        self.wait("vector", pe, v_rep)
        def f_c1(e):
            e.tensor_scalar_mul(self.cMUr[:, :], self.ps[5][:, 0:16], 1.0 / NTOK_GN)
            return e.tensor_scalar_mul(self.cRSr[:, :], self.ps[5][:, 16:32], 1.0 / NTOK_GN)
        v_c1 = self.on("vector", f_c1, inc=ve)
        self.mark_ps_free(5, ve)
        self.wait("vector", ve, v_c1)
        def f_c2a(e):
            return e.tensor_mul(self.cACOL[:, :], self.cMUr[:, :], self.cMUr[:, :])
        v_c2a = self.on("vector", f_c2a, inc=ve)
        self.wait("vector", ve, v_c2a)
        def f_c2(e):
            return e.tensor_sub(self.cRSr[:, :], self.cRSr[:, :], self.cACOL[:, :])
        v_c2 = self.on("vector", f_c2, inc=ve)
        self.wait("scalar", ve, v_c2)
        def f_sd(e):
            return e.activation(out=self.cRSr[:, :], in_=self.cRSr[:, :], func=AF.Sqrt,
                                bias=self.cEPS[:, 0:1])
        v_sd = self.on("scalar", f_sd, inc=se)
        self.wait("vector", se, v_sd)
        def f_rs(e):
            return e.reciprocal(self.cRSr[:, :], self.cRSr[:, :])
        v_rs = self.on("vector", f_rs, inc=ve)
        self.wait("vector", ve, v_rs)
        def f_ab1(e):
            return e.tensor_mul(self.cACOL[:, :], self.cRSr[:, :], self.cGNW[:, :])
        v_a1 = self.on("vector", f_ab1, inc=ve)
        self.wait("vector", ve, v_a1)
        def f_ab2(e):
            return e.tensor_mul(self.cMUr[:, :], self.cMUr[:, :], self.cACOL[:, :])
        v_a2 = self.on("vector", f_ab2, inc=ve)
        self.wait("vector", ve, v_a2)
        def f_ab3(e):
            return e.tensor_sub(self.cBCOL[:, :], self.cGNB[:, :], self.cMUr[:, :])
        self.v_ab = self.on("vector", f_ab3, inc=ve)
        self.mark_ps_free(5, ve)

    def phase4_norm_gate(self):
        ve, se, gp = self.ve, self.se, self.gp
        self.wait("vector", ve, self.v_ab)
        self.wait("vector", se, self.se.v)
        for h in range(NH):
            par = 4 + h % 2
            def f0(e, h=h, par=par):
                return e.tensor_scalar(self.sSP[:, par, :], self.sRET[:, h * T:(h + 1) * T],
                                       self.cACOL[:, h:h + 1], self.cBCOL[:, h:h + 1],
                                       ALU.mult, ALU.add)
            v0 = self.on("vector", f0, inc=ve)
            self.wait("vector", ve, v0)
            def f1(e, h=h, par=par):
                return e.tensor_tensor(self.sRET[:, h * T:(h + 1) * T],
                                       self.sSP[:, par, :],
                                       self.sG[:, h * T:(h + 1) * T], ALU.mult)
            self.on("vector", f1, inc=ve)
        self.v_gated = self.ve.v
        self.v_gated_gp = 0

    def phase5_wo(self):
        pe, ve = self.pe, self.ve
        self.wait("tensor", ve, self.v_gated)
        self.wait("tensor", self.gp, self.v_gated_gp)
        x1_free = {0: 0, 1: 0}
        for m in range(NCH):
            blk = 128 + m
            bctr, bval = self._att_block_val(blk)
            par_x = m % 2
            if x1_free[par_x]:
                self.wait("sync", ve, x1_free[par_x])
            v_x = self.dma("sync", self.dX1[par_x], self.sX1[:, par_x, :],
                           self.x_fm[m * 128:(m + 1) * 128, :])
            self.wait("tensor", bctr, bval)
            for th in range(2):
                tile = (2 * m + th) % 4
                self.pe_wait_free(tile)
                parw = blk % 4
                def f(e, parw=parw, tile=tile, th=th):
                    last = None
                    for kc in range(NCH):
                        last = e.matmul(
                            self.ps[tile][:, 0:512],
                            self.sWA[:, parw, kc, :],
                            self.sRET[:, kc * T + th * 512: kc * T + th * 512 + 512],
                            start=(kc == 0), stop=(kc == NCH - 1),
                            skip_group_check=True)
                    return last
                v_mm = self.on("tensor", f, inc=pe)
                self.wait("vector", pe, v_mm)
                self.wait("vector", self.dX1[par_x], v_x)
                def fe(e, m=m, th=th, tile=tile, par_x=par_x):
                    return e.scalar_tensor_tensor(
                        out=self.sA2[:, m * T + th * 512: m * T + th * 512 + 512],
                        in0=self.ps[tile][:, 0:512],
                        scalar=self.cBO[:, m:m + 1],
                        in1=self.sX1[:, par_x, th * 512: th * 512 + 512],
                        op0=ALU.add, op1=ALU.add)
                self.on("vector", fe, inc=ve)
                self.mark_ps_free(tile, ve)
            x1_free[par_x] = ve.v
            self._att_consumed(blk)
        self.v_resid1 = ve.v
        self.v_wo_pe = pe.v

    def _ln_stats(self):
        pe, ve, se = self.pe, self.ve, self.se
        for t in range(4):
            self.pe_wait_free(t)
        cb_busy = {0: 0, 1: 0}
        for kc in range(NCH):
            pcb = kc % 2
            if cb_busy[pcb]:
                self.wait("vector", pe, cb_busy[pcb])
                self.wait("scalar", pe, cb_busy[pcb])
            def f_c(e, kc=kc, pcb=pcb):
                return e.tensor_copy(self.sCB[:, pcb, :], self.sA2[:, kc * T:(kc + 1) * T])
            v_c = self.on("vector", f_c, inc=ve)
            def f_q(e, kc=kc, pcb=pcb):
                return e.activation(out=self.sSP[:, 4 + pcb, :],
                                    in_=self.sA2[:, kc * T:(kc + 1) * T],
                                    func=AF.Square)
            v_q = self.on("scalar", f_q, inc=se)
            self.wait("tensor", ve, v_c)
            self.wait("tensor", se, v_q)
            def f_mm(e, kc=kc, pcb=pcb):
                last = None
                for th in range(2):
                    e.matmul(self.ps[0 + th][:, 0:512], self.cONES[:, :],
                             self.sCB[:, pcb, th * 512: th * 512 + 512],
                             start=(kc == 0), stop=(kc == NCH - 1),
                             skip_group_check=True)
                    last = e.matmul(self.ps[2 + th][:, 0:512], self.cONES[:, :],
                                    self.sSP[:, 4 + pcb, th * 512: th * 512 + 512],
                                    start=(kc == 0), stop=(kc == NCH - 1),
                                    skip_group_check=True)
                return last
            cb_busy[pcb] = self.on("tensor", f_mm, inc=pe)
        v_stats_pe = self.pe.v
        self.wait("vector", pe, v_stats_pe)
        def f_mr0(e):
            last = None
            for th in range(2):
                sl = slice(th * 512, th * 512 + 512)
                e.tensor_scalar_mul(self.sM[:, sl], self.ps[th][:, 0:512], 1.0 / NTOK_LN)
                last = e.tensor_scalar_mul(self.sSC1[:, sl], self.ps[2 + th][:, 0:512],
                                           1.0 / NTOK_LN)
            return last
        v_mr0 = self.on("vector", f_mr0, inc=ve)
        self.wait("vector", ve, v_mr0)
        def f_mr1(e):
            return e.tensor_mul(self.sSC2[:, :], self.sM[:, :], self.sM[:, :])
        v_mr1 = self.on("vector", f_mr1, inc=ve)
        self.wait("vector", ve, v_mr1)
        def f_mr2(e):
            return e.tensor_sub(self.sSC1[:, :], self.sSC1[:, :], self.sSC2[:, :])
        v_var = self.on("vector", f_mr2, inc=ve)
        for t in range(4):
            self.mark_ps_free(t, ve)
        self.wait("scalar", ve, v_var)
        def f_sd(e):
            return e.activation(out=self.sSC2[:, :], in_=self.sSC1[:, :], func=AF.Sqrt,
                                bias=self.cEPS[:, 0:1])
        v_sd = self.on("scalar", f_sd, inc=se)
        self.wait("vector", se, v_sd)
        def f_r(e):
            return e.reciprocal(self.sR[:, :], self.sSC2[:, :])
        return self.on("vector", f_r, inc=ve)

    def phase6_ln1(self):
        pe, ve = self.pe, self.ve
        self.wait("vector", pe, self.v_wo_pe)
        self.wait("vector", ve, self.v_resid1)
        self.wait("scalar", pe, self.v_wo_pe)
        self.wait("scalar", ve, self.v_resid1)
        self.wait("tensor", ve, self.v_resid1)
        v_lnr = self._ln_stats()
        self.wait("vector", ve, v_lnr)
        def f_mr(e):
            return e.tensor_mul(self.sMR[:, :], self.sM[:, :], self.sR[:, :])
        v_mrp = self.on("vector", f_mr, inc=ve)
        self.wait("vector", ve, v_mrp)
        def f_nmr(e):
            return e.tensor_scalar_mul(self.cNMR[0:1, :], self.sMR[0:1, :], -1.0)
        self.v_nmr = self.on("vector", f_nmr, inc=ve)
        self.v_ychunk = {}
        for kc in range(NCH):
            scr = self.sSC1 if kc % 2 == 0 else self.sSC2
            def f0(e, kc=kc, scr=scr):
                return e.tensor_sub(scr[:, :], self.sA2[:, kc * T:(kc + 1) * T], self.sM[:, :])
            v0 = self.on("vector", f0, inc=ve)
            self.wait("vector", ve, v0)
            def f1(e, kc=kc, scr=scr):
                return e.tensor_tensor(self.sY[:, kc * T:(kc + 1) * T], scr[:, :],
                                       self.sR[:, :], ALU.mult)
            self.v_ychunk[kc] = self.on("vector", f1, inc=ve)
        self.v_ybf = self.ve.v

    def _ffn_issue(self, kind, j):
        st = self.ffn_st[kind]
        while st["next"] <= j:
            i = st["next"]
            if i >= 2:
                gate = st["consumed"].get(i - 2)
                if gate:
                    self.wait("gpsimd", self.pe, gate)
            if kind == "g":
                srcap = self.wgate[:, i * 128:(i + 1) * 128]
                ctr, dst = self.dFG[i % 2], self.sWF[:, i % 2]
            elif kind == "u":
                srcap = self.wup[:, i * 128:(i + 1) * 128]
                ctr, dst = self.dFU[i % 2], self.sWF[:, 2 + i % 2]
            else:
                iq, mo = i // 16, i % 16
                srcap = self.wdown[iq * 2048:(iq + 1) * 2048, mo * 128:(mo + 1) * 128]
                ctr, dst = self.dFD[i % 2], self.sWD[:, i % 2]
            v = self.dma("gpsimd", ctr, dst, srcap.rearrange("(k p) c -> p k c", p=128))
            st["vals"].append((ctr, v))
            st["next"] += 1
        return st["vals"][j]

    def _ffn_consumed(self, kind, j):
        st = self.ffn_st[kind]
        st["consumed"][j] = self.pe.v
        if st["next"] <= j + 2 and st["next"] < st["count"]:
            self._ffn_issue(kind, min(j + 2, st["count"] - 1))

    def _down_evac(self, iq, sl, tile):
        ve = self.ve
        if iq == 0:
            scr = self.sSC1 if tile % 2 == 0 else self.sSC2
            def fe0(e, sl=sl, scr=scr):
                th0 = (sl.start % T) // 512
                tsl = slice(th0 * 512, th0 * 512 + 512)
                return e.tensor_tensor(scr[:, 0:512], self.sA2[:, sl],
                                       self.sR[:, tsl], ALU.mult)
            v0 = self.on("vector", fe0, inc=ve)
            self.wait("vector", ve, v0)
            def fe2(e, sl=sl, tile=tile, scr=scr):
                return e.tensor_add(self.sA2[:, sl], scr[:, 0:512],
                                    self.ps[tile][:, 0:512])
            self.on("vector", fe2, inc=ve)
        else:
            def fe(e, sl=sl, tile=tile):
                return e.tensor_add(self.sA2[:, sl], self.sA2[:, sl],
                                    self.ps[tile][:, 0:512])
            self.on("vector", fe, inc=ve)

    def phase7_ffn(self):
        pe, ve, se = self.pe, self.ve, self.se
        self.ffn_st = {k: {"next": 0, "vals": [], "consumed": {}, "count": 64}
                       for k in ("g", "u", "d")}
        self.wait("gpsimd", self.pe, self.v_wo_pe)
        sg_busy = {0: 0, 1: 0}
        for iq in range(4):
            pending = []
            for mi in range(16):
                gmi = iq * 16 + mi
                gctr, gval = self._ffn_issue("g", gmi)
                uctr, uval = self._ffn_issue("u", gmi)
                self.wait("tensor", gctr, gval)
                self.wait("tensor", uctr, uval)
                for th in range(2):
                    tG = mi % 2
                    tU = 2 + mi % 2
                    self.pe_wait_free(tG)
                    self.pe_wait_free(tU)
                    if iq == 0 and mi == 0:
                        for piece in range(4):
                            self.wait("tensor", ve, self.v_ychunk[piece * 4 + 3])
                            def fp(e, th=th, tG=tG, tU=tU, piece=piece):
                                last = None
                                for kc in range(piece * 4, piece * 4 + 4):
                                    e.matmul(self.ps[tG][:, 0:512],
                                             self.sWF[:, 0, kc, :],
                                             self.sY[:, kc * T + th * 512: kc * T + th * 512 + 512],
                                             start=(kc == 0), stop=(kc == NCH - 1),
                                             skip_group_check=True)
                                    last = e.matmul(
                                        self.ps[tU][:, 0:512],
                                        self.sWF[:, 2, kc, :],
                                        self.sY[:, kc * T + th * 512: kc * T + th * 512 + 512],
                                        start=(kc == 0), stop=(kc == NCH - 1),
                                        skip_group_check=True)
                                return last
                            v_mm = self.on("tensor", fp, inc=pe)
                    else:
                        def f(e, mi=mi, th=th, tG=tG, tU=tU):
                            for kc in range(NCH):
                                e.matmul(self.ps[tG][:, 0:512],
                                         self.sWF[:, mi % 2, kc, :],
                                         self.sY[:, kc * T + th * 512: kc * T + th * 512 + 512],
                                         start=(kc == 0), stop=(kc == NCH - 1),
                                         skip_group_check=True)
                            last = None
                            for kc in range(NCH):
                                last = e.matmul(
                                    self.ps[tU][:, 0:512],
                                    self.sWF[:, 2 + mi % 2, kc, :],
                                    self.sY[:, kc * T + th * 512: kc * T + th * 512 + 512],
                                    start=(kc == 0), stop=(kc == NCH - 1),
                                    skip_group_check=True)
                            return last
                        v_mm = self.on("tensor", f, inc=pe)
                    self.wait("scalar", pe, v_mm)
                    if sg_busy[th]:
                        self.wait("scalar", ve, sg_busy[th])
                    def f_s(e, tG=tG, th=th):
                        return e.activation(out=self.cSG[:, th, :], in_=self.ps[tG][:, 0:512],
                                            func=AF.Silu)
                    v_si = self.on("scalar", f_s, inc=se)
                    self.mark_ps_free(tG, se)
                    self.wait("vector", se, v_si)
                    self.wait("vector", pe, v_mm)
                    def f_h(e, mi=mi, th=th, tU=tU):
                        if RAW_H:
                            return e.tensor_copy(
                                self.sH[:, mi * T + th * 512: mi * T + th * 512 + 512],
                                self.ps[tU][:, 0:512])
                        return e.tensor_tensor(
                            self.sH[:, mi * T + th * 512: mi * T + th * 512 + 512],
                            self.cSG[:, th, :], self.ps[tU][:, 0:512], ALU.mult)
                    v_h = self.on("vector", f_h, inc=ve)
                    sg_busy[th] = v_h
                    self.mark_ps_free(tU, ve)
                self._ffn_consumed("g", gmi)
                self._ffn_consumed("u", gmi)
                if mi == 15 and not SKIP_SPARSIFY:
                    while pending:
                        pending.pop(0)()
                    v_sp012 = self.ve.v
                    pending += self._sparsify_steps(3)
                if mi % 4 == 3 and mi != 15 and not SKIP_SPARSIFY:
                    pending += self._sparsify_steps(mi // 4)
                for _ in range(5):
                    if pending:
                        pending.pop(0)()
            while pending:
                pending.pop(0)()
            v_sp3 = self.ve.v
            if SKIP_SPARSIFY:
                v_sp012 = v_sp3
            if iq == 0:
                self.wait("tensor", ve, self.v_nmr)
            head_mo = 2
            for mo in range(head_mo):
                dctr, dval = self._ffn_issue("d", iq * 16 + mo)
                self.wait("tensor", dctr, dval)
                self.wait("tensor", ve, v_sp012)
                for th in range(2):
                    tile = 2 * mo + th
                    self.pe_wait_free(tile)
                    def fA(e, mo=mo, th=th, tile=tile):
                        last = None
                        for ik in range(12):
                            last = e.matmul(
                                self.ps[tile][:, 0:512],
                                self.sWD[:, mo % 2, ik, :],
                                self.sH[:, ik * T + th * 512: ik * T + th * 512 + 512],
                                start=(ik == 0), stop=False,
                                skip_group_check=True)
                        return last
                    self.on("tensor", fA, inc=pe)
            self.wait("tensor", ve, v_sp3)
            for mo in range(head_mo):
                for th in range(2):
                    tile = 2 * mo + th
                    def fB(e, mo=mo, th=th, tile=tile, iq=iq):
                        last = None
                        for ik in range(12, 16):
                            last = e.matmul(
                                self.ps[tile][:, 0:512],
                                self.sWD[:, mo % 2, ik, :],
                                self.sH[:, ik * T + th * 512: ik * T + th * 512 + 512],
                                start=False, stop=(ik == 15 and iq != 0),
                                skip_group_check=True)
                        if iq == 0:
                            last = e.matmul(
                                self.ps[tile][:, 0:512], self.cONERb[0:1, :],
                                self.cNMR[0:1, th * 512: th * 512 + 512],
                                start=False, stop=True, skip_group_check=True)
                        return last
                    v_mm = self.on("tensor", fB, inc=pe)
                    self.wait("vector", pe, v_mm)
                    sl = slice(mo * T + th * 512, mo * T + th * 512 + 512)
                    self._down_evac(iq, sl, tile)
                    self.mark_ps_free(tile, ve)
                self._ffn_consumed("d", iq * 16 + mo)
            for mo in range(head_mo, NCH):
                gmo = iq * 16 + mo
                dctr, dval = self._ffn_issue("d", gmo)
                self.wait("tensor", dctr, dval)
                for th in range(2):
                    tile = 4 + th
                    self.pe_wait_free(tile)
                    def f(e, mo=mo, th=th, tile=tile, iq=iq):
                        last = None
                        for ik in range(16):
                            last = e.matmul(
                                self.ps[tile][:, 0:512],
                                self.sWD[:, mo % 2, ik, :],
                                self.sH[:, ik * T + th * 512: ik * T + th * 512 + 512],
                                start=(ik == 0), stop=(ik == 15 and iq != 0),
                                skip_group_check=True)
                        if iq == 0:
                            last = e.matmul(
                                self.ps[tile][:, 0:512], self.cONERb[0:1, :],
                                self.cNMR[0:1, th * 512: th * 512 + 512],
                                start=False, stop=True, skip_group_check=True)
                        return last
                    v_mm = self.on("tensor", f, inc=pe)
                    self.wait("vector", pe, v_mm)
                    sl = slice(mo * T + th * 512, mo * T + th * 512 + 512)
                    self._down_evac(iq, sl, tile)
                    self.mark_ps_free(tile, ve)
                self._ffn_consumed("d", gmo)
        self.v_resid2 = self.ve.v
        self.v_ffn_pe = self.pe.v

    def _sparsify_steps(self, c0):
        ve, se = self.ve, self.se
        sl = [slice((4 * c0 + j) * T, (4 * c0 + j) * T + T) for j in range(4)]
        s = [self.sSP[:, i, :] for i in range(6)]
        st = {}
        def t0():
            self.wait("scalar", ve, self.ve.v)
            def fa01(e):
                e.activation(out=s[3], in_=self.sH[:, sl[0]], func=AF.Abs)
                return e.activation(out=s[4], in_=self.sH[:, sl[1]], func=AF.Abs)
            st["a01"] = self.on("scalar", fa01, inc=se)
        def t1():
            self.wait("vector", se, st["a01"])
            def fmm01(e):
                e.tensor_tensor(s[0], s[3], s[4], ALU.max)
                return e.tensor_tensor(s[1], s[3], s[4], ALU.min)
            st["mm01"] = self.on("vector", fmm01, inc=ve)
        def t2():
            self.wait("scalar", ve, st["mm01"])
            def fa23(e):
                e.activation(out=s[3], in_=self.sH[:, sl[2]], func=AF.Abs)
                return e.activation(out=s[4], in_=self.sH[:, sl[3]], func=AF.Abs)
            st["a23"] = self.on("scalar", fa23, inc=se)
        def t3():
            self.wait("vector", se, st["a23"])
            def fmm23(e):
                e.tensor_tensor(s[2], s[3], s[4], ALU.max)
                return e.tensor_tensor(s[5], s[3], s[4], ALU.min)
            st["mm23"] = self.on("vector", fmm23, inc=ve)
        def t4():
            self.wait("vector", ve, st["mm23"])
            def fab(e):
                e.tensor_tensor(s[0], s[0], s[2], ALU.min)
                return e.tensor_tensor(s[1], s[1], s[5], ALU.max)
            st["ab"] = self.on("vector", fab, inc=ve)
        def t5():
            self.wait("vector", ve, st["ab"])
            def fthr(e):
                return e.tensor_tensor(s[2], s[0], s[1], ALU.max)
            st["thr"] = self.on("vector", fthr, inc=ve)
        steps = [t0, t1, t2, t3, t4, t5]
        for j in range(4):
            def tj_a(j=j):
                self.wait("scalar", ve, self.ve.v)
                def f6(e, j=j):
                    return e.activation(out=s[3], in_=self.sH[:, sl[j]], func=AF.Abs)
                st["aj"] = self.on("scalar", f6, inc=se)
            def tj_b(j=j):
                self.wait("vector", se, st["aj"])
                def f7a(e):
                    return e.tensor_tensor(s[4], s[3], s[2], ALU.is_ge)
                st["ge"] = self.on("vector", f7a, inc=ve)
            def tj_c(j=j):
                self.wait("vector", ve, st["ge"])
                def f7(e, j=j):
                    return e.tensor_tensor(self.sH[:, sl[j]], self.sH[:, sl[j]], s[4], ALU.mult)
                self.on("vector", f7, inc=ve)
            steps += [tj_a, tj_b, tj_c]
        return steps

    def phase8_ln2_out(self):
        pe, ve, dO = self.pe, self.ve, self.dO
        self.wait("vector", pe, self.v_ffn_pe)
        self.wait("vector", ve, self.v_resid2)
        self.wait("scalar", pe, self.v_ffn_pe)
        self.wait("scalar", ve, self.v_resid2)
        self.wait("tensor", ve, self.v_resid2)
        self._ln_stats()
        for kc in range(NCH):
            par = kc % 2
            def f0(e, kc=kc, par=par):
                scr = self.sSC1 if par == 0 else self.sSC2
                return e.tensor_sub(scr[:, :], self.sA2[:, kc * T:(kc + 1) * T], self.sM[:, :])
            v0 = self.on("vector", f0, inc=ve)
            self.wait("vector", ve, v0)
            def f1(e, kc=kc, par=par):
                scr = self.sSC1 if par == 0 else self.sSC2
                return e.tensor_tensor(self.sA2[:, kc * T:(kc + 1) * T], scr[:, :],
                                       self.sR[:, :], ALU.mult)
            v_z = self.on("vector", f1, inc=ve)
            self.wait("sync", ve, v_z)
            self.dma("sync", dO, self.out_fm[kc * 128:(kc + 1) * 128, :],
                     self.sA2[:, kc * T:(kc + 1) * T])

    def emit_debug_dumps(self):
        d = self.dbg
        if not d:
            return
        self.wait("gpsimd", self.ve, self.ve.v)
        self.wait("gpsimd", self.se, self.se.v)
        srcs = dict(q=self.sQ, k=self.sK, vt=self.sVT, g=self.sG, ret=self.sRET,
                    st=self.sSf, acol=self.cACOL, bcol=self.cBCOL, resid1=self.sA2,
                    ybf=self.sY, h=self.sH, gated=self.sRET, csum=self.cSUM,
                    ctot=self.cTOT, ct1=self.cT1, ct3=self.cT3, cmur=self.cMUr,
                    crsr=self.cRSr, wf=self.sWF, wd=self.sWD, csg=self.cSG)
        for name in d:
            self.dma("gpsimd", self.dDBG, d[name], srcs[name][:, :])

    def finalize(self):
        for ctr in ([self.dO, self.dC, self.dXB, self.dXBa, self.dSL, self.dLL, self.dST,
                     self.dSTL, self.dTR, self.dDBG] + self.dWA + self.dFG +
                    self.dFU + self.dFD + self.dX1):
            self.wait("sync", ctr)


def _prep_inputs(inputs):
    perm = _perm_ffn()
    x = np.asarray(inputs["L0_attn_x"], np.float32)
    col = lambda b: np.ascontiguousarray(np.asarray(b, np.float32).reshape(16, 128).T)
    shared = dict(
        wq=np.ascontiguousarray(inputs["L0_attn_wq"], np.float32),
        wk=np.ascontiguousarray(inputs["L0_attn_wk"], np.float32),
        wv=np.ascontiguousarray(inputs["L0_attn_wv"], np.float32),
        wg=np.ascontiguousarray(inputs["L0_attn_wg"], np.float32),
        wo=np.ascontiguousarray(inputs["L0_attn_wo"], np.float32),
        wgate=np.ascontiguousarray(np.asarray(inputs["L0_ffn_w_gate"], np.float32)[:, perm]),
        wup=np.ascontiguousarray(np.asarray(inputs["L0_ffn_w_up"], np.float32)[:, perm]),
        wdown=np.ascontiguousarray(np.asarray(inputs["L0_ffn_w_down"], np.float32)[perm, :]),
        bq_c=col(inputs["L0_attn_bq"]), bk_c=col(inputs["L0_attn_bk"]),
        bg_c=col(inputs["L0_attn_bg"]), bo_c=col(inputs["L0_attn_bo"]),
        gnw_c=col(inputs["L0_attn_gn_w"]), gnb_c=col(inputs["L0_attn_gn_b"]),
        bv_c=col(inputs["L0_attn_bv"]),
        gamma_rep=np.ascontiguousarray(
            np.broadcast_to(np.asarray(inputs["L0_attn_gamma"], np.float32), (128, 16))),
    )
    in_maps = []
    for c in range(8):
        b, qt = c // 4, c % 4
        m = dict(shared)
        m["x_fm"] = np.ascontiguousarray(x[b, qt * T:(qt + 1) * T, :].T)
        mix = np.full(4, 1e9, np.float32)
        for qp in range(qt):
            mix[qp] = 1024.0 * (qt - qp - 1)
        m["mixw_rep"] = np.ascontiguousarray(np.broadcast_to(mix, (128, 4)))
        in_maps.append(m)
    return in_maps


_CACHED = {}


def kernel(**inputs):
    if "nc" not in _CACHED:
        _CACHED["nc"] = Builder().build()
    nc = _CACHED["nc"]
    in_maps = _prep_inputs(inputs)
    res = run_bass_kernel_spmd(nc, in_maps, core_ids=list(range(8)))
    out = np.empty((B, SEQ, HID), np.float32)
    for c in range(8):
        b, qt = c // 4, c % 4
        out[b, qt * T:(qt + 1) * T, :] = res.results[c]["out_fm"].T
    return out


# version 16 (final)
# speedup vs baseline: 1.0318x; 1.0130x over previous
import numpy as np
import ml_dtypes
import concourse.bass as bass
import concourse.mybir as mybir
from concourse.ap import AP
from concourse.bass_utils import run_bass_kernel_spmd

F32 = mybir.dt.float32
BF16 = mybir.dt.bfloat16
AF = mybir.ActivationFunctionType
ALU = mybir.AluOpType
BFNP = ml_dtypes.bfloat16

B, SEQ, HID, NH = 2, 4096, 2048, 16
T = 1024
NCH = 16
IFF = 8192
EPS = 1e-5
NTOK_LN = float(HID)
NTOK_GN = float(SEQ * 128)

STAGE = 99
DUMPS = ()
SUB12 = "all"
SKIP_SPARSIFY = False
RAW_H = False


def _perm_ffn():
    i = np.arange(IFF)
    g, j = i // 4, i % 4
    newpos = (4 * (g // 128) + j) * 128 + (g % 128)
    perm = np.empty(IFF, np.int64)
    perm[newpos] = i
    return perm


class Ctr:
    def __init__(self, sem):
        self.sem = sem
        self.v = 0


class Builder:
    def __init__(self):
        nc = bass.Bass("TRN2", target_bir_lowering=False)
        self.nc = nc
        self.prog = []

        def ein(name, shape):
            return nc.dram_tensor(name, shape, F32, kind="ExternalInput").ap()
        self.x_fm = ein("x_fm", [HID, T])
        self.w = {n: ein(n, [HID, HID]) for n in ("wq", "wk", "wv", "wg", "wo")}
        self.wgate = ein("wgate", [HID, IFF])
        self.wup = ein("wup", [HID, IFF])
        self.wdown = ein("wdown", [IFF, HID])
        self.bq_c = ein("bq_c", [128, 16]); self.bk_c = ein("bk_c", [128, 16])
        self.bg_c = ein("bg_c", [128, 16]); self.bo_c = ein("bo_c", [128, 16])
        self.gnw_c = ein("gnw_c", [128, 16]); self.gnb_c = ein("gnb_c", [128, 16])
        self.bv_c = ein("bv_c", [128, 16])
        self.gamma_rep = ein("gamma_rep", [128, 16])
        self.mixw_rep = ein("mixw_rep", [128, 4])
        self.out_fm = nc.dram_tensor("out_fm", [HID, T], F32, kind="ExternalOutput").ap()

        self.dbg = {}
        for name, shape in [("q", [128, NH * T]), ("k", [128, NH * 512]),
                            ("vt", [128, 4 * HID]), ("g", [128, NH * T]),
                            ("ret", [128, NH * T]), ("st", [128, HID]),
                            ("acol", [128, 16]), ("bcol", [128, 16]),
                            ("resid1", [128, NCH * T]), ("ybf", [128, NCH * T]),
                            ("h", [128, 16 * T]), ("gated", [128, NH * T]),
                            ("csum", [128, 32]), ("ctot", [32, 1]),
                            ("ct1", [16, 1]), ("ct3", [16, 1]),
                            ("cmur", [128, 16]), ("crsr", [128, 16]),
                            ("wf", [128, 4 * NCH * 128]), ("wd", [128, 2 * NCH * 128]),
                            ("csg", [128, 2 * 512])]:
            if name in DUMPS:
                self.dbg[name] = nc.dram_tensor("dbg_" + name, shape, F32,
                                                kind="ExternalOutput").ap()

        self.l_loc = nc.dram_tensor("l_loc", [128, HID], F32).ap()
        self.l_all = nc.dram_tensor("l_all", [4 * 128, HID], F32).ap()
        self.st_loc = nc.dram_tensor("st_loc", [32, 1], F32).ap()
        self.st_all = nc.dram_tensor("st_all", [32, 1], F32).ap()

        self.i_ident_b = nc.inline_tensor(np.eye(128, dtype=np.float32).astype(BFNP), "i_identb").ap()
        self.i_tri = nc.inline_tensor(np.tril(np.ones((128, 128), dtype=np.float32)).T.copy(), "i_tri").ap()
        self.i_iop1 = nc.inline_tensor(
            np.broadcast_to(np.arange(1, 129, dtype=np.float32), (128, 128)).copy(), "i_iop1").ap()
        self.i_ior = nc.inline_tensor(
            np.broadcast_to(127.0 - np.arange(128, dtype=np.float32), (128, 128)).copy(), "i_ior").ap()
        self.i_ones_b = nc.inline_tensor(np.ones((128, 128), np.float32).astype(BFNP), "i_onesb").ap()
        self.i_ones_c = nc.inline_tensor(np.ones((128, 1), np.float32), "i_onesc").ap()
        self.i_ones_r = nc.inline_tensor(np.ones((1, 128), np.float32), "i_onesr").ap()
        self.i_ones_rb = nc.inline_tensor(np.ones((1, 128), np.float32).astype(BFNP), "i_onesrb").ap()

        KB = 1024
        TOP = 229344
        total = 192 * KB
        base = TOP - total
        assert base > 27 * KB, base
        def at(name, shape, dtype, rel):
            return nc.alloc_sbuf_tensor_at(name, shape, dtype, offset=base + rel)
        self.sQ = at("sQ", [128, NH * T], BF16, 0)
        self.sDQ = at("sDQ", [128, NH * 128], BF16, 32 * KB)
        self.sDK = at("sDK", [128, NH * 128], BF16, 36 * KB)
        self.sMASK = at("sMASK", [128, NH * 128], BF16, 40 * KB)
        self.sL = at("sL", [128, HID], F32, 44 * KB)
        self.sSf = at("sSf", [128, HID], F32, 52 * KB)
        self.sSb = at("sSb", [128, HID], BF16, 60 * KB)
        self.sSin = at("sSin", [128, HID], BF16, 60 * KB)
        self.sA2 = at("sA2", [128, NCH * T], F32, 0)
        self.sRET = at("sRET", [128, NH * T], BF16, 64 * KB)
        self.sM = at("sM", [128, T], F32, 64 * KB)
        self.sR = at("sR", [128, T], F32, 68 * KB)
        self.sMR = at("sMR", [128, T], F32, 72 * KB)
        self.sSC1 = at("sSC1", [128, T], F32, 76 * KB)
        self.sSC2 = at("sSC2", [128, T], F32, 80 * KB)
        self.sCB = at("sCB", [128, 2, T], BF16, 84 * KB)
        self.sWD = at("sWD", [128, 2, NCH, 128], BF16, 88 * KB)
        self.sG = at("sG", [128, NH * T], BF16, 96 * KB)
        self.sY = at("sY", [128, NCH * T], BF16, 96 * KB)
        self.sK = at("sK", [128, NH * 512], BF16, 128 * KB)
        self.sLB = at("sLB", [128, 4, HID], F32, 128 * KB)
        self.sVT = at("sVT", [128, 4 * HID], BF16, 144 * KB)
        self.sH = at("sH", [128, 16 * T], BF16, 128 * KB)
        self.sXB = at("sXB", [128, NCH, 512], BF16, 160 * KB)
        self.sSq = at("sSq", [128, T], BF16, 160 * KB)
        self.sX1 = at("sX1", [128, 2, T], F32, 160 * KB)
        self.sSP = at("sSP", [128, 6, T], BF16, 160 * KB)
        self.sWA = at("sWA", [128, 4, NCH, 128], BF16, 176 * KB)
        self.sWF = at("sWF", [128, 4, NCH, 128], BF16, 176 * KB)

        def small(name, shape, dtype=F32):
            return nc.alloc_sbuf_tensor(name, shape, dtype)
        self.cBQ = small("cBQ", [128, 16]); self.cBK = small("cBK", [128, 16])
        self.cBV = small("cBV", [128, 16])
        self.cSV = small("cSV", [128, 2, 512], BF16)
        self.cBG = small("cBG", [128, 16]); self.cBO = small("cBO", [128, 16])
        self.cGNW = small("cGNW", [128, 16]); self.cGNB = small("cGNB", [128, 16])
        self.cGAM = small("cGAM", [128, 16]); self.cMIX = small("cMIX", [128, 4])
        self.cLNG = small("cLNG", [128, 16])
        self.cG128 = small("cG128", [128, 16])
        self.cGinv = small("cGinv", [128, 16])
        self.cG128t = small("cG128t", [128, 8, 16])
        self.cWMIX = small("cWMIX", [128, 4, 16])
        self.cONES = small("cONES", [128, 128], BF16)
        self.cONEC = small("cONEC", [128, 1])
        self.cONER = small("cONER", [1, 128])
        self.cIDb = small("cIDb", [128, 128], BF16)
        self.cIOT = small("cIOT", [128, 128])
        self.cSUM = small("cSUM", [128, 32])
        self.cTOT = small("cTOT", [32, 1])
        self.cT1 = small("cT1", [16, 1]); self.cT2 = small("cT2", [16, 1])
        self.cT3 = small("cT3", [16, 1]); self.cT4 = small("cT4", [16, 1])
        self.cEPS = small("cEPS", [128, 1])
        self.cTRa = small("cTRa", [1, 16])
        self.cTRb = small("cTRb", [1, 16])
        self.cMUr = small("cMUr", [128, 16]); self.cRSr = small("cRSr", [128, 16])
        self.cACOL = small("cACOL", [128, 16]); self.cBCOL = small("cBCOL", [128, 16])
        self.cSCB = small("cSCB", [128, 2, 512], BF16)
        self.cKT2 = small("cKT2", [128, 2, 512], BF16)
        self.cSG = small("cSG", [128, 2, 512], BF16)
        self.cONERb = small("cONERb", [1, 128], BF16)
        self.cNMR = small("cNMR", [1, T], BF16)

        self.ps = [nc.alloc_psum_tensor(f"ps{i}", [128, 512], F32) for i in range(6)]
        self.psTt = [nc.alloc_psum_tensor(f"psT{i}", [128, 1024], BF16) for i in range(2)]
        self.ps_free = {i: (None, 0) for i in range(9)}

        self.dC = Ctr(nc.alloc_semaphore("dC"))
        self.dXB = Ctr(nc.alloc_semaphore("dXB"))
        self.dXBa = Ctr(nc.alloc_semaphore("dXBa"))
        self.dWA = [Ctr(nc.alloc_semaphore(f"dWA{i}")) for i in range(4)]
        self.dFG = [Ctr(nc.alloc_semaphore(f"dFG{i}")) for i in range(2)]
        self.dFU = [Ctr(nc.alloc_semaphore(f"dFU{i}")) for i in range(2)]
        self.dFD = [Ctr(nc.alloc_semaphore(f"dFD{i}")) for i in range(2)]
        self.dX1 = [Ctr(nc.alloc_semaphore(f"dX1{i}")) for i in range(2)]
        self.dSL = Ctr(nc.alloc_semaphore("dSL"))
        self.dLL = Ctr(nc.alloc_semaphore("dLL"))
        self.dST = Ctr(nc.alloc_semaphore("dST"))
        self.dSTL = Ctr(nc.alloc_semaphore("dSTL"))
        self.dTR = Ctr(nc.alloc_semaphore("dTR"))
        self.dO = Ctr(nc.alloc_semaphore("dO"))
        self.dDBG = Ctr(nc.alloc_semaphore("dDBG"))
        self.cc = Ctr(nc.alloc_semaphore("cc"))
        self.pe = Ctr(nc.alloc_semaphore("pe"))
        self.gp = Ctr(nc.alloc_semaphore("gp"))
        self._lastw = {}
        self.ve = Ctr(nc.alloc_semaphore("ve"))
        self.se = Ctr(nc.alloc_semaphore("se"))

    def on(self, eng, fn, inc=None):
        if inc is None:
            self.prog.append((eng, fn))
            return None
        inc.v += 1
        sem = inc.sem
        def wrapped(e, fn=fn, sem=sem):
            inst = fn(e)
            inst.then_inc(sem, 1)
        self.prog.append((eng, wrapped))
        return inc.v

    def dma(self, eng, ctr, dst, src):
        ctr.v += 16
        sem = ctr.sem
        def f(e, dst=dst, src=src, sem=sem):
            e.dma_start(out=dst, in_=src).then_inc(sem, 16)
        self.prog.append((eng, f))
        return ctr.v

    def wait(self, eng, ctr, val=None):
        v = ctr.v if val is None else val
        if v <= 0:
            return
        key = (eng, ctr.sem.num)
        if self._lastw.get(key, -1) >= v:
            return
        self._lastw[key] = v
        sem = ctr.sem
        def f(e, sem=sem, v=v):
            e.wait_ge(sem, v)
        self.prog.append((eng, f))

    def pe_wait_free(self, tile):
        ctr, v = self.ps_free[tile]
        if ctr is not None and v > 0:
            self.wait("tensor", ctr, v)

    def mark_ps_free(self, tile, ctr):
        self.ps_free[tile] = (ctr, ctr.v)

    def build(self):
        nc = self.nc
        self.phase0_consts_tables()
        if STAGE >= 1:
            self.phase12_proj_retention()
        if STAGE >= 3 and SUB12 == "all":
            self.phase3_states_correction()
            if STAGE >= 4:
                self.phase35_stats()
                self.phase4_norm_gate()
            if STAGE >= 5:
                self.phase5_wo()
            if STAGE >= 6:
                self.phase6_ln1()
            if STAGE >= 7:
                self.phase7_ffn()
            if STAGE >= 8:
                self.phase8_ln2_out()
        self.emit_debug_dumps()
        self.finalize()
        with nc.Block() as block:
            items = self.prog
            def mk(name):
                def runner(e):
                    for en, fn in items:
                        if en == name:
                            fn(e)
                return runner
            block.sync(mk("sync"))
            block.tensor(mk("tensor"))
            block.vector(mk("vector"))
            block.scalar(mk("scalar"))
            block.gpsimd(mk("gpsimd"))
        return nc

    def phase0_consts_tables(self):
        dC, ve, se = self.dC, self.ve, self.se
        for dst, src in [(self.cBQ[:, :], self.bq_c), (self.cBK[:, :], self.bk_c),
                         (self.cBG[:, :], self.bg_c), (self.cBO[:, :], self.bo_c),
                         (self.cGNW[:, :], self.gnw_c), (self.cGNB[:, :], self.gnb_c),
                         (self.cGAM[:, :], self.gamma_rep), (self.cMIX[:, :], self.mixw_rep),
                         (self.cONES[:, :], self.i_ones_b), (self.cONEC[:, :], self.i_ones_c),
                         (self.cONER[:, :], self.i_ones_r),
                         (self.cIDb[:, :], self.i_ident_b), (self.cBV[:, :], self.bv_c),
                         (self.cONERb[:, :], self.i_ones_rb)]:
            self.dma("sync", dC, dst, src)
        self.dma("sync", dC, self.cIOT[:, :], self.i_iop1)

        self.wait("vector", dC, dC.v)
        def f1(e):
            return e.tensor_scalar_max(self.cGAM[:, :], self.cGAM[:, :], 1e-30)
        v_clamp = self.on("vector", f1, inc=ve)

        self.wait("scalar", ve, v_clamp)
        self.wait("scalar", dC, dC.v)
        def f2a(e):
            return e.activation(out=self.cLNG[:, :], in_=self.cGAM[:, :], func=AF.Ln)
        v_ln = self.on("scalar", f2a, inc=se)
        self.wait("scalar", se, v_ln)
        def f2(e):
            e.activation(out=self.cG128[:, :], in_=self.cLNG[:, :], func=AF.Exp, scale=128.0)
            e.activation(out=self.cGinv[:, :], in_=self.cLNG[:, :], func=AF.Exp, scale=-128.0)
            for t in range(8):
                e.activation(out=self.cG128t[:, t, :], in_=self.cLNG[:, :], func=AF.Exp,
                             scale=128.0 * t)
            last = None
            for h in range(NH):
                last = e.activation(out=self.sDQ[:, h * 128:(h + 1) * 128],
                                    in_=self.cIOT[:, :], func=AF.Exp,
                                    scale=self.cLNG[:, h:h + 1])
            return last
        v_dq = self.on("scalar", f2, inc=se)
        self.wait("sync", se, v_dq)
        self.dma("sync", dC, self.cIOT[:, :], self.i_ior)
        self.wait("scalar", dC, dC.v)
        def f3(e):
            last = None
            for h in range(NH):
                last = e.activation(out=self.sDK[:, h * 128:(h + 1) * 128],
                                    in_=self.cIOT[:, :], func=AF.Exp,
                                    scale=self.cLNG[:, h:h + 1])
            return last
        v_dk = self.on("scalar", f3, inc=se)
        self.wait("sync", se, v_dk)
        self.dma("sync", dC, self.cIOT[:, :], self.i_tri)

        self.wait("vector", se, v_dk)
        self.wait("vector", dC, dC.v)
        def f4(e):
            for h in range(NH):
                e.tensor_scalar(self.sMASK[:, h * 128:(h + 1) * 128], self.cIOT[:, :],
                                self.cGinv[:, h:h + 1], None, ALU.mult)
            for qt in range(4):
                e.tensor_scalar(self.cWMIX[:, qt, :], self.cLNG[:, :],
                                self.cMIX[:, qt:qt + 1], None, ALU.mult)
            e.memset(self.sSf[:, :], 0.0)
            e.memset(self.cEPS[:, :], EPS)
            return e.memset(self.sSb[:, :], 0.0)
        v_mask = self.on("vector", f4, inc=ve)
        self.wait("scalar", ve, v_mask)
        def f5(e):
            return e.activation(out=self.cWMIX[:, :, :], in_=self.cWMIX[:, :, :], func=AF.Exp)
        self.c_tables_se = self.on("scalar", f5, inc=se)
        self.c_tables_ve = v_mask

    def _att_stream_order(self):
        mats = ["wq", "wk", "wv", "wg"]
        order = []
        for half in range(2):
            for i in range(64):
                order.append((mats[i // 16], i % 16))
        for b in range(16):
            order.append(("wo", b))
        return order

    def _issue_att_load(self):
        st = self.att_st
        i = st["next"]
        mat, b = self.att_order[i]
        if i >= 4:
            gate = st["consumed"].get(i - 4)
            if gate:
                self.wait("gpsimd", self.pe, gate)
        src = self.w[mat][:, b * 128:(b + 1) * 128].rearrange("(k p) c -> p k c", p=128)
        ctr = self.dWA[i % 4]
        v = self.dma("gpsimd", ctr, self.sWA[:, i % 4], src)
        st["vals"].append((ctr, v))
        st["next"] += 1

    def _att_block_val(self, idx):
        while self.att_st["next"] <= idx:
            self._issue_att_load()
        return self.att_st["vals"][idx]

    def _att_consumed(self, idx):
        self.att_st["consumed"][idx] = self.pe.v
        if self.att_st["next"] < len(self.att_order):
            self._issue_att_load()

    def phase12_proj_retention(self):
        pe, ve, se = self.pe, self.ve, self.se
        self.att_order = self._att_stream_order()
        self.att_st = {"next": 0, "vals": [], "consumed": {}}
        self._att_block_val(3)
        self.sb_ready = {hg: 0 for hg in range(4)}
        self.scb_busy = {0: 0, 1: 0}
        self.kt_busy = {0: 0, 1: 0}

        for half in range(2):
            if half == 1:
                self.wait("gpsimd", pe, self.v_xb_users)
            v_xb_lo = self.dma(
                "gpsimd", self.dXBa, self.sXB[:, 0:8, :],
                self.x_fm[0:1024, half * 512:(half + 1) * 512].rearrange("(k p) t -> p k t", p=128))
            v_xb = self.dma(
                "gpsimd", self.dXB, self.sXB[:, 8:16, :],
                self.x_fm[1024:2048, half * 512:(half + 1) * 512].rearrange("(k p) t -> p k t", p=128))
            base_blk = half * 64
            if half == 0:
                self.wait("vector", se, self.c_tables_se)

            for wi, wname in enumerate(["wq", "wk"]):
                if SUB12 == "q" and wi == 1:
                    return
                for m in range(NCH):
                    if isinstance(SUB12, str) and SUB12.startswith("q") and len(SUB12) > 1 \
                            and (wi > 0 or m >= int(SUB12[1:])):
                        return
                    blk = base_blk + wi * 16 + m
                    bctr, bval = self._att_block_val(blk)
                    self.wait("tensor", bctr, bval)
                    tile = m % 4
                    self.pe_wait_free(tile)
                    par = blk % 4
                    if wi == 0 and m < 2:
                        self.wait("tensor", self.dXBa, v_xb_lo)
                        def fl(e, par=par, tile=tile):
                            last = None
                            for kc in range(8):
                                last = e.matmul(self.ps[tile][:, 0:512],
                                                self.sWA[:, par, kc, :],
                                                self.sXB[:, kc, :],
                                                start=(kc == 0), stop=False,
                                                skip_group_check=True)
                            return last
                        self.on("tensor", fl, inc=pe)
                        self.wait("tensor", self.dXB, v_xb)
                        def fh(e, par=par, tile=tile):
                            last = None
                            for kc in range(8, NCH):
                                last = e.matmul(self.ps[tile][:, 0:512],
                                                self.sWA[:, par, kc, :],
                                                self.sXB[:, kc, :],
                                                start=False, stop=(kc == NCH - 1),
                                                skip_group_check=True)
                            return last
                        v_mm = self.on("tensor", fh, inc=pe)
                    else:
                        self.wait("tensor", self.dXB, v_xb)
                        def f(e, par=par, tile=tile):
                            last = None
                            for kc in range(NCH):
                                last = e.matmul(self.ps[tile][:, 0:512],
                                                self.sWA[:, par, kc, :],
                                                self.sXB[:, kc, :],
                                                start=(kc == 0), stop=(kc == NCH - 1),
                                                skip_group_check=True)
                            return last
                        v_mm = self.on("tensor", f, inc=pe)
                    self._att_consumed(blk)
                    self.wait("vector", pe, v_mm)
                    if wname == "wq":
                        dst = self.sQ[:, m * T + half * 512: m * T + half * 512 + 512]
                        dec, bias = self.sDQ, self.cBQ
                    else:
                        dst = self.sK[:, m * 512:(m + 1) * 512]
                        dec, bias = self.sDK, self.cBK
                    def fe(e, m=m, tile=tile, dst=dst, dec=dec, bias=bias):
                        din = AP(dec, m * 128, [[NH * 128, 128], [0, 4], [1, 128]])
                        return e.scalar_tensor_tensor(
                            out=dst.rearrange("p (r c) -> p r c", r=4),
                            in0=self.ps[tile][:, 0:512].rearrange("p (r c) -> p r c", r=4),
                            scalar=bias[:, m:m + 1], in1=din,
                            op0=ALU.add, op1=ALU.mult)
                    self.on("vector", fe, inc=ve)
                    self.mark_ps_free(tile, ve)

            if SUB12 == "k":
                return
            pend_tr = [None]
            def emit_vt_transposes():
                m, par, v_csv = pend_tr[0]
                tT = m % 2
                self.pe_wait_free(7 + tT)
                self.wait("tensor", ve, v_csv)
                def f_tr(e, par=par, tT=tT):
                    last = None
                    for ts in range(4):
                        last = e.transpose(
                            self.psTt[tT][:, ts * 128: ts * 128 + 128],
                            self.cSV[:, par, ts * 128: ts * 128 + 128],
                            self.cIDb[:, :])
                    return last
                v_tr = self.on("tensor", f_tr, inc=pe)
                self.csv_busy[par] = v_tr
                self.wait("vector", pe, v_tr)
                def f_tre(e, m=m, tT=tT):
                    out = AP(self.sVT, m * 128, [[4 * HID, 128], [HID, 4], [1, 128]])
                    return e.tensor_copy(
                        out, self.psTt[tT][:, 0:512].rearrange("p (r c) -> p r c", r=4))
                self.on("vector", f_tre, inc=ve)
                self.mark_ps_free(7 + tT, ve)
                pend_tr[0] = None
            self.csv_busy = {0: 0, 1: 0}
            for m in range(NCH):
                blk = base_blk + 32 + m
                bctr, bval = self._att_block_val(blk)
                self.wait("tensor", bctr, bval)
                self.wait("tensor", self.dXB, v_xb)
                tile = 2 + m % 4 if False else (2 + m % 4)
                tile = 2 + (m % 4)
                self.pe_wait_free(tile)
                par = blk % 4
                def f(e, par=par, tile=tile):
                    last = None
                    for kc in range(NCH):
                        last = e.matmul(self.ps[tile][:, 0:512],
                                        self.sWA[:, par, kc, :],
                                        self.sXB[:, kc, :],
                                        start=(kc == 0), stop=(kc == NCH - 1),
                                        skip_group_check=True)
                    return last
                v_mm = self.on("tensor", f, inc=pe)
                self._att_consumed(blk)
                cpar = m % 2
                self.wait("vector", pe, max(v_mm, self.csv_busy[cpar]))
                def fe(e, m=m, tile=tile, cpar=cpar):
                    return e.tensor_scalar(self.cSV[:, cpar, :], self.ps[tile][:, 0:512],
                                           self.cBV[:, m:m + 1], None, ALU.add)
                v_csv = self.on("vector", fe, inc=ve)
                self.mark_ps_free(tile, ve)
                if pend_tr[0] is not None:
                    emit_vt_transposes()
                pend_tr[0] = (m, cpar, v_csv)
            emit_vt_transposes()

            if SUB12 == "vt":
                return
            for m in ([] if half == 1 and SUB12 == "all" and STAGE >= 3 else range(NCH)):
                blk = base_blk + 48 + m
                bctr, bval = self._att_block_val(blk)
                self.wait("tensor", bctr, bval)
                self.wait("tensor", self.dXB, v_xb)
                tile = m % 4
                self.pe_wait_free(tile)
                par = blk % 4
                def f(e, par=par, tile=tile):
                    last = None
                    for kc in range(NCH):
                        last = e.matmul(self.ps[tile][:, 0:512],
                                        self.sWA[:, par, kc, :],
                                        self.sXB[:, kc, :],
                                        start=(kc == 0), stop=(kc == NCH - 1),
                                        skip_group_check=True)
                    return last
                v_mm = self.on("tensor", f, inc=pe)
                self._att_consumed(blk)
                self.wait("scalar", pe, v_mm)
                def fe(e, m=m, tile=tile, half=half):
                    return e.activation(
                        out=self.sG[:, m * T + half * 512: m * T + half * 512 + 512],
                        in_=self.ps[tile][:, 0:512], func=AF.Silu,
                        bias=self.cBG[:, m:m + 1])
                self.on("scalar", fe, inc=se)
                self.mark_ps_free(tile, se)

            self.v_xb_users = pe.v

            if SUB12 == "g" or STAGE < 2:
                self.v_ret_done_ve = ve.v
                return

            if half == 0:
                self.wait("tensor", se, self.c_tables_se)
                self.wait("tensor", ve, self.c_tables_ve)

            def stage_a(tl, hg):
                gt = half * 4 + tl
                tS = hg % 2
                tT = hg % 2
                self.pe_wait_free(tS)
                self.pe_wait_free(7 + tT)
                def f_a(e, tl=tl, gt=gt, hg=hg, tS=tS, tT=tT):
                    for hl in range(4):
                        h = hg * 4 + hl
                        e.matmul(
                            self.ps[tS][:, hl * 128: hl * 128 + 128],
                            self.sK[:, h * 512 + tl * 128: h * 512 + tl * 128 + 128],
                            self.sQ[:, h * T + gt * 128: h * T + gt * 128 + 128],
                            start=True, stop=True, skip_group_check=True)
                    last = None
                    for hl in range(4):
                        h = hg * 4 + hl
                        last = e.transpose(
                            self.psTt[tT][:, hl * 128: hl * 128 + 128],
                            self.sK[:, h * 512 + tl * 128: h * 512 + tl * 128 + 128],
                            self.cIDb[:, :])
                    return last
                v_a = self.on("tensor", f_a, inc=pe)
                self.wait("vector", pe, max(v_a, self.scb_busy[hg % 2], self.kt_busy[tT]))
                def f_me(e, hg=hg, tS=tS):
                    return e.tensor_tensor(self.cSCB[:, hg % 2, :],
                                           self.ps[tS][:, 0:512],
                                           self.sMASK[:, hg * 512: hg * 512 + 512],
                                           ALU.mult)
                v_me = self.on("vector", f_me, inc=ve)
                self.mark_ps_free(tS, ve)
                def f_kte(e, tT=tT):
                    return e.tensor_copy(self.cKT2[:, tT, :],
                                         self.psTt[tT][:, 0:512])
                v_kte = self.on("vector", f_kte, inc=ve)
                self.mark_ps_free(7 + tT, ve)
                return (tl, hg, v_me, v_kte)

            def stage_b(rec):
                tl, hg, v_me, v_kte = rec
                gt = half * 4 + tl
                tR = 2 + hg % 2
                tSt = 4 + hg % 2
                tT = hg % 2
                self.pe_wait_free(tR)
                self.pe_wait_free(tSt)
                self.wait("tensor", ve, v_kte)
                if self.sb_ready[hg]:
                    self.wait("tensor", ve, self.sb_ready[hg])
                def f_b(e, tl=tl, gt=gt, hg=hg, tR=tR, tSt=tSt, tT=tT):
                    for hl in range(4):
                        h = hg * 4 + hl
                        e.matmul(self.ps[tR][:, hl * 128: hl * 128 + 128],
                                 self.sVT[:, tl * HID + h * 128: tl * HID + h * 128 + 128],
                                 self.cSCB[:, hg % 2, hl * 128: hl * 128 + 128],
                                 start=True, stop=False, skip_group_check=True)
                        e.matmul(self.ps[tR][:, hl * 128: hl * 128 + 128],
                                 self.sSb[:, h * 128: h * 128 + 128],
                                 self.sQ[:, h * T + gt * 128: h * T + gt * 128 + 128],
                                 start=False, stop=True, skip_group_check=True)
                    last = None
                    for hl in range(4):
                        h = hg * 4 + hl
                        last = e.matmul(
                            self.ps[tSt][:, hl * 128: hl * 128 + 128],
                            self.cKT2[:, tT, hl * 128: hl * 128 + 128],
                            self.sVT[:, tl * HID + h * 128: tl * HID + h * 128 + 128],
                            start=True, stop=True, skip_group_check=True)
                    return last
                v_b = self.on("tensor", f_b, inc=pe)
                self.scb_busy[hg % 2] = v_b
                self.kt_busy[tT] = v_b
                self.wait("vector", pe, v_b)
                def f_re(e, gt=gt, hg=hg, tR=tR):
                    out = AP(self.sRET, (hg * 4) * T + gt * 128,
                             [[NH * T, 128], [T, 4], [1, 128]])
                    return e.tensor_copy(
                        out, self.ps[tR][:, 0:512].rearrange("p (r c) -> p r c", r=4))
                self.on("vector", f_re, inc=ve)
                self.mark_ps_free(tR, ve)
                def f_su(e, hg=hg, tSt=tSt):
                    last = None
                    for hl in range(4):
                        h = hg * 4 + hl
                        last = e.scalar_tensor_tensor(
                            out=self.sSf[:, h * 128: h * 128 + 128],
                            in0=self.sSf[:, h * 128: h * 128 + 128],
                            scalar=self.cG128[:, h:h + 1],
                            in1=self.ps[tSt][:, hl * 128: hl * 128 + 128],
                            op0=ALU.mult, op1=ALU.add)
                    return last
                v_sup = self.on("vector", f_su, inc=ve)
                self.mark_ps_free(tSt, ve)
                self.wait("vector", ve, v_sup)
                def f_sub(e, hg=hg):
                    return e.tensor_copy(self.sSb[:, hg * 512: hg * 512 + 512],
                                         self.sSf[:, hg * 512: hg * 512 + 512])
                self.sb_ready[hg] = self.on("vector", f_sub, inc=ve)

            items = [(tl, hg) for tl in range(4) for hg in range(4)]
            prev = None
            for it in items:
                rec = stage_a(*it)
                if prev is not None:
                    stage_b(prev)
                prev = rec
            stage_b(prev)
        self.v_ret_done_ve = self.ve.v

    def _emit_g_half1(self):
        pe, se = self.pe, self.se
        for m in range(NCH):
            blk = 64 + 48 + m
            bctr, bval = self._att_block_val(blk)
            self.wait("tensor", bctr, bval)
            tile = m % 4
            self.pe_wait_free(tile)
            par = blk % 4
            def f(e, par=par, tile=tile):
                last = None
                for kc in range(NCH):
                    last = e.matmul(self.ps[tile][:, 0:512],
                                    self.sWA[:, par, kc, :],
                                    self.sXB[:, kc, :],
                                    start=(kc == 0), stop=(kc == NCH - 1),
                                    skip_group_check=True)
                return last
            v_mm = self.on("tensor", f, inc=pe)
            self._att_consumed(blk)
            self.wait("scalar", pe, v_mm)
            def fe(e, m=m, tile=tile):
                return e.activation(
                    out=self.sG[:, m * T + 512: m * T + 1024],
                    in_=self.ps[tile][:, 0:512], func=AF.Silu,
                    bias=self.cBG[:, m:m + 1])
            self.on("scalar", fe, inc=se)
            self.mark_ps_free(tile, se)

    def phase3_states_correction(self):
        pe, ve, cc, se = self.pe, self.ve, self.cc, self.se
        self.wait("sync", ve, self.v_ret_done_ve)
        v_l = self.dma("sync", self.dLL, self.l_loc, self.sSf[:, :])
        self.wait("gpsimd", self.dLL, v_l)
        cc.v += 1
        def f_cc(e):
            e.collective_compute(
                "AllGather", ALU.bypass,
                replica_groups=[[0, 1, 2, 3], [4, 5, 6, 7]],
                ins=[self.l_loc], outs=[self.l_all],
            ).then_inc(self.cc.sem, 1)
        self.prog.append(("gpsimd", f_cc))
        v_ccag = cc.v
        self._emit_g_half1()
        self.wait("sync", cc, v_ccag)
        v_lb = self.dma("sync", self.dSL, self.sLB[:, :, :],
                        self.l_all.rearrange("(q p) c -> p q c", p=128))
        self.wait("vector", self.dSL, v_lb)
        v_acc = 0
        for qt in range(4):
            if v_acc:
                self.wait("vector", ve, v_acc)
            def f_acc(e, qt=qt):
                last = None
                for h in range(NH):
                    if qt == 0:
                        last = e.tensor_scalar(
                            self.sSin[:, h * 128: h * 128 + 128],
                            self.sLB[:, qt, h * 128: h * 128 + 128],
                            self.cWMIX[:, qt, h:h + 1], None, ALU.mult)
                    else:
                        last = e.scalar_tensor_tensor(
                            out=self.sSin[:, h * 128: h * 128 + 128],
                            in0=self.sLB[:, qt, h * 128: h * 128 + 128],
                            scalar=self.cWMIX[:, qt, h:h + 1],
                            in1=self.sSin[:, h * 128: h * 128 + 128],
                            op0=ALU.mult, op1=ALU.add)
                return last
            v_acc = self.on("vector", f_acc, inc=ve)
        v_sin = v_acc

        self.wait("tensor", ve, v_sin)
        stats_pend = []
        def drain_stat():
            fn, vdep = stats_pend.pop(0)
            self.wait("vector", ve, vdep)
            self.on("vector", fn, inc=ve)
        for hg in range(4):
            for gt in range(8):
                if stats_pend and gt % 2 == 0:
                    drain_stat()
                tile = gt % 2
                self.pe_wait_free(tile)
                def f_mm(e, gt=gt, hg=hg, tile=tile):
                    last = None
                    for hl in range(4):
                        h = hg * 4 + hl
                        last = e.matmul(
                            self.ps[tile][:, hl * 128: hl * 128 + 128],
                            self.sSin[:, h * 128: h * 128 + 128],
                            self.sQ[:, h * T + gt * 128: h * T + gt * 128 + 128],
                            start=True, stop=True, skip_group_check=True)
                    return last
                v_mm = self.on("tensor", f_mm, inc=pe)
                self.wait("vector", pe, v_mm)
                def f_ev(e, gt=gt, hg=hg, tile=tile):
                    last = None
                    for hl in range(4):
                        h = hg * 4 + hl
                        dst = self.sRET[:, h * T + gt * 128: h * T + gt * 128 + 128]
                        last = e.scalar_tensor_tensor(
                            out=dst,
                            in0=self.ps[tile][:, hl * 128: hl * 128 + 128],
                            scalar=self.cG128t[:, gt, h:h + 1],
                            in1=dst, op0=ALU.mult, op1=ALU.add)
                    return last
                self.on("vector", f_ev, inc=ve)
                self.mark_ps_free(tile, ve)
            v_done = self.ve.v
            self.wait("scalar", ve, v_done)
            for hl in range(4):
                self.wait("scalar", se, self.se.v)
                def f_q(e, h=hg * 4 + hl):
                    return e.activation(out=self.sSq[:, :],
                                        in_=self.sRET[:, h * T:(h + 1) * T],
                                        func=AF.Square,
                                        accum_out=self.cSUM[:, 16 + h:17 + h])
                self.on("scalar", f_q, inc=se)
            for hl in range(4):
                def f_s(e, h=hg * 4 + hl):
                    return e.reduce_sum(self.cSUM[:, h:h + 1],
                                        self.sRET[:, h * T:(h + 1) * T],
                                        mybir.AxisListType.X)
                stats_pend.append((f_s, v_done))
            if hg == 3:
                while stats_pend:
                    drain_stat()
        self.v_ret_final_ve = ve.v

    def phase35_stats(self):
        pe, ve, se, cc = self.pe, self.ve, self.se, self.cc
        v_sums = self.ve.v
        v_sq = self.se.v
        self.wait("tensor", ve, v_sums)
        self.wait("tensor", se, v_sq)
        self.pe_wait_free(5)
        def f_rr(e):
            return e.matmul(self.ps[5][0:32, 0:1], self.cSUM[:, :], self.cONEC[:, :],
                            start=True, stop=True, skip_group_check=True)
        v_rr = self.on("tensor", f_rr, inc=pe)
        self.wait("vector", pe, v_rr)
        def f_ev(e):
            return e.tensor_copy(self.cTOT[:, :], self.ps[5][0:32, 0:1])
        v_tot = self.on("vector", f_ev, inc=ve)
        self.mark_ps_free(5, ve)
        self.wait("sync", ve, v_tot)
        v_st = self.dma("sync", self.dST, self.st_loc, self.cTOT[:, :])
        self.wait("gpsimd", self.dST, v_st)
        cc.v += 1
        def f_cc(e):
            e.collective_compute(
                "AllReduce", ALU.add,
                replica_groups=[[0, 1, 2, 3], [4, 5, 6, 7]],
                ins=[self.st_loc], outs=[self.st_all],
            ).then_inc(cc.sem, 1)
        self.prog.append(("gpsimd", f_cc))
        self.wait("sync", cc, cc.v)
        self.dma("sync", self.dTR, self.cTRa[0:1, :], self.st_all[0:16, 0:1])
        v_t2 = self.dma("sync", self.dTR, self.cTRb[0:1, :], self.st_all[16:32, 0:1])
        self.wait("tensor", self.dTR, v_t2)
        self.pe_wait_free(5)
        def f_rep(e):
            e.matmul(self.ps[5][:, 0:16], self.cONER[0:1, :], self.cTRa[0:1, :],
                     start=True, stop=True, skip_group_check=True)
            return e.matmul(self.ps[5][:, 16:32], self.cONER[0:1, :], self.cTRb[0:1, :],
                            start=True, stop=True, skip_group_check=True)
        v_rep = self.on("tensor", f_rep, inc=pe)
        self.wait("vector", pe, v_rep)
        def f_c1(e):
            e.tensor_scalar_mul(self.cMUr[:, :], self.ps[5][:, 0:16], 1.0 / NTOK_GN)
            return e.tensor_scalar_mul(self.cRSr[:, :], self.ps[5][:, 16:32], 1.0 / NTOK_GN)
        v_c1 = self.on("vector", f_c1, inc=ve)
        self.mark_ps_free(5, ve)
        self.wait("vector", ve, v_c1)
        def f_c2a(e):
            return e.tensor_mul(self.cACOL[:, :], self.cMUr[:, :], self.cMUr[:, :])
        v_c2a = self.on("vector", f_c2a, inc=ve)
        self.wait("vector", ve, v_c2a)
        def f_c2(e):
            return e.tensor_sub(self.cRSr[:, :], self.cRSr[:, :], self.cACOL[:, :])
        v_c2 = self.on("vector", f_c2, inc=ve)
        self.wait("scalar", ve, v_c2)
        def f_sd(e):
            return e.activation(out=self.cRSr[:, :], in_=self.cRSr[:, :], func=AF.Sqrt,
                                bias=self.cEPS[:, 0:1])
        v_sd = self.on("scalar", f_sd, inc=se)
        self.wait("vector", se, v_sd)
        def f_rs(e):
            return e.reciprocal(self.cRSr[:, :], self.cRSr[:, :])
        v_rs = self.on("vector", f_rs, inc=ve)
        self.wait("vector", ve, v_rs)
        def f_ab1(e):
            return e.tensor_mul(self.cACOL[:, :], self.cRSr[:, :], self.cGNW[:, :])
        v_a1 = self.on("vector", f_ab1, inc=ve)
        self.wait("vector", ve, v_a1)
        def f_ab2(e):
            return e.tensor_mul(self.cMUr[:, :], self.cMUr[:, :], self.cACOL[:, :])
        v_a2 = self.on("vector", f_ab2, inc=ve)
        self.wait("vector", ve, v_a2)
        def f_ab3(e):
            return e.tensor_sub(self.cBCOL[:, :], self.cGNB[:, :], self.cMUr[:, :])
        self.v_ab = self.on("vector", f_ab3, inc=ve)
        self.mark_ps_free(5, ve)

    def phase4_norm_gate(self):
        ve, se, gp = self.ve, self.se, self.gp
        self.wait("vector", ve, self.v_ab)
        self.wait("vector", se, self.se.v)
        for h in range(NH):
            par = 4 + h % 2
            def f0(e, h=h, par=par):
                return e.tensor_scalar(self.sSP[:, par, :], self.sRET[:, h * T:(h + 1) * T],
                                       self.cACOL[:, h:h + 1], self.cBCOL[:, h:h + 1],
                                       ALU.mult, ALU.add)
            v0 = self.on("vector", f0, inc=ve)
            self.wait("vector", ve, v0)
            def f1(e, h=h, par=par):
                return e.tensor_tensor(self.sRET[:, h * T:(h + 1) * T],
                                       self.sSP[:, par, :],
                                       self.sG[:, h * T:(h + 1) * T], ALU.mult)
            self.on("vector", f1, inc=ve)
        self.v_gated = self.ve.v
        self.v_gated_gp = 0

    def phase5_wo(self):
        pe, ve = self.pe, self.ve
        self.wait("tensor", ve, self.v_gated)
        self.wait("tensor", self.gp, self.v_gated_gp)
        x1_free = {0: 0, 1: 0}
        for m in range(NCH):
            blk = 128 + m
            bctr, bval = self._att_block_val(blk)
            par_x = m % 2
            if x1_free[par_x]:
                self.wait("sync", ve, x1_free[par_x])
            v_x = self.dma("sync", self.dX1[par_x], self.sX1[:, par_x, :],
                           self.x_fm[m * 128:(m + 1) * 128, :])
            self.wait("tensor", bctr, bval)
            for th in range(2):
                tile = (2 * m + th) % 4
                self.pe_wait_free(tile)
                parw = blk % 4
                def f(e, parw=parw, tile=tile, th=th):
                    last = None
                    for kc in range(NCH):
                        last = e.matmul(
                            self.ps[tile][:, 0:512],
                            self.sWA[:, parw, kc, :],
                            self.sRET[:, kc * T + th * 512: kc * T + th * 512 + 512],
                            start=(kc == 0), stop=(kc == NCH - 1),
                            skip_group_check=True)
                    return last
                v_mm = self.on("tensor", f, inc=pe)
                self.wait("vector", pe, v_mm)
                self.wait("vector", self.dX1[par_x], v_x)
                def fe(e, m=m, th=th, tile=tile, par_x=par_x):
                    return e.scalar_tensor_tensor(
                        out=self.sA2[:, m * T + th * 512: m * T + th * 512 + 512],
                        in0=self.ps[tile][:, 0:512],
                        scalar=self.cBO[:, m:m + 1],
                        in1=self.sX1[:, par_x, th * 512: th * 512 + 512],
                        op0=ALU.add, op1=ALU.add)
                self.on("vector", fe, inc=ve)
                self.mark_ps_free(tile, ve)
            x1_free[par_x] = ve.v
            self._att_consumed(blk)
        self.v_resid1 = ve.v
        self.v_wo_pe = pe.v

    def _ln_stats(self):
        pe, ve, se = self.pe, self.ve, self.se
        for t in range(4):
            self.pe_wait_free(t)
        cb_busy = {0: 0, 1: 0}
        for kc in range(NCH):
            pcb = kc % 2
            if cb_busy[pcb]:
                self.wait("vector", pe, cb_busy[pcb])
                self.wait("scalar", pe, cb_busy[pcb])
            def f_c(e, kc=kc, pcb=pcb):
                return e.tensor_copy(self.sCB[:, pcb, :], self.sA2[:, kc * T:(kc + 1) * T])
            v_c = self.on("vector", f_c, inc=ve)
            def f_q(e, kc=kc, pcb=pcb):
                return e.activation(out=self.sSP[:, 4 + pcb, :],
                                    in_=self.sA2[:, kc * T:(kc + 1) * T],
                                    func=AF.Square)
            v_q = self.on("scalar", f_q, inc=se)
            self.wait("tensor", ve, v_c)
            self.wait("tensor", se, v_q)
            def f_mm(e, kc=kc, pcb=pcb):
                last = None
                for th in range(2):
                    e.matmul(self.ps[0 + th][:, 0:512], self.cONES[:, :],
                             self.sCB[:, pcb, th * 512: th * 512 + 512],
                             start=(kc == 0), stop=(kc == NCH - 1),
                             skip_group_check=True)
                    last = e.matmul(self.ps[2 + th][:, 0:512], self.cONES[:, :],
                                    self.sSP[:, 4 + pcb, th * 512: th * 512 + 512],
                                    start=(kc == 0), stop=(kc == NCH - 1),
                                    skip_group_check=True)
                return last
            cb_busy[pcb] = self.on("tensor", f_mm, inc=pe)
        v_stats_pe = self.pe.v
        self.wait("vector", pe, v_stats_pe)
        def f_mr0(e):
            last = None
            for th in range(2):
                sl = slice(th * 512, th * 512 + 512)
                e.tensor_scalar_mul(self.sM[:, sl], self.ps[th][:, 0:512], 1.0 / NTOK_LN)
                last = e.tensor_scalar_mul(self.sSC1[:, sl], self.ps[2 + th][:, 0:512],
                                           1.0 / NTOK_LN)
            return last
        v_mr0 = self.on("vector", f_mr0, inc=ve)
        self.wait("vector", ve, v_mr0)
        def f_mr1(e):
            return e.tensor_mul(self.sSC2[:, :], self.sM[:, :], self.sM[:, :])
        v_mr1 = self.on("vector", f_mr1, inc=ve)
        self.wait("vector", ve, v_mr1)
        def f_mr2(e):
            return e.tensor_sub(self.sSC1[:, :], self.sSC1[:, :], self.sSC2[:, :])
        v_var = self.on("vector", f_mr2, inc=ve)
        for t in range(4):
            self.mark_ps_free(t, ve)
        self.wait("scalar", ve, v_var)
        def f_sd(e):
            return e.activation(out=self.sSC2[:, :], in_=self.sSC1[:, :], func=AF.Sqrt,
                                bias=self.cEPS[:, 0:1])
        v_sd = self.on("scalar", f_sd, inc=se)
        self.wait("vector", se, v_sd)
        def f_r(e):
            return e.reciprocal(self.sR[:, :], self.sSC2[:, :])
        return self.on("vector", f_r, inc=ve)

    def phase6_ln1(self):
        pe, ve = self.pe, self.ve
        self.wait("vector", pe, self.v_wo_pe)
        self.wait("vector", ve, self.v_resid1)
        self.wait("scalar", pe, self.v_wo_pe)
        self.wait("scalar", ve, self.v_resid1)
        self.wait("tensor", ve, self.v_resid1)
        v_lnr = self._ln_stats()
        self.wait("vector", ve, v_lnr)
        def f_mr(e):
            return e.tensor_mul(self.sMR[:, :], self.sM[:, :], self.sR[:, :])
        v_mrp = self.on("vector", f_mr, inc=ve)
        self.wait("vector", ve, v_mrp)
        def f_nmr(e):
            return e.tensor_scalar_mul(self.cNMR[0:1, :], self.sMR[0:1, :], -1.0)
        self.v_nmr = self.on("vector", f_nmr, inc=ve)
        self.v_ychunk = {}
        for kc in range(NCH):
            scr = self.sSC1 if kc % 2 == 0 else self.sSC2
            def f0(e, kc=kc, scr=scr):
                return e.tensor_sub(scr[:, :], self.sA2[:, kc * T:(kc + 1) * T], self.sM[:, :])
            v0 = self.on("vector", f0, inc=ve)
            self.wait("vector", ve, v0)
            def f1(e, kc=kc, scr=scr):
                return e.tensor_tensor(self.sY[:, kc * T:(kc + 1) * T], scr[:, :],
                                       self.sR[:, :], ALU.mult)
            self.v_ychunk[kc] = self.on("vector", f1, inc=ve)
        self.v_ybf = self.ve.v

    def _ffn_issue(self, kind, j):
        st = self.ffn_st[kind]
        while st["next"] <= j:
            i = st["next"]
            if i >= 2:
                gate = st["consumed"].get(i - 2)
                if gate:
                    self.wait("gpsimd", self.pe, gate)
            if kind == "g":
                srcap = self.wgate[:, i * 128:(i + 1) * 128]
                ctr, dst = self.dFG[i % 2], self.sWF[:, i % 2]
            elif kind == "u":
                srcap = self.wup[:, i * 128:(i + 1) * 128]
                ctr, dst = self.dFU[i % 2], self.sWF[:, 2 + i % 2]
            else:
                iq, mo = i // 16, i % 16
                srcap = self.wdown[iq * 2048:(iq + 1) * 2048, mo * 128:(mo + 1) * 128]
                ctr, dst = self.dFD[i % 2], self.sWD[:, i % 2]
            v = self.dma("gpsimd", ctr, dst, srcap.rearrange("(k p) c -> p k c", p=128))
            st["vals"].append((ctr, v))
            st["next"] += 1
        return st["vals"][j]

    def _ffn_consumed(self, kind, j):
        st = self.ffn_st[kind]
        st["consumed"][j] = self.pe.v
        if st["next"] <= j + 2 and st["next"] < st["count"]:
            self._ffn_issue(kind, min(j + 2, st["count"] - 1))

    def _down_evac(self, iq, sl, tile):
        ve = self.ve
        if iq == 0:
            scr = self.sSC1 if tile % 2 == 0 else self.sSC2
            def fe0(e, sl=sl, scr=scr):
                th0 = (sl.start % T) // 512
                tsl = slice(th0 * 512, th0 * 512 + 512)
                return e.tensor_tensor(scr[:, 0:512], self.sA2[:, sl],
                                       self.sR[:, tsl], ALU.mult)
            v0 = self.on("vector", fe0, inc=ve)
            self.wait("vector", ve, v0)
            def fe2(e, sl=sl, tile=tile, scr=scr):
                return e.tensor_add(self.sA2[:, sl], scr[:, 0:512],
                                    self.ps[tile][:, 0:512])
            self.on("vector", fe2, inc=ve)
        else:
            def fe(e, sl=sl, tile=tile):
                return e.tensor_add(self.sA2[:, sl], self.sA2[:, sl],
                                    self.ps[tile][:, 0:512])
            self.on("vector", fe, inc=ve)

    def phase7_ffn(self):
        pe, ve, se = self.pe, self.ve, self.se
        self.ffn_st = {k: {"next": 0, "vals": [], "consumed": {}, "count": 64}
                       for k in ("g", "u", "d")}
        self.wait("gpsimd", self.pe, self.v_wo_pe)
        sg_busy = {0: 0, 1: 0}
        for iq in range(4):
            pending = []
            self._ffn_issue("g", min(iq * 16 + 1, 63))
            self._ffn_issue("u", min(iq * 16 + 1, 63))
            self._ffn_issue("d", min(iq * 16 + 1, 63))
            for mi in range(16):
                gmi = iq * 16 + mi
                gctr, gval = self._ffn_issue("g", gmi)
                uctr, uval = self._ffn_issue("u", gmi)
                self.wait("tensor", gctr, gval)
                self.wait("tensor", uctr, uval)
                for th in range(2):
                    tG = mi % 2
                    tU = 2 + mi % 2
                    self.pe_wait_free(tG)
                    self.pe_wait_free(tU)
                    if iq == 0 and mi == 0:
                        for piece in range(4):
                            self.wait("tensor", ve, self.v_ychunk[piece * 4 + 3])
                            def fp(e, th=th, tG=tG, tU=tU, piece=piece):
                                last = None
                                for kc in range(piece * 4, piece * 4 + 4):
                                    e.matmul(self.ps[tG][:, 0:512],
                                             self.sWF[:, 0, kc, :],
                                             self.sY[:, kc * T + th * 512: kc * T + th * 512 + 512],
                                             start=(kc == 0), stop=(kc == NCH - 1),
                                             skip_group_check=True)
                                    last = e.matmul(
                                        self.ps[tU][:, 0:512],
                                        self.sWF[:, 2, kc, :],
                                        self.sY[:, kc * T + th * 512: kc * T + th * 512 + 512],
                                        start=(kc == 0), stop=(kc == NCH - 1),
                                        skip_group_check=True)
                                return last
                            v_mm = self.on("tensor", fp, inc=pe)
                    else:
                        def f(e, mi=mi, th=th, tG=tG, tU=tU):
                            for kc in range(NCH):
                                e.matmul(self.ps[tG][:, 0:512],
                                         self.sWF[:, mi % 2, kc, :],
                                         self.sY[:, kc * T + th * 512: kc * T + th * 512 + 512],
                                         start=(kc == 0), stop=(kc == NCH - 1),
                                         skip_group_check=True)
                            last = None
                            for kc in range(NCH):
                                last = e.matmul(
                                    self.ps[tU][:, 0:512],
                                    self.sWF[:, 2 + mi % 2, kc, :],
                                    self.sY[:, kc * T + th * 512: kc * T + th * 512 + 512],
                                    start=(kc == 0), stop=(kc == NCH - 1),
                                    skip_group_check=True)
                            return last
                        v_mm = self.on("tensor", f, inc=pe)
                    self.wait("scalar", pe, v_mm)
                    if sg_busy[th]:
                        self.wait("scalar", ve, sg_busy[th])
                    def f_s(e, tG=tG, th=th):
                        return e.activation(out=self.cSG[:, th, :], in_=self.ps[tG][:, 0:512],
                                            func=AF.Silu)
                    v_si = self.on("scalar", f_s, inc=se)
                    self.mark_ps_free(tG, se)
                    self.wait("vector", se, v_si)
                    self.wait("vector", pe, v_mm)
                    def f_h(e, mi=mi, th=th, tU=tU):
                        if RAW_H:
                            return e.tensor_copy(
                                self.sH[:, mi * T + th * 512: mi * T + th * 512 + 512],
                                self.ps[tU][:, 0:512])
                        return e.tensor_tensor(
                            self.sH[:, mi * T + th * 512: mi * T + th * 512 + 512],
                            self.cSG[:, th, :], self.ps[tU][:, 0:512], ALU.mult)
                    v_h = self.on("vector", f_h, inc=ve)
                    sg_busy[th] = v_h
                    self.mark_ps_free(tU, ve)
                self._ffn_consumed("g", gmi)
                self._ffn_consumed("u", gmi)
                if mi == 15 and not SKIP_SPARSIFY:
                    while pending:
                        pending.pop(0)()
                    v_sp012 = self.ve.v
                    pending += self._sparsify_steps(3)
                if mi % 4 == 3 and mi != 15 and not SKIP_SPARSIFY:
                    pending += self._sparsify_steps(mi // 4)
                for _ in range(5):
                    if pending:
                        pending.pop(0)()
            while pending:
                pending.pop(0)()
            v_sp3 = self.ve.v
            if SKIP_SPARSIFY:
                v_sp012 = v_sp3
            if iq == 0:
                self.wait("tensor", ve, self.v_nmr)
            head_mo = 2
            for mo in range(head_mo):
                dctr, dval = self._ffn_issue("d", iq * 16 + mo)
                self.wait("tensor", dctr, dval)
                self.wait("tensor", ve, v_sp012)
                for th in range(2):
                    tile = 2 * mo + th
                    self.pe_wait_free(tile)
                    def fA(e, mo=mo, th=th, tile=tile):
                        last = None
                        for ik in range(12):
                            last = e.matmul(
                                self.ps[tile][:, 0:512],
                                self.sWD[:, mo % 2, ik, :],
                                self.sH[:, ik * T + th * 512: ik * T + th * 512 + 512],
                                start=(ik == 0), stop=False,
                                skip_group_check=True)
                        return last
                    self.on("tensor", fA, inc=pe)
            self.wait("tensor", ve, v_sp3)
            for mo in range(head_mo):
                for th in range(2):
                    tile = 2 * mo + th
                    def fB(e, mo=mo, th=th, tile=tile, iq=iq):
                        last = None
                        for ik in range(12, 16):
                            last = e.matmul(
                                self.ps[tile][:, 0:512],
                                self.sWD[:, mo % 2, ik, :],
                                self.sH[:, ik * T + th * 512: ik * T + th * 512 + 512],
                                start=False, stop=(ik == 15 and iq != 0),
                                skip_group_check=True)
                        if iq == 0:
                            last = e.matmul(
                                self.ps[tile][:, 0:512], self.cONERb[0:1, :],
                                self.cNMR[0:1, th * 512: th * 512 + 512],
                                start=False, stop=True, skip_group_check=True)
                        return last
                    v_mm = self.on("tensor", fB, inc=pe)
                    self.wait("vector", pe, v_mm)
                    sl = slice(mo * T + th * 512, mo * T + th * 512 + 512)
                    self._down_evac(iq, sl, tile)
                    self.mark_ps_free(tile, ve)
                self._ffn_consumed("d", iq * 16 + mo)
            for mo in range(head_mo, NCH):
                gmo = iq * 16 + mo
                dctr, dval = self._ffn_issue("d", gmo)
                self.wait("tensor", dctr, dval)
                for th in range(2):
                    tile = 4 + th
                    self.pe_wait_free(tile)
                    def f(e, mo=mo, th=th, tile=tile, iq=iq):
                        last = None
                        for ik in range(16):
                            last = e.matmul(
                                self.ps[tile][:, 0:512],
                                self.sWD[:, mo % 2, ik, :],
                                self.sH[:, ik * T + th * 512: ik * T + th * 512 + 512],
                                start=(ik == 0), stop=(ik == 15 and iq != 0),
                                skip_group_check=True)
                        if iq == 0:
                            last = e.matmul(
                                self.ps[tile][:, 0:512], self.cONERb[0:1, :],
                                self.cNMR[0:1, th * 512: th * 512 + 512],
                                start=False, stop=True, skip_group_check=True)
                        return last
                    v_mm = self.on("tensor", f, inc=pe)
                    self.wait("vector", pe, v_mm)
                    sl = slice(mo * T + th * 512, mo * T + th * 512 + 512)
                    self._down_evac(iq, sl, tile)
                    self.mark_ps_free(tile, ve)
                self._ffn_consumed("d", gmo)
        self.v_resid2 = self.ve.v
        self.v_ffn_pe = self.pe.v

    def _sparsify_steps(self, c0):
        ve, se = self.ve, self.se
        sl = [slice((4 * c0 + j) * T, (4 * c0 + j) * T + T) for j in range(4)]
        s = [self.sSP[:, i, :] for i in range(6)]
        st = {}
        def t0():
            self.wait("scalar", ve, self.ve.v)
            def fa01(e):
                e.activation(out=s[3], in_=self.sH[:, sl[0]], func=AF.Abs)
                return e.activation(out=s[4], in_=self.sH[:, sl[1]], func=AF.Abs)
            st["a01"] = self.on("scalar", fa01, inc=se)
        def t1():
            self.wait("vector", se, st["a01"])
            def fmm01(e):
                e.tensor_tensor(s[0], s[3], s[4], ALU.max)
                return e.tensor_tensor(s[1], s[3], s[4], ALU.min)
            st["mm01"] = self.on("vector", fmm01, inc=ve)
        def t2():
            self.wait("scalar", ve, st["mm01"])
            def fa23(e):
                e.activation(out=s[3], in_=self.sH[:, sl[2]], func=AF.Abs)
                return e.activation(out=s[4], in_=self.sH[:, sl[3]], func=AF.Abs)
            st["a23"] = self.on("scalar", fa23, inc=se)
        def t3():
            self.wait("vector", se, st["a23"])
            def fmm23(e):
                e.tensor_tensor(s[2], s[3], s[4], ALU.max)
                return e.tensor_tensor(s[5], s[3], s[4], ALU.min)
            st["mm23"] = self.on("vector", fmm23, inc=ve)
        def t4():
            self.wait("vector", ve, st["mm23"])
            def fab(e):
                e.tensor_tensor(s[0], s[0], s[2], ALU.min)
                return e.tensor_tensor(s[1], s[1], s[5], ALU.max)
            st["ab"] = self.on("vector", fab, inc=ve)
        def t5():
            self.wait("vector", ve, st["ab"])
            def fthr(e):
                return e.tensor_tensor(s[2], s[0], s[1], ALU.max)
            st["thr"] = self.on("vector", fthr, inc=ve)
        steps = [t0, t1, t2, t3, t4, t5]
        for j in range(4):
            def tj_a(j=j):
                self.wait("scalar", ve, self.ve.v)
                def f6(e, j=j):
                    return e.activation(out=s[3], in_=self.sH[:, sl[j]], func=AF.Abs)
                st["aj"] = self.on("scalar", f6, inc=se)
            def tj_b(j=j):
                self.wait("vector", se, st["aj"])
                def f7a(e):
                    return e.tensor_tensor(s[4], s[3], s[2], ALU.is_ge)
                st["ge"] = self.on("vector", f7a, inc=ve)
            def tj_c(j=j):
                self.wait("vector", ve, st["ge"])
                def f7(e, j=j):
                    return e.tensor_tensor(self.sH[:, sl[j]], self.sH[:, sl[j]], s[4], ALU.mult)
                self.on("vector", f7, inc=ve)
            steps += [tj_a, tj_b, tj_c]
        return steps

    def phase8_ln2_out(self):
        pe, ve, dO = self.pe, self.ve, self.dO
        self.wait("vector", pe, self.v_ffn_pe)
        self.wait("vector", ve, self.v_resid2)
        self.wait("scalar", pe, self.v_ffn_pe)
        self.wait("scalar", ve, self.v_resid2)
        self.wait("tensor", ve, self.v_resid2)
        self._ln_stats()
        for kc in range(NCH):
            par = kc % 2
            def f0(e, kc=kc, par=par):
                scr = self.sSC1 if par == 0 else self.sSC2
                return e.tensor_sub(scr[:, :], self.sA2[:, kc * T:(kc + 1) * T], self.sM[:, :])
            v0 = self.on("vector", f0, inc=ve)
            self.wait("vector", ve, v0)
            def f1(e, kc=kc, par=par):
                scr = self.sSC1 if par == 0 else self.sSC2
                return e.tensor_tensor(self.sA2[:, kc * T:(kc + 1) * T], scr[:, :],
                                       self.sR[:, :], ALU.mult)
            v_z = self.on("vector", f1, inc=ve)
            self.wait("sync", ve, v_z)
            self.dma("sync", dO, self.out_fm[kc * 128:(kc + 1) * 128, :],
                     self.sA2[:, kc * T:(kc + 1) * T])

    def emit_debug_dumps(self):
        d = self.dbg
        if not d:
            return
        self.wait("gpsimd", self.ve, self.ve.v)
        self.wait("gpsimd", self.se, self.se.v)
        srcs = dict(q=self.sQ, k=self.sK, vt=self.sVT, g=self.sG, ret=self.sRET,
                    st=self.sSf, acol=self.cACOL, bcol=self.cBCOL, resid1=self.sA2,
                    ybf=self.sY, h=self.sH, gated=self.sRET, csum=self.cSUM,
                    ctot=self.cTOT, ct1=self.cT1, ct3=self.cT3, cmur=self.cMUr,
                    crsr=self.cRSr, wf=self.sWF, wd=self.sWD, csg=self.cSG)
        for name in d:
            self.dma("gpsimd", self.dDBG, d[name], srcs[name][:, :])

    def finalize(self):
        for ctr in ([self.dO, self.dC, self.dXB, self.dXBa, self.dSL, self.dLL, self.dST,
                     self.dSTL, self.dTR, self.dDBG] + self.dWA + self.dFG +
                    self.dFU + self.dFD + self.dX1):
            self.wait("sync", ctr)


def _prep_inputs(inputs):
    perm = _perm_ffn()
    x = np.asarray(inputs["L0_attn_x"], np.float32)
    col = lambda b: np.ascontiguousarray(np.asarray(b, np.float32).reshape(16, 128).T)
    shared = dict(
        wq=np.ascontiguousarray(inputs["L0_attn_wq"], np.float32),
        wk=np.ascontiguousarray(inputs["L0_attn_wk"], np.float32),
        wv=np.ascontiguousarray(inputs["L0_attn_wv"], np.float32),
        wg=np.ascontiguousarray(inputs["L0_attn_wg"], np.float32),
        wo=np.ascontiguousarray(inputs["L0_attn_wo"], np.float32),
        wgate=np.ascontiguousarray(np.asarray(inputs["L0_ffn_w_gate"], np.float32)[:, perm]),
        wup=np.ascontiguousarray(np.asarray(inputs["L0_ffn_w_up"], np.float32)[:, perm]),
        wdown=np.ascontiguousarray(np.asarray(inputs["L0_ffn_w_down"], np.float32)[perm, :]),
        bq_c=col(inputs["L0_attn_bq"]), bk_c=col(inputs["L0_attn_bk"]),
        bg_c=col(inputs["L0_attn_bg"]), bo_c=col(inputs["L0_attn_bo"]),
        gnw_c=col(inputs["L0_attn_gn_w"]), gnb_c=col(inputs["L0_attn_gn_b"]),
        bv_c=col(inputs["L0_attn_bv"]),
        gamma_rep=np.ascontiguousarray(
            np.broadcast_to(np.asarray(inputs["L0_attn_gamma"], np.float32), (128, 16))),
    )
    in_maps = []
    for c in range(8):
        b, qt = c // 4, c % 4
        m = dict(shared)
        m["x_fm"] = np.ascontiguousarray(x[b, qt * T:(qt + 1) * T, :].T)
        mix = np.full(4, 1e9, np.float32)
        for qp in range(qt):
            mix[qp] = 1024.0 * (qt - qp - 1)
        m["mixw_rep"] = np.ascontiguousarray(np.broadcast_to(mix, (128, 4)))
        in_maps.append(m)
    return in_maps


_CACHED = {}


def kernel(**inputs):
    if "nc" not in _CACHED:
        _CACHED["nc"] = Builder().build()
    nc = _CACHED["nc"]
    in_maps = _prep_inputs(inputs)
    res = run_bass_kernel_spmd(nc, in_maps, core_ids=list(range(8)))
    out = np.empty((B, SEQ, HID), np.float32)
    for c in range(8):
        b, qt = c // 4, c % 4
        out[b, qt * T:(qt + 1) * T, :] = res.results[c]["out_fm"].T
    return out
